# version 18
# baseline (speedup 1.0000x reference)
import contextlib
import os as _os
import numpy as np
import ml_dtypes
import concourse.bass as bass
import concourse.mybir as mybir
from concourse.bass_utils import run_bass_kernel_spmd

F32 = mybir.dt.float32
BF16 = mybir.dt.bfloat16
AF = mybir.ActivationFunctionType
ALU = mybir.AluOpType
AX = mybir.AxisListType
BF = ml_dtypes.bfloat16

NCORES = 8
BPC = 2
L = 4096
D = 1024
INNER = 2048
HEADS = 8
STATE = 64
HD = 256
KCONV = 4
EPS = 1e-5

CH_A = 256
CH_B = 512
SUB = 128
KD = D // 128
KI = INNER // 128
PO = STATE * HEADS
PK = 2 * PO + HEADS

NT_A = CH_A // 128
NCH_A = L // CH_A
NT_B = CH_B // 128
NSUB = CH_B // SUB
NCH_B = L // CH_B


class Prog:
    ENGS = ("pe", "dve", "act", "pool", "sp")
    NSLOT = 8

    def __init__(self):
        self.ops = {e: [] for e in self.ENGS}
        self.cnt = {e: 0 for e in self.ENGS}
        self.dma_slot_cnt = {}
        self.dma_n = {"sp": 0, "pool": 0}
        self.hw = {}
        self.last_w = {}
        self.readers = {}

    def op(self, eng, reads=(), writes=(), emit=None, dma=False):
        needs = {}

        def need(src, v):
            if v > needs.get(src, -1):
                needs[src] = v

        reads = list(reads)
        writes = list(writes)
        excl = [k for k in reads if k.startswith("pb") or k.startswith("pt")]
        for k in excl:
            reads.remove(k)
            writes.append(k)
        for k in reads:
            for src, v in self.last_w.get(k, {}).items():
                need(src, v)
        for k in writes:
            if not dma:
                for src, v in self.last_w.get(k, {}).items():
                    need(src, v)
            else:
                for src, v in self.last_w.get(k, {}).items():
                    if src in self.ENGS:
                        need(src, v)
            for r_src, r_cnt in self.readers.get(k, {}).items():
                need(r_src, r_cnt)

        if dma:
            slot = self.dma_n[eng] % self.NSLOT
            self.dma_n[eng] += 1
            srckey = f"{eng}{slot}"
            prev = self.dma_slot_cnt.get(srckey, 0)
            if prev > 0:
                need(srckey, prev)
            self.dma_slot_cnt[srckey] = prev + 16
            after_src = srckey
            after_val = prev + 16
        else:
            srckey = eng
            self.cnt[eng] += 1
            after_src = eng
            after_val = self.cnt[eng]
            if after_val > 255 and self.hw.get((eng, eng), 0) < after_val - 255:
                need(eng, after_val - 200)

        waits = []
        for src, v in sorted(needs.items()):
            cur = self.hw.get((eng, src), 0)
            if v > cur:
                waits.append((src, v))
                self.hw[(eng, src)] = v
        self.ops[eng].append((waits, emit, after_src if dma else None))
        for k in reads:
            self.readers.setdefault(k, {})[after_src] = after_val
        for k in writes:
            if dma:
                d = self.last_w.setdefault(k, {})
                for src in [s for s in d if s in self.ENGS]:
                    del d[src]
                d[after_src] = max(d.get(after_src, 0), after_val)
            else:
                self.last_w[k] = {after_src: after_val}
                self.readers[k] = {}
        return after_val

    def fence(self, keys):
        self.op("dve", reads=list(keys), writes=list(keys),
                emit=lambda e: e.engine_nop())


def build_program():
    _PH = int(_os.environ.get("KBISECT", "5"))
    _SB = int(_os.environ.get("KBISECT2", "9"))
    _SM = int(_os.environ.get("KBISECT3", "9"))
    nc = bass.Bass("TRN2", target_bir_lowering=False)

    dx = nc.dram_tensor("x", [BPC, L, D], F32, kind="ExternalInput").ap()
    dWx = nc.dram_tensor("Wx", [INNER, D], F32, kind="ExternalInput").ap()
    dWz = nc.dram_tensor("Wz", [INNER, D], F32, kind="ExternalInput").ap()
    dcw = nc.dram_tensor("conv_w", [128, KI * KCONV], F32, kind="ExternalInput").ap()
    dcb = nc.dram_tensor("conv_b", [128, KI], F32, kind="ExternalInput").ap()
    dWb = nc.dram_tensor("Wb", [PO, INNER], F32, kind="ExternalInput").ap()
    dWc = nc.dram_tensor("Wc", [PO, INNER], F32, kind="ExternalInput").ap()
    dWdt = nc.dram_tensor("Wdt", [HEADS, INNER], F32, kind="ExternalInput").ap()
    dbdt = nc.dram_tensor("b_dt", [HEADS, 1], F32, kind="ExternalInput").ap()
    dloga = nc.dram_tensor("log_a", [HEADS, 1], F32, kind="ExternalInput").ap()
    ddpar = nc.dram_tensor("d_param", [HEADS, 1], F32, kind="ExternalInput").ap()
    dgnw = nc.dram_tensor("gn_w", [128, KI], F32, kind="ExternalInput").ap()
    dgnb = nc.dram_tensor("gn_b", [128, KI], F32, kind="ExternalInput").ap()
    dWout = nc.dram_tensor("Wout", [D, INNER], F32, kind="ExternalInput").ap()
    dWf = nc.dram_tensor("Wf", [D, D], F32, kind="ExternalInput").ap()
    dident = nc.dram_tensor("ident", [128, 128], BF16, kind="ExternalInput").ap()
    didentf = nc.dram_tensor("identf", [8, 8], F32, kind="ExternalInput").ap()
    dnegr = nc.dram_tensor("negr", [128, 512], BF16, kind="ExternalInput").ap()
    dones = nc.dram_tensor("ones_c", [1, 128], F32, kind="ExternalInput").ap()
    donesb = nc.dram_tensor("ones_b", [128, 1], F32, kind="ExternalInput").ap()
    dseg = nc.dram_tensor("seg", [HEADS, CH_B], F32, kind="ExternalInput").ap()
    dsel = nc.dram_tensor("sel", [HEADS, 4 * 128], BF16, kind="ExternalInput").ap()
    dselh = nc.dram_tensor("selh", [HEADS, HEADS * 128], F32, kind="ExternalInput").ap()

    dout = nc.dram_tensor("out", [BPC, L, D], F32, kind="ExternalOutput").ap()
    sp_x1 = nc.dram_tensor("sp_x1", [BPC * L, D], BF16, kind="Internal").ap()
    sp_xc = nc.dram_tensor("sp_xc", [BPC, INNER, L], BF16, kind="Internal").ap()
    sp_z = nc.dram_tensor("sp_z", [BPC, INNER, L], BF16, kind="Internal").ap()
    sp_y = nc.dram_tensor("sp_y", [BPC, INNER, L], BF16, kind="Internal").ap()

    p = Prog()
    ctx = contextlib.ExitStack()
    with ctx:
        ST = lambda name, shape, dt_: ctx.enter_context(nc.sbuf_tensor(name, shape, dt_))

        wS = ST("wS", [128, KD * D], BF16)
        slotA = ST("slotA", [128, KI * PK], BF16)
        slotB = ST("slotB", [128, KD * INNER], BF16)
        G1 = ST("G1", [128, 8192], BF16)
        G2 = ST("G2", [128, 8192], BF16)
        G3 = ST("G3", [128, 8192], BF16)
        G4 = ST("G4", [128, 8192], BF16)
        G5 = ST("G5", [128, 6144], BF16)
        xp = ST("xp", [128, 515], BF16)
        xpB = ST("xpB", [128, 515], BF16)
        halo = ST("halo", [128, KI * 3], BF16)
        acc_a = ST("acc_a", [128, CH_B], F32)
        acc_b = ST("acc_b", [128, CH_B], F32)
        s_sb = ST("s_sb", [8, CH_B], F32)
        nslf = ST("nslf", [8, CH_B], F32)
        dtv = ST("dtv", [8, CH_B], F32)
        scbf = ST("scbf", [8, CH_B], F32)
        esr = ST("esr", [8, CH_B], BF16)
        tin = ST("tin", [8, 2 * CH_B], F32)
        colT = ST("colT", [128, NSUB * 16], F32)
        lamc = ST("lamc", [128, NSUB * 4], F32)
        S_a = ST("S_a", [128, 4 * HD], F32)
        S_b = ST("S_b", [128, 4 * HD], F32)
        S_bf = ST("S_bf", [128, 4 * HD], BF16)
        ys = ST("ys", [128, NCH_B * NSUB * 8], F32)
        yss = ST("yss", [128, NCH_B * NSUB * 8], F32)
        c1 = ST("c1", [128, 2 * NT_A], F32)
        c2 = ST("c2", [128, 4 * NT_A], F32)
        cm = ST("cm", [128, 2 * NT_A], F32)
        cv = ST("cv", [128, 2 * NT_A], F32)
        crv = ST("crv", [128, 2 * NT_A], F32)
        cnm = ST("cnm", [128, 2 * NT_A], F32)
        dumpA = ST("dumpA", [128, 512], BF16)
        esb = ST("esbuf", [128, CH_B], BF16)
        acc_s1 = ST("acc_s1", [128, CH_B], BF16)
        gnwp = ST("gnwp", [128, BPC * KI], F32)
        gnbp = ST("gnbp", [128, BPC * KI], F32)
        stat8 = ST("stat8", [1, 56], F32)
        rinv8 = ST("rinv8", [128, BPC * 16], F32)
        wstg = ST("wstg", [128, 2048], BF16)
        ident = ST("identc", [128, 128], BF16)
        identf = ST("identfc", [8, 8], F32)
        negr = ST("negrc", [128, 512], BF16)
        onesc = ST("onescc", [1, 128], F32)
        onesb = ST("onesbc", [128, 1], F32)
        segc = ST("segcc", [8, CH_B], F32)
        selc = ST("selcc", [8, 4 * 128], BF16)
        selh = ST("selhc", [8, 8 * 128], F32)
        cwc = ST("cwcc", [128, KI * KCONV], F32)
        cbc = ST("cbcc", [128, KI], F32)
        bdtc = ST("bdtcc", [8, 1], F32)
        acol = ST("acolc", [8, 1], F32)
        dcol = ST("dcolc", [128, 8], F32)
        dic = ST("dicc", [128, 8 * 128], BF16)
        gnwc = ST("gnwcc", [128, KI], F32)
        gnbc = ST("gnbcc", [128, KI], F32)
        cst = ST("cstc", [128, 2], F32)

        psum = [ctx.enter_context(nc.psum_tensor(f"pb{i}", [128, 512], F32))
                for i in range(6)]
        ptr = [ctx.enter_context(nc.psum_tensor(f"pt{i}", [128, 1024], BF16))
               for i in range(2)]
        PS = [f"pb{i}" for i in range(6)]
        PT = [f"pt{i}" for i in range(2)]

        sems = {e: ctx.enter_context(nc.semaphore(f"sm_{e}")) for e in Prog.ENGS}
        for q in ("sp", "pool"):
            for s in range(Prog.NSLOT):
                sems[f"{q}{s}"] = ctx.enter_context(nc.semaphore(f"sm_{q}{s}"))
        block = ctx.enter_context(nc.Block())

        def dma(queue, out, in_, reads, writes):
            p.op(queue, reads=reads, writes=writes,
                 emit=lambda e, o=out, i=in_: e.dma_start(out=o, in_=i), dma=True)

        bfv = lambda t, c0, c1_: t[:, c0:c1_]
        f32v = lambda t, c0, c1_: t[:].bitcast(F32)[:, c0:c1_]
        v_x0p = [bfv(G5, 0, 2048), bfv(G4, 0, 2048)]
        v_x1p = [bfv(G5, 2048, 4096), bfv(G4, 2048, 4096)]
        v_utmp = [bfv(G1, 0, 2048), bfv(G1, 2048, 4096)]
        v_ufmp = [bfv(G1, 4096, 6144), bfv(G1, 6144, 8192),
                  bfv(G3, 4096, 6144), bfv(G3, 6144, 8192)]
        u2_src = [(G1, 4096), (G3, 4096)]
        v_u1fp = [bfv(G3, 0, 2048), bfv(G3, 2048, 4096)]
        v_u2 = bfv(G1, 0, KD * CH_B)
        v_xcb1 = bfv(G2, 0, 2048)
        v_zb1 = bfv(G2, 2048, 4096)
        v_xcr = bfv(G1, 0, 8192)
        v_xcs = bfv(G2, 0, 8192)
        v_yt = bfv(G3, 0, 8192)
        v_xtmA = bfv(G4, 0, 2048)
        v_xtmB = bfv(G4, 2048, 4096)
        v_bfm = bfv(G4, 4096, 6144)
        v_ctil = bfv(G4, 6144, 8192)
        v_btmA = bfv(G5, 0, 1024)
        v_btmB = bfv(G5, 1024, 2048)
        v_mg = bfv(G5, 2048, 3072)
        v_eb = bfv(G5, 3072, 4096)
        v_cth = bfv(G5, 4096, 6144)
        v_yrp = [bfv(G1, 0, 4096), bfv(G1, 4096, 8192)]
        v_zrp = [bfv(G2, 0, 4096), bfv(G2, 4096, 8192)]
        v_gyp = [bfv(G3, 0, 4096), bfv(G3, 4096, 8192)]
        v_x1r = bfv(G4, 0, 2048)
        v_x2p = [bfv(G4, 2048, 4096), bfv(G5, 0, 2048)]
        v_utm2 = bfv(G4, 4096, 6144)
        v_ufm2p = [bfv(G4, 6144, 8192), bfv(G5, 2048, 4096)]
        v_x3 = f32v(G5, 2048, 3072)

        ALL_POOL_KEYS = ["x0_0", "x0_1", "x1b_0", "x1b_1", "utm_0", "utm_1",
                         "ufm_0", "ufm_1", "ufm_2", "ufm_3", "u1f_0", "u1f_1", "x2_0", "x2_1", "ufm2_0", "ufm2_1",
                         "u2", "xcb1", "zb1",
                         "xcr", "xcs", "xcs0", "xcs1", "yt", "xtmA", "xtmB", "bfm", "ctil",
                         "btmA", "btmB", "mg", "eb", "cth", "yr", "zr", "gy",
                         "x1r", "utm2", "x3", "yr_0", "yr_1", "zr_0", "zr_1", "gy_0", "gy_1"]

        p.op("dve", [], ["cst"], emit=lambda e: e.memset(cst[:, 0:1], EPS))
        p.op("dve", [], ["cst"], emit=lambda e: e.memset(cst[:, 1:2], -0.5))
        dma("sp", ident[:], dident, [], ["ident"])
        dma("sp", identf[:], didentf, [], ["identf"])
        dma("sp", negr[:], dnegr, [], ["negr"])
        dma("sp", onesc[:], dones, [], ["ones"])
        dma("sp", onesb[:], donesb, [], ["onesb"])
        dma("sp", segc[:], dseg, [], ["seg"])
        dma("sp", selc[:], dsel, [], ["sel"])
        dma("sp", selh[:], dselh, [], ["selh"])
        dma("sp", cwc[:], dcw, [], ["cwc"])
        dma("sp", cbc[:], dcb, [], ["cbc"])
        dma("sp", bdtc[:], dbdt, [], ["bdtc"])
        dma("sp", gnwc[:], dgnw, [], ["gnwc"])
        dma("sp", gnbc[:], dgnb, [], ["gnbc"])
        dma("sp", acol[:], dloga, [], ["acol"])
        dma("sp", dcol[0:8, 0:1], ddpar, [], ["dcol"])

        p.op("act", ["acol"], ["acol"],
             emit=lambda e: e.activation(out=acol[:], in_=acol[:], func=AF.Exp))
        p.op("dve", ["acol"], ["acol"],
             emit=lambda e: e.tensor_scalar(out=acol[:], in0=acol[:], scalar1=-1.0,
                                            scalar2=None, op0=ALU.mult))
        p.op("pe", ["dcol", "identf"], [PS[0]],
             emit=lambda e: e.transpose(psum[0][0:1, 0:8], dcol[0:8, 0:1],
                                        identf[0:8, 0:8]))
        p.op("dve", [PS[0]], ["dcol"],
             emit=lambda e: e.tensor_copy(out=dcol[0:1, 0:8], in_=psum[0][0:1, 0:8]))
        p.op("pe", ["dcol", "ones"], [PS[1]],
             emit=lambda e: e.matmul(psum[1][:, 0:8], onesc[0:1, 0:128],
                                     dcol[0:1, 0:8], start=True, stop=True))
        p.op("dve", [PS[1]], ["dcol"],
             emit=lambda e: e.tensor_copy(out=dcol[:, 0:8], in_=psum[1][:, 0:8]))
        for h in range(8):
            p.op("dve", ["dcol", "ident"], ["dic"],
                 emit=lambda e, h=h: e.tensor_scalar(
                     out=dic[:, h * 128:(h + 1) * 128], in0=ident[:],
                     scalar1=dcol[:, h:h + 1], scalar2=None, op0=ALU.mult))

        def prep_w(dram, rows, cols, dst, dst_key, col_of, sign=False):
            n_ot = rows // 128
            n_kt = cols // 128
            for ot in range(n_ot):
                dma("pool", wstg[:, 0:cols], dram[ot * 128:(ot + 1) * 128, :],
                    [], ["wstg"])
                if sign:
                    p.op("act", ["wstg"], ["wstg"],
                         emit=lambda e, cols=cols: e.sign(out=wstg[:, 0:cols],
                                                          in_=wstg[:, 0:cols]))
                for g in range(0, n_kt, 8):
                    hi = min(g + 8, n_kt)
                    bi = (ot * 2 + g // 8) % 2
                    bank, bkey = ptr[bi], PT[bi]
                    for kt in range(g, hi):
                        p.op("pe", ["wstg", "ident"], [bkey],
                             emit=lambda e, kt=kt, g=g, bank=bank: e.transpose(
                                 bank[:, (kt - g) * 128:(kt - g + 1) * 128],
                                 wstg[:, kt * 128:(kt + 1) * 128], ident[:]))
                    for kt in range(g, hi):
                        col = col_of(kt, ot)
                        p.op("dve", [bkey], [dst_key],
                             emit=lambda e, kt=kt, g=g, col=col, bank=bank:
                             e.tensor_copy(out=dst[:, col:col + 128],
                                           in_=bank[:, (kt - g) * 128:(kt - g + 1) * 128]))

        prep_w(dWf, D, D, wS, "wS", lambda kt, ot: kt * D + ot * 128, sign=True)
        prep_w(dWx, INNER, D, slotA, "slotA", lambda kt, ot: kt * INNER + ot * 128)
        prep_w(dWz, INNER, D, slotB, "slotB", lambda kt, ot: kt * INNER + ot * 128)

        def layernorm(src, src_key, dst, dst_key, par=0):
            o = par * NT_A
            o2 = par * 2 * NT_A
            kc = [f"c1_{par}", f"c2_{par}", f"cm_{par}", f"cv_{par}",
                  f"cnm_{par}", f"crv_{par}"]
            for tt in range(NT_A):
                sl = src[:, tt * D:(tt + 1) * D]
                p.op("dve", [src_key], [kc[0]],
                     emit=lambda e, sl=sl, tt=tt: e.reduce_sum(
                         out=c1[:, o + tt:o + tt + 1], in_=sl, axis=AX.X))
                for hf in range(2):
                    p.op("act", [src_key], [kc[1], "dumpA"],
                         emit=lambda e, src=src, tt=tt, hf=hf: e.activation(
                             out=dumpA[:], in_=src[:, tt * D + hf * 512:
                                                   tt * D + (hf + 1) * 512],
                             func=AF.Square,
                             accum_out=c2[:, o2 + hf * NT_A + tt:
                                          o2 + hf * NT_A + tt + 1]))
            p.op("dve", [kc[0]], [kc[2]],
                 emit=lambda e: e.tensor_scalar(out=cm[:, o:o + NT_A],
                                                in0=c1[:, o:o + NT_A],
                                                scalar1=1.0 / D, scalar2=None,
                                                op0=ALU.mult))
            p.op("dve", [kc[1]], [kc[3]],
                 emit=lambda e: e.tensor_add(out=cv[:, o:o + NT_A],
                                             in0=c2[:, o2:o2 + NT_A],
                                             in1=c2[:, o2 + NT_A:o2 + 2 * NT_A]))
            p.op("dve", [kc[3]], [kc[3]],
                 emit=lambda e: e.tensor_scalar(out=cv[:, o:o + NT_A],
                                                in0=cv[:, o:o + NT_A],
                                                scalar1=1.0 / D, scalar2=None,
                                                op0=ALU.mult))
            p.op("dve", [kc[2]], [kc[4]],
                 emit=lambda e: e.tensor_mul(out=cnm[:, o:o + NT_A],
                                             in0=cm[:, o:o + NT_A],
                                             in1=cm[:, o:o + NT_A]))
            p.op("dve", [kc[3], kc[4]], [kc[3]],
                 emit=lambda e: e.tensor_sub(out=cv[:, o:o + NT_A],
                                             in0=cv[:, o:o + NT_A],
                                             in1=cnm[:, o:o + NT_A]))
            p.op("act", [kc[3], "cst"], [kc[3]],
                 emit=lambda e: e.activation(out=cv[:, o:o + NT_A],
                                             in_=cv[:, o:o + NT_A], func=AF.Ln,
                                             bias=cst[:, 0:1]))
            p.op("act", [kc[3], "cst"], [kc[5]],
                 emit=lambda e: e.activation(out=crv[:, o:o + NT_A],
                                             in_=cv[:, o:o + NT_A], func=AF.Exp,
                                             scale=cst[:, 1:2]))
            p.op("dve", [kc[2], kc[5]], [kc[4]],
                 emit=lambda e: e.tensor_mul(out=cnm[:, o:o + NT_A],
                                             in0=cm[:, o:o + NT_A],
                                             in1=crv[:, o:o + NT_A]))
            p.op("dve", [kc[4]], [kc[4]],
                 emit=lambda e: e.tensor_scalar(out=cnm[:, o:o + NT_A],
                                                in0=cnm[:, o:o + NT_A],
                                                scalar1=-1.0, scalar2=None,
                                                op0=ALU.mult))
            for tt in range(NT_A):
                p.op("act", [src_key, kc[5], kc[4]], [dst_key],
                     emit=lambda e, tt=tt: e.activation(
                         out=dst[:, tt * D:(tt + 1) * D],
                         in_=src[:, tt * D:(tt + 1) * D], func=AF.Identity,
                         bias=cnm[:, o + tt:o + tt + 1],
                         scale=crv[:, o + tt:o + tt + 1]))

        def transpose_u(src, src_key, dst, dst_key, chunk):
            nt = chunk // 128
            for tt in range(nt):
                bi = tt % 2
                bank, bkey = ptr[bi], PT[bi]
                for kt in range(KD):
                    p.op("pe", [src_key, "ident"], [bkey],
                         emit=lambda e, tt=tt, kt=kt, bank=bank: e.transpose(
                             bank[:, kt * 128:(kt + 1) * 128],
                             src[:, tt * D + kt * 128:tt * D + (kt + 1) * 128],
                             ident[:]))
                for kt in range(KD):
                    p.op("dve", [bkey], [dst_key],
                         emit=lambda e, tt=tt, kt=kt, bank=bank, chunk=chunk:
                         e.tensor_copy(
                             out=dst[:, kt * chunk + tt * 128:kt * chunk + (tt + 1) * 128],
                             in_=bank[:, kt * 128:(kt + 1) * 128]))

        def p1a1_chunk(b, c):
            T0 = c * CH_A
            par = c % 2
            v_x0, v_x1 = v_x0p[par], v_x1p[par]
            v_utm, v_u1f, v_ufm = v_utmp[par], v_u1fp[par], v_ufmp[c % 4]
            kx0, kx1 = f"x0_{par}", f"x1b_{par}"
            kut, ku1, kuf = f"utm_{par}", f"u1f_{par}", f"ufm_{c % 4}"
            x_view = bass.AP(dx.tensor, (b * L + T0) * D,
                             [[D, 128], [128 * D, NT_A], [1, D]])
            dma("pool", v_x0.rearrange("p (a b) -> p a b", a=NT_A), x_view,
                [], [kx0])
            layernorm(v_x0, kx0, v_utm, kut, par)
            transpose_u(v_utm, kut, v_u1f, ku1, CH_A)
            for tt in range(NT_A):
                for nh in range(2):
                    bi = 4 + (tt * 2 + nh) % 2
                    bank, bkey = psum[bi], PS[bi]

                    def f1(e, tt=tt, nh=nh, bank=bank, v_u1f=v_u1f):
                        last = None
                        for k in range(KD):
                            last = e.matmul(
                                bank[:],
                                v_u1f[:, k * CH_A + tt * 128:k * CH_A + (tt + 1) * 128],
                                wS[:, k * D + nh * 512:k * D + (nh + 1) * 512],
                                start=(k == 0), stop=(k == KD - 1))
                        return last
                    p.op("pe", [ku1, "wS"], [bkey], emit=f1)
                    p.op("dve", [bkey, kx0], [kx1],
                         emit=lambda e, tt=tt, nh=nh, bank=bank, v_x1=v_x1,
                         v_x0=v_x0:
                         e.scalar_tensor_tensor(
                             out=v_x1[:, tt * D + nh * 512:tt * D + (nh + 1) * 512],
                             in0=bank[:], scalar=0.5,
                             in1=v_x0[:, tt * D + nh * 512:tt * D + (nh + 1) * 512],
                             op0=ALU.mult, op1=ALU.add))
            x1_view = bass.AP(sp_x1.tensor, (b * L + T0) * D,
                              [[D, 128], [128 * D, NT_A], [1, D]])
            dma("sp", x1_view, v_x1.rearrange("p (a b) -> p a b", a=NT_A),
                [kx1], ["sp_x1"])
            layernorm(v_x1, kx1, v_utm, kut, par)
            transpose_u(v_utm, kut, v_ufm, kuf, CH_A)

        def u2_rhs(k, pp):
            t, off = u2_src[pp]
            return bass.AP(t, off + k * CH_A,
                           [[8192, 128], [2048, 2], [1, CH_A]])

        for b in range(BPC if _PH >= 2 else 0):
            p.op("dve", [], ["halo"], emit=lambda e: e.memset(halo[:], 0.0))
            p1a1_chunk(b, 0)
            p1a1_chunk(b, 1)
            for cp in range(NCH_B):
                T0 = cp * CH_B
                if cp + 1 < NCH_B:
                    p1a1_chunk(b, 2 * cp + 2)
                    p1a1_chunk(b, 2 * cp + 3)
                if _PH < 3:
                    continue
                for ot in range(KI):
                    bi = ot % 4
                    bank, bkey = psum[bi], PS[bi]

                    pp = cp % 2
                    ku2a, ku2b = f"ufm_{2 * pp}", f"ufm_{2 * pp + 1}"

                    def wxmm(e, ot=ot, bank=bank, pp=pp):
                        last = None
                        for k in range(KD):
                            last = e.matmul(
                                bank[:],
                                slotA[:, k * INNER + ot * 128:k * INNER + (ot + 1) * 128],
                                u2_rhs(k, pp), start=(k == 0), stop=(k == KD - 1))
                        return last
                    p.op("pe", [ku2a, ku2b, "slotA"], [bkey], emit=wxmm)
                    xpc = xp if ot % 2 == 0 else xpB
                    kxp = "xp" if ot % 2 == 0 else "xpB"
                    p.op("act", [bkey], [kxp],
                         emit=lambda e, bank=bank, xpc=xpc: e.activation(
                             out=xpc[:, 3:3 + CH_B], in_=bank[:], func=AF.Copy))
                    p.op("dve", ["halo", kxp], [kxp],
                         emit=lambda e, ot=ot, xpc=xpc: e.tensor_copy(
                             out=xpc[:, 0:3], in_=halo[:, ot * 3:ot * 3 + 3]))
                    p.op("dve", [kxp], ["halo"],
                         emit=lambda e, ot=ot, xpc=xpc: e.tensor_copy(
                             out=halo[:, ot * 3:ot * 3 + 3],
                             in_=xpc[:, CH_B:CH_B + 3]))
                    acc0 = acc_a if ot % 2 == 0 else acc_b
                    ka0 = "acc_a" if ot % 2 == 0 else "acc_b"
                    p.op("dve", [kxp, "cwc", "cbc"], [ka0],
                         emit=lambda e, ot=ot, xpc=xpc, acc0=acc0: e.tensor_scalar(
                             out=acc0[:], in0=xpc[:, 0:CH_B],
                             scalar1=cwc[:, ot * 4:ot * 4 + 1],
                             scalar2=cbc[:, ot:ot + 1], op0=ALU.mult, op1=ALU.add))
                    p.op("dve", [kxp, "cwc", ka0], [ka0],
                         emit=lambda e, ot=ot, xpc=xpc, acc0=acc0:
                         e.scalar_tensor_tensor(
                             out=acc0[:], in0=xpc[:, 1:1 + CH_B],
                             scalar=cwc[:, ot * 4 + 1:ot * 4 + 2], in1=acc0[:],
                             op0=ALU.mult, op1=ALU.add))
                    p.op("dve", [kxp, "cwc", ka0], [ka0],
                         emit=lambda e, ot=ot, xpc=xpc, acc0=acc0:
                         e.scalar_tensor_tensor(
                             out=acc0[:], in0=xpc[:, 2:2 + CH_B],
                             scalar=cwc[:, ot * 4 + 2:ot * 4 + 3], in1=acc0[:],
                             op0=ALU.mult, op1=ALU.add))
                    p.op("dve", [kxp, "cwc", ka0], [ka0],
                         emit=lambda e, ot=ot, xpc=xpc, acc0=acc0:
                         e.scalar_tensor_tensor(
                             out=acc0[:], in0=xpc[:, 3:3 + CH_B],
                             scalar=cwc[:, ot * 4 + 3:ot * 4 + 4], in1=acc0[:],
                             op0=ALU.mult, op1=ALU.add))
                    p.op("act", [ka0], ["xcb1"],
                         emit=lambda e, ot=ot, acc0=acc0: e.activation(
                             out=v_xcb1[:, (ot % 4) * CH_B:(ot % 4 + 1) * CH_B],
                             in_=acc0[:], func=AF.Silu))

                    def wzmm(e, ot=ot, bank=bank, pp=pp):
                        last = None
                        for k in range(KD):
                            last = e.matmul(
                                bank[:],
                                slotB[:, k * INNER + ot * 128:k * INNER + (ot + 1) * 128],
                                u2_rhs(k, pp), start=(k == 0), stop=(k == KD - 1))
                        return last
                    p.op("pe", [ku2a, ku2b, "slotB"], [bkey], emit=wzmm)
                    p.op("act", [bkey], ["zb1"],
                         emit=lambda e, ot=ot, bank=bank: e.activation(
                             out=v_zb1[:, (ot % 4) * CH_B:(ot % 4 + 1) * CH_B],
                             in_=bank[:], func=AF.Copy))
                    if ot % 4 == 3:
                        q = ot // 4
                        xc_view = bass.AP(sp_xc.tensor,
                                          b * INNER * L + q * 4 * 128 * L + T0,
                                          [[L, 128], [128 * L, 4], [1, CH_B]])
                        dma("sp", xc_view,
                            bass.AP(v_xcb1.tensor, 0,
                                    [[8192, 128], [CH_B, 4], [1, CH_B]]),
                            ["xcb1"], ["sp_xc"])
                        z_view = bass.AP(sp_z.tensor,
                                         b * INNER * L + q * 4 * 128 * L + T0,
                                         [[L, 128], [128 * L, 4], [1, CH_B]])
                        dma("pool", z_view,
                            bass.AP(v_zb1.tensor, 2048,
                                    [[8192, 128], [CH_B, 4], [1, CH_B]]),
                            ["zb1"], ["sp_z"])

        prep_w(dWb, PO, INNER, slotA, "slotA", lambda kt, ot: kt * PK + ot * 128)
        prep_w(dWc, PO, INNER, slotA, "slotA",
               lambda kt, ot: kt * PK + PO + ot * 128)
        dma("pool", wstg[0:8, 0:INNER], dWdt, [], ["wstg"])
        for kt in range(KI):
            bi = kt % 2
            bank, bkey = ptr[bi], PT[bi]
            p.op("pe", ["wstg", "ident"], [bkey],
                 emit=lambda e, kt=kt, bank=bank: e.transpose(
                     bank[:, 0:8], wstg[0:8, kt * 128:(kt + 1) * 128],
                     ident[0:8, 0:8]))
            p.op("dve", [bkey], ["slotA"],
                 emit=lambda e, kt=kt, bank=bank: e.tensor_copy(
                     out=slotA[:, kt * PK + 2 * PO:kt * PK + 2 * PO + 8],
                     in_=bank[:, 0:8]))

        p.fence(ALL_POOL_KEYS)
        p.op("dve", [], ["ctil"], emit=lambda e: e.memset(v_ctil, 0.0))
        p.op("dve", [], ["cth"], emit=lambda e: e.memset(v_cth, 0.0))
        p.op("dve", [], ["btmA"], emit=lambda e: e.memset(v_btmA, 0.0))
        p.op("dve", [], ["btmB"], emit=lambda e: e.memset(v_btmB, 0.0))
        for b in range(BPC if _PH >= 4 else 0):
            p.op("dve", [], ["S_a"], emit=lambda e: e.memset(S_a[:], 0.0))
            p.op("dve", [], ["S_bf"], emit=lambda e: e.memset(S_bf[:], 0.0))
            S_cur, S_nxt = S_a, S_b
            S_keys = ["S_a", "S_b"]
            for c in range(NCH_B):
                T0 = c * CH_B
                xcs_t = [G2, G1][c % 2]
                v_xcs_c = xcs_t[:, 0:8192]
                kxcs = f"xcs{c % 2}"
                for q in range(4):
                    xc_view = bass.AP(sp_xc.tensor,
                                      b * INNER * L + q * 4 * 128 * L + T0,
                                      [[L, 128], [128 * L, 4], [1, CH_B]])
                    dma("sp",
                        bass.AP(xcs_t, q * 4 * CH_B,
                                [[8192, 128], [CH_B, 4], [1, CH_B]]),
                        xc_view, ["sp_xc"], [kxcs])
                if _SB < 2:
                    continue
                def dtmm(e, v_xcs_c=v_xcs_c):
                    last = None
                    for k in range(KI):
                        last = e.matmul(
                            psum[4][0:8, :],
                            slotA[:, k * PK + 2 * PO:k * PK + 2 * PO + 8],
                            v_xcs_c[:, k * CH_B:(k + 1) * CH_B],
                            start=(k == 0), stop=(k == KI - 1))
                    return last
                p.op("pe", [kxcs, "slotA"], [PS[4]], emit=dtmm)
                if _SB < 3:
                    continue
                pr = tin[0:8, 0:CH_B]
                t1 = dtv[0:8, :]
                t2 = tin[0:8, CH_B:2 * CH_B]
                p.op("dve", [PS[4], "bdtc"], ["tin"],
                     emit=lambda e: e.tensor_scalar(out=pr, in0=psum[4][0:8, :],
                                                    scalar1=bdtc[:], scalar2=None,
                                                    op0=ALU.add))
                p.op("dve", ["tin"], ["dtv"],
                     emit=lambda e: e.tensor_scalar(out=t1, in0=pr, scalar1=-1.0,
                                                    scalar2=None, op0=ALU.mult))
                p.op("dve", ["tin", "dtv"], ["tin"],
                     emit=lambda e: e.tensor_tensor(out=t2, in0=pr, in1=t1,
                                                    op=ALU.min))
                p.op("act", ["tin"], ["tin"],
                     emit=lambda e: e.activation(out=t2, in_=t2, func=AF.Exp))
                p.op("act", ["tin"], ["tin"],
                     emit=lambda e: e.activation(out=t2, in_=t2, func=AF.Ln,
                                                 bias=1.0))
                p.op("dve", ["tin"], ["dtv"],
                     emit=lambda e: e.tensor_scalar(out=t1, in0=pr, scalar1=0.0,
                                                    scalar2=None, op0=ALU.max))
                p.op("dve", ["tin", "dtv"], ["dtv"],
                     emit=lambda e: e.tensor_tensor(out=dtv[:], in0=t1, in1=t2,
                                                    op=ALU.add))
                p.op("dve", ["dtv", "acol"], ["scbf"],
                     emit=lambda e: e.tensor_scalar(out=scbf[:], in0=dtv[:],
                                                    scalar1=acol[:], scalar2=None,
                                                    op0=ALU.mult))
                p.op("dve", ["scbf", "seg"], ["s_sb"],
                     emit=lambda e: e.tensor_tensor_scan(
                         out=s_sb[0:8, :], data0=segc[:], data1=scbf[:], initial=0.0,
                         op0=ALU.mult, op1=ALU.add))
                p.op("act", ["dtv"], ["tin"],
                     emit=lambda e: e.activation(out=t2, in_=dtv[:], func=AF.Ln))
                p.op("dve", ["tin", "s_sb"], ["nslf"],
                     emit=lambda e: e.tensor_sub(out=nslf[:], in0=t2,
                                                 in1=s_sb[0:8, :]))
                p.op("act", ["s_sb"], ["esr"],
                     emit=lambda e: e.activation(out=esr[:], in_=s_sb[0:8, :],
                                                 func=AF.Exp))
                for su in range(NSUB):
                    p.op("act", ["s_sb", "nslf"], ["scbf"],
                         emit=lambda e, su=su: e.activation(
                             out=scbf[:, su * SUB:(su + 1) * SUB],
                             in_=nslf[:, su * SUB:(su + 1) * SUB], func=AF.Exp,
                             bias=s_sb[0:8, (su + 1) * SUB - 1:(su + 1) * SUB]))
                if _SB < 4:
                    continue
                for su in range(NSUB):
                    bi = su % 2
                    p.op("pe", ["nslf", "identf"], [PS[5]],
                         emit=lambda e, su=su: e.transpose(
                             psum[5][:, 0:8], nslf[:, su * SUB:(su + 1) * SUB],
                             identf[0:8, 0:8]))
                    p.op("dve", [PS[5]], ["colT"],
                         emit=lambda e, su=su: e.tensor_copy(
                             out=colT[:, su * 16:su * 16 + 8], in_=psum[5][:, 0:8]))
                    p.op("pe", ["scbf", "identf"], [PS[5]],
                         emit=lambda e, su=su: e.transpose(
                             psum[5][:, 0:8], scbf[:, su * SUB:(su + 1) * SUB],
                             identf[0:8, 0:8]))
                    p.op("dve", [PS[5]], ["colT"],
                         emit=lambda e, su=su: e.tensor_copy(
                             out=colT[:, su * 16 + 8:su * 16 + 16],
                             in_=psum[5][:, 0:8]))
                for su in range(NSUB):
                    def lam(e, su=su):
                        last = None
                        for g in range(4):
                            last = e.matmul(
                                psum[5][:, su * 4 + g:su * 4 + g + 1],
                                selc[:, g * 128:(g + 1) * 128],
                                esr[:, (su + 1) * SUB - 1:(su + 1) * SUB],
                                start=True, stop=True)
                        return last
                    p.op("pe", ["esr", "sel"], [PS[5]], emit=lam)
                    p.op("dve", [PS[5]], ["lamc"],
                         emit=lambda e, su=su: e.tensor_copy(
                             out=lamc[:, su * 4:su * 4 + 4],
                             in_=psum[5][:, su * 4:su * 4 + 4]))
                if _SB < 5:
                    continue
                for m in range(4):
                    bi = m % 4
                    bank, bkey = psum[bi], PS[bi]

                    def bmm(e, m=m, bank=bank, v_xcs_c=v_xcs_c):
                        last = None
                        for k in range(KI):
                            last = e.matmul(
                                bank[:],
                                slotA[:, k * PK + m * 128:k * PK + (m + 1) * 128],
                                v_xcs_c[:, k * CH_B:(k + 1) * CH_B],
                                start=(k == 0), stop=(k == KI - 1))
                        return last
                    p.op("pe", [kxcs, "slotA"], [bkey], emit=bmm)
                    p.op("dve", [bkey], ["bfm"],
                         emit=lambda e, m=m, bank=bank: e.tensor_copy(
                             out=v_bfm[:, m * CH_B:(m + 1) * CH_B], in_=bank[:]))
                for m in range(4):
                    bi = m % 4
                    bank, bkey = psum[bi], PS[bi]

                    def cmm(e, m=m, bank=bank, v_xcs_c=v_xcs_c):
                        last = None
                        for k in range(KI):
                            last = e.matmul(
                                bank[:],
                                slotA[:, k * PK + PO + m * 128:k * PK + PO + (m + 1) * 128],
                                v_xcs_c[:, k * CH_B:(k + 1) * CH_B],
                                start=(k == 0), stop=(k == KI - 1))
                        return last
                    p.op("pe", [kxcs, "slotA"], [bkey], emit=cmm)

                    def esmm(e, m=m):
                        return e.matmul(psum[4][:], selc[:, m * 128:(m + 1) * 128],
                                        esr[:], start=True, stop=True)
                    p.op("pe", ["esr", "sel"], [PS[4]], emit=esmm)
                    p.op("dve", [PS[4]], ["esbuf"],
                         emit=lambda e: e.tensor_copy(out=esb[:], in_=psum[4][:]))
                    p.op("dve", [bkey, "esbuf"], ["ctil"],
                         emit=lambda e, m=m, bank=bank: e.tensor_tensor(
                             out=v_ctil[0:64, m * CH_B:(m + 1) * CH_B],
                             in0=bank[0:64, :], in1=esb[0:64, :], op=ALU.mult))
                    p.op("dve", [bkey, "esbuf"], ["cth"],
                         emit=lambda e, m=m, bank=bank: e.tensor_tensor(
                             out=v_cth[64:128, m * CH_B:(m + 1) * CH_B],
                             in0=bank[64:128, :], in1=esb[64:128, :], op=ALU.mult))

                if _SB < 6:
                    continue
                for su in range(NSUB):
                    t0 = su * SUB
                    xtm = v_xtmA if su % 2 == 0 else v_xtmB
                    xkey = "xtmA" if su % 2 == 0 else "xtmB"
                    btm = v_btmA if su % 2 == 0 else v_btmB
                    btkey = "btmA" if su % 2 == 0 else "btmB"
                    xv = bass.AP(sp_xc.tensor, b * INNER * L + T0 + t0,
                                 [[L, INNER], [1, 128]])
                    p.op("sp", ["sp_xc"], [xkey],
                         emit=lambda e, xv=xv, xtm=xtm: e.dma_start(
                             out=xtm[:], in_=xv, transpose=True), dma=True)
                    for h in range(8):
                        bi = h % 2
                        bank, bkey = ptr[bi], PT[bi]
                        p.op("pe", ["bfm", "ident"], [bkey],
                             emit=lambda e, h=h, bank=bank, t0=t0: e.transpose(
                                 bank[:, 0:64],
                                 v_bfm[(h % 2) * 64:(h % 2) * 64 + 64,
                                       (h // 2) * CH_B + t0:(h // 2) * CH_B + t0 + 128],
                                 ident[(h % 2) * 64:(h % 2) * 64 + 64,
                                       (h % 2) * 64:(h % 2) * 64 + 64]))
                        bcol = (h // 2) * 256 + (h % 2) * 192
                        p.op("dve", [bkey, "colT"], [btkey],
                             emit=lambda e, h=h, su=su, bank=bank, btm=btm, bcol=bcol:
                             e.tensor_scalar(
                                 out=btm[:, bcol:bcol + 64],
                                 in0=bank[:, 0:64],
                                 scalar1=colT[:, su * 16 + 8 + h:su * 16 + 9 + h],
                                 scalar2=None, op0=ALU.mult))
                    if _SB < 7:
                        continue
                    for hb in range(2):
                        mb, mkey = psum[2 + hb], PS[2 + hb]
                        gb, gkey = psum[4 + hb], PS[4 + hb]

                        def maskmm(e, hb=hb, mb=mb, t0=t0):
                            e.matmul(mb[:], ident[:], negr[:],
                                     start=True, stop=False)
                            last = None
                            for hq in range(4):
                                h = hb * 4 + hq
                                sl = mb[:, hq * 128:(hq + 1) * 128]
                                last = e.matmul(sl, selh[:, h * 128:(h + 1) * 128],
                                                s_sb[0:8, t0:t0 + 128],
                                                start=False, stop=(hq == 3))
                            return last
                        if _SM >= 1:
                            p.op("pe", ["s_sb", "selh", "ident", "negr"], [mkey],
                                 emit=maskmm)
                        for hq in range(4 if _SM >= 2 else 0):
                            h = hb * 4 + hq
                            p.op("act", [mkey, "colT"], ["eb"],
                                 emit=lambda e, h=h, hq=hq, su=su, mb=mb:
                                 e.activation(
                                     out=v_eb[:, h * 128:(h + 1) * 128],
                                     in_=mb[:, hq * 128:(hq + 1) * 128],
                                     func=AF.Exp,
                                     bias=colT[:, su * 16 + h:su * 16 + h + 1]))

                        def gmm(e, hb=hb, gb=gb, t0=t0):
                            last = None
                            for hq in range(4):
                                h = hb * 4 + hq
                                cpad = v_ctil if h % 2 == 0 else v_cth
                                last = e.matmul(
                                    gb[:, hq * 128:(hq + 1) * 128],
                                    v_bfm[:, (h // 2) * CH_B + t0:(h // 2) * CH_B + t0 + 128],
                                    cpad[:, (h // 2) * CH_B + t0:(h // 2) * CH_B + t0 + 128],
                                    start=True, stop=True)
                            return last
                        if _SM >= 3:
                            p.op("pe", ["bfm", "ctil", "cth"], [gkey], emit=gmm)
                        if _SM >= 4:
                            p.op("dve", [gkey, "eb"], ["mg"],
                                 emit=lambda e, gb=gb, hb=hb: e.tensor_tensor(
                                     out=v_mg[:, hb * 512:(hb + 1) * 512], in0=gb[:],
                                     in1=v_eb[:, hb * 512:(hb + 1) * 512], op=ALU.mult))
                            p.op("dve", ["mg", "dic"], ["mg"],
                                 emit=lambda e, hb=hb: e.tensor_tensor(
                                     out=v_mg[:, hb * 512:(hb + 1) * 512],
                                     in0=v_mg[:, hb * 512:(hb + 1) * 512],
                                     in1=dic[:, hb * 512:(hb + 1) * 512], op=ALU.add))
                    if _SB < 8:
                        continue
                    for j in range(4):
                        yb, ykey = psum[j], PS[j]

                        def ymm(e, j=j, t0=t0, xtm=xtm):
                            last = None
                            for q in range(4):
                                ct = 4 * j + q
                                h = ct // 2
                                sl = psum[j][:, q * 128:(q + 1) * 128]
                                e.matmul(sl,
                                         xtm[:, ct * 128:(ct + 1) * 128],
                                         v_mg[:, h * 128:(h + 1) * 128],
                                         start=True, stop=False)
                                cpad = v_ctil if h % 2 == 0 else v_cth
                                last = e.matmul(
                                    sl,
                                    S_bf[:, (h // 2) * HD + (ct % 2) * 128:
                                         (h // 2) * HD + (ct % 2) * 128 + 128],
                                    cpad[:, (h // 2) * CH_B + t0:(h // 2) * CH_B + t0 + 128],
                                    start=False, stop=True)
                            return last
                        p.op("pe", [xkey, "mg", "dic", "S_bf", "ctil", "cth"], [ykey],
                             emit=ymm)
                        for half in range(2):
                            col = c * NSUB * 8 + su * 8 + j * 2 + half
                            p.op("dve", [ykey], ["ys", "yt"],
                                 emit=lambda e, j=j, su=su, half=half, col=col:
                                 e.tensor_scalar(
                                     out=bass.AP(v_yt.tensor,
                                                 (4 * j + 2 * half) * CH_B + su * 128,
                                                 [[8192, 128], [CH_B, 2], [1, 128]]),
                                     in0=psum[j][:, half * 256:(half + 1) * 256],
                                     scalar1=1.0, scalar2=0.0, op0=ALU.mult,
                                     op1=ALU.add,
                                     accum_out=ys[:, col:col + 1]))
                            p.op("act", [ykey], ["yss", "dumpA"],
                                 emit=lambda e, j=j, half=half, col=col:
                                 e.activation(
                                     out=dumpA[:, 0:256],
                                     in_=psum[j][:, half * 256:(half + 1) * 256],
                                     func=AF.Square, accum_out=yss[:, col:col + 1]))
                    if _SB < 9:
                        continue
                    for g in range(4):
                        bi = 4 + (g % 2)
                        db, dkey = psum[bi], PS[bi]

                        def dsmm(e, g=g, db=db, xtm=xtm, btm=btm):
                            e.matmul(db[:, 0:HD],
                                     btm[:, g * 256:g * 256 + 128],
                                     xtm[:, (2 * g) * HD:(2 * g + 1) * HD],
                                     start=True, stop=False)
                            return e.matmul(
                                db[:, 0:HD],
                                btm[:, g * 256 + 128:g * 256 + 256],
                                xtm[:, (2 * g + 1) * HD:(2 * g + 2) * HD],
                                start=False, stop=True)
                        p.op("pe", [btkey, xkey], [dkey], emit=dsmm)
                        p.op("dve", [dkey, S_keys[0], "lamc"], [S_keys[1]],
                             emit=lambda e, g=g, su=su, db=db, Sc=S_cur, Sn=S_nxt:
                             e.scalar_tensor_tensor(
                                 out=Sn[:, g * HD:(g + 1) * HD],
                                 in0=Sc[:, g * HD:(g + 1) * HD],
                                 scalar=lamc[:, su * 4 + g:su * 4 + g + 1],
                                 in1=db[:, 0:HD], op0=ALU.mult, op1=ALU.add))
                        p.op("dve", [S_keys[1]], ["S_bf"],
                             emit=lambda e, g=g, Sn=S_nxt: e.tensor_copy(
                                 out=S_bf[:, g * HD:(g + 1) * HD],
                                 in_=Sn[:, g * HD:(g + 1) * HD]))
                    S_cur, S_nxt = S_nxt, S_cur
                    S_keys = S_keys[::-1]
                for q in range(4):
                    y_view = bass.AP(sp_y.tensor, b * INNER * L + q * 4 * 128 * L + T0,
                                     [[L, 128], [128 * L, 4], [1, CH_B]])
                    dma("pool", y_view,
                        bass.AP(v_yt.tensor, q * 4 * CH_B,
                                [[8192, 128], [CH_B, 4], [1, CH_B]]),
                        ["yt"], ["sp_y"])
            def gnred(e):
                e.matmul(psum[0][0:1, 0:NCH_B * NSUB * 8], onesb[:, 0:1], ys[:],
                         start=True, stop=True)
                return e.matmul(psum[1][0:1, 0:NCH_B * NSUB * 8], onesb[:, 0:1],
                                yss[:], start=True, stop=True)
            p.op("pe", ["ys", "yss", "onesb"], [PS[0], PS[1]], emit=gnred)
            p.op("dve", [PS[0]], ["stat8"],
                 emit=lambda e: e.reduce_sum(
                     out=stat8[0:1, 0:8],
                     in_=psum[0][0:1, 0:NCH_B * NSUB * 8].rearrange(
                         "p (a b) -> p b a", b=8), axis=AX.X))
            p.op("dve", [PS[1]], ["stat8"],
                 emit=lambda e: e.reduce_sum(
                     out=stat8[0:1, 8:16],
                     in_=psum[1][0:1, 0:NCH_B * NSUB * 8].rearrange(
                         "p (a b) -> p b a", b=8), axis=AX.X))
            NGN = float(L * HD)
            p.op("dve", ["stat8"], ["stat8"],
                 emit=lambda e: e.tensor_scalar(out=stat8[0:1, 16:24],
                                                in0=stat8[0:1, 0:8],
                                                scalar1=1.0 / NGN, scalar2=None,
                                                op0=ALU.mult))
            p.op("dve", ["stat8"], ["stat8"],
                 emit=lambda e: e.tensor_scalar(out=stat8[0:1, 24:32],
                                                in0=stat8[0:1, 8:16],
                                                scalar1=1.0 / NGN, scalar2=None,
                                                op0=ALU.mult))
            p.op("dve", ["stat8"], ["stat8"],
                 emit=lambda e: e.tensor_mul(out=stat8[0:1, 32:40],
                                             in0=stat8[0:1, 16:24],
                                             in1=stat8[0:1, 16:24]))
            p.op("dve", ["stat8"], ["stat8"],
                 emit=lambda e: e.tensor_sub(out=stat8[0:1, 40:48],
                                             in0=stat8[0:1, 24:32],
                                             in1=stat8[0:1, 32:40]))
            p.op("act", ["stat8", "cst"], ["stat8"],
                 emit=lambda e: e.activation(out=stat8[0:1, 40:48],
                                             in_=stat8[0:1, 40:48], func=AF.Ln,
                                             bias=cst[0:1, 0:1]))
            p.op("act", ["stat8", "cst"], ["stat8"],
                 emit=lambda e: e.activation(out=stat8[0:1, 48:56],
                                             in_=stat8[0:1, 40:48], func=AF.Exp,
                                             scale=cst[0:1, 1:2]))
            def bc16(e):
                e.matmul(psum[2][:, 0:8], onesc[0:1, 0:128], stat8[0:1, 48:56],
                         start=True, stop=True)
                return e.matmul(psum[2][:, 8:16], onesc[0:1, 0:128],
                                stat8[0:1, 16:24], start=True, stop=True)
            p.op("pe", ["stat8", "ones"], [PS[2]], emit=bc16)
            p.op("dve", [PS[2]], ["rinv8"],
                 emit=lambda e, b=b: e.tensor_copy(out=rinv8[:, b * 16:b * 16 + 16],
                                                   in_=psum[2][:, 0:16]))
            p.op("dve", ["rinv8", "gnwc"], ["gnwp"],
                 emit=lambda e, b=b: e.tensor_tensor(
                     out=gnwp[:, b * KI:(b + 1) * KI], in0=gnwc[:],
                     in1=bass.AP(rinv8, b * 16,
                                 [[BPC * 16, 128], [1, 8], [0, 2]]),
                     op=ALU.mult))
            p.op("dve", ["gnwp", "rinv8"], ["gnbp"],
                 emit=lambda e, b=b: e.tensor_tensor(
                     out=gnbp[:, b * KI:(b + 1) * KI],
                     in0=gnwp[:, b * KI:(b + 1) * KI],
                     in1=bass.AP(rinv8, b * 16 + 8,
                                 [[BPC * 16, 128], [1, 8], [0, 2]]),
                     op=ALU.mult))
            p.op("dve", ["gnbp", "gnbc"], ["gnbp"],
                 emit=lambda e, b=b: e.tensor_sub(
                     out=gnbp[:, b * KI:(b + 1) * KI], in0=gnbc[:],
                     in1=gnbp[:, b * KI:(b + 1) * KI]))

        prep_w(dWout, D, INNER, slotB, "slotB", lambda kt, ot: kt * D + ot * 128)

        p.fence(ALL_POOL_KEYS)

        def p2_loads(b, c):
            T0 = c * CH_A
            par = c % 2
            v_yr, v_zr = v_yrp[par], v_zrp[par]
            kyr, kzr = f"yr_{par}", f"zr_{par}"
            for q in range(4):
                y_view = bass.AP(sp_y.tensor, b * INNER * L + q * 4 * 128 * L + T0,
                                 [[L, 128], [128 * L, 4], [1, CH_A]])
                dma("sp",
                    bass.AP(v_yr.tensor, par * 4096 + q * 4 * CH_A,
                            [[8192, 128], [CH_A, 4], [1, CH_A]]),
                    y_view, ["sp_y"], [kyr])
                z_view = bass.AP(sp_z.tensor, b * INNER * L + q * 4 * 128 * L + T0,
                                 [[L, 128], [128 * L, 4], [1, CH_A]])
                dma("pool",
                    bass.AP(v_zr.tensor, par * 4096 + q * 4 * CH_A,
                            [[8192, 128], [CH_A, 4], [1, CH_A]]),
                    z_view, ["sp_z"], [kzr])

        def p2_gy(b, c):
            par = c % 2
            v_yr, v_zr, v_gy = v_yrp[par], v_zrp[par], v_gyp[par]
            kyr, kzr, kgy = f"yr_{par}", f"zr_{par}", f"gy_{par}"
            for og in range(KI // 2):
                ot = 2 * og
                acc0 = acc_a if og % 2 == 0 else acc_b
                accs = acc_s1
                ka = f"acca{og % 2}"
                kb = "accb"
                p.op("dve", [kyr, "gnwp", "gnbp"], [ka],
                     emit=lambda e, ot=ot, b=b, acc0=acc0, v_yr=v_yr:
                     e.tensor_scalar(
                         out=acc0[:, 0:CH_A],
                         in0=v_yr[:, ot * CH_A:(ot + 1) * CH_A],
                         scalar1=gnwp[:, b * KI + ot:b * KI + ot + 1],
                         scalar2=gnbp[:, b * KI + ot:b * KI + ot + 1],
                         op0=ALU.mult, op1=ALU.add))
                p.op("dve", [kyr, "gnwp", "gnbp", ka], [ka],
                     emit=lambda e, ot=ot, b=b, acc0=acc0, v_yr=v_yr:
                     e.tensor_scalar(
                         out=acc0[:, CH_A:2 * CH_A],
                         in0=v_yr[:, (ot + 1) * CH_A:(ot + 2) * CH_A],
                         scalar1=gnwp[:, b * KI + ot + 1:b * KI + ot + 2],
                         scalar2=gnbp[:, b * KI + ot + 1:b * KI + ot + 2],
                         op0=ALU.mult, op1=ALU.add))
                p.op("act", [kzr], [kb],
                     emit=lambda e, ot=ot, accs=accs, v_zr=v_zr: e.activation(
                         out=accs[:, 0:2 * CH_A],
                         in_=v_zr[:, ot * CH_A:(ot + 2) * CH_A], func=AF.Silu))
                p.op("dve", [ka, kb], [kgy],
                     emit=lambda e, ot=ot, acc0=acc0, accs=accs, v_gy=v_gy:
                     e.tensor_tensor(
                         out=v_gy[:, ot * CH_A:(ot + 2) * CH_A],
                         in0=acc0[:, 0:2 * CH_A], in1=accs[:, 0:2 * CH_A],
                         op=ALU.mult))

        def p2_main_a(b, c):
            T0 = c * CH_A
            par = c % 2
            v_x2 = v_x2p[par]
            v_gy = v_gyp[par]
            kx2, kgy = f"x2_{par}", f"gy_{par}"
            x1_view = bass.AP(sp_x1.tensor, (b * L + T0) * D,
                              [[D, 128], [128 * D, NT_A], [1, D]])
            dma("sp", v_x1r.rearrange("p (a b) -> p a b", a=NT_A), x1_view,
                ["sp_x1"], ["x1r"])
            for tt in range(NT_A):
                for nh in range(2):
                    bi = (tt * 2 + nh) % 4
                    bank, bkey = psum[bi], PS[bi]

                    def womm(e, tt=tt, nh=nh, bank=bank, v_gy=v_gy):
                        last = None
                        for k in range(KI):
                            last = e.matmul(
                                bank[:],
                                v_gy[:, k * CH_A + tt * 128:k * CH_A + (tt + 1) * 128],
                                slotB[:, k * D + nh * 512:k * D + (nh + 1) * 512],
                                start=(k == 0), stop=(k == KI - 1))
                        return last
                    p.op("pe", [kgy, "slotB"], [bkey], emit=womm)
                    p.op("dve", [bkey, "x1r"], [kx2],
                         emit=lambda e, tt=tt, nh=nh, bank=bank, v_x2=v_x2:
                         e.tensor_tensor(
                             out=v_x2[:, tt * D + nh * 512:tt * D + (nh + 1) * 512],
                             in0=bank[:],
                             in1=v_x1r[:, tt * D + nh * 512:tt * D + (nh + 1) * 512],
                             op=ALU.add))

        def p2_ln(b, c):
            par = c % 2
            layernorm(v_x2p[par], f"x2_{par}", v_utm2, "utm2", par)

        def p2_main_b(b, c):
            T0 = c * CH_A
            par = c % 2
            v_x2 = v_x2p[par]
            v_ufm2 = v_ufm2p[par]
            kx2, kuf2 = f"x2_{par}", f"ufm2_{par}"
            transpose_u(v_utm2, "utm2", v_ufm2, kuf2, CH_A)
            for tt in range(NT_A):
                for nh in range(2):
                    bi = 4 + (tt * 2 + nh) % 2
                    bank, bkey = psum[bi], PS[bi]

                    def f2(e, tt=tt, nh=nh, bank=bank, v_ufm2=v_ufm2):
                        last = None
                        for k in range(KD):
                            last = e.matmul(
                                bank[:],
                                v_ufm2[:, k * CH_A + tt * 128:k * CH_A + (tt + 1) * 128],
                                wS[:, k * D + nh * 512:k * D + (nh + 1) * 512],
                                start=(k == 0), stop=(k == KD - 1))
                        return last
                    p.op("pe", [kuf2, "wS"], [bkey], emit=f2)
                    p.op("dve", [bkey, kx2], ["x3"],
                         emit=lambda e, tt=tt, nh=nh, bank=bank, v_x2=v_x2:
                         e.scalar_tensor_tensor(
                             out=v_x3[:, nh * 512:(nh + 1) * 512],
                             in0=bank[:], scalar=0.5,
                             in1=v_x2[:, tt * D + nh * 512:tt * D + (nh + 1) * 512],
                             op0=ALU.mult, op1=ALU.add))
                out_view = bass.AP(dout.tensor, (b * L + T0 + tt * 128) * D,
                                   [[D, 128], [1, D]])
                dma("sp", out_view, v_x3[:], ["x3"], ["dout"])

        if _PH >= 5:
            chunks = [(b, c) for b in range(BPC) for c in range(NCH_A)]
            p2_loads(*chunks[0])
            p2_gy(*chunks[0])
            p2_main_a(*chunks[0])
            for i, (b, c) in enumerate(chunks):
                p2_ln(b, c)
                if i + 1 < len(chunks):
                    p2_loads(*chunks[i + 1])
                    p2_gy(*chunks[i + 1])
                    p2_main_a(*chunks[i + 1])
                p2_main_b(b, c)

        p.op("pool", ["dout"], [], emit=lambda e: e.engine_nop())

        eng_map = {"pe": block.tensor, "dve": block.vector, "act": block.scalar,
                   "pool": block.gpsimd, "sp": block.sync}
        for eng in Prog.ENGS:
            ops = p.ops[eng]
            if not ops:
                continue

            def section(e, _ops=ops, _sem=sems[eng]):
                for waits, emit, slotkey in _ops:
                    for src, v in waits:
                        e.wait_ge(sems[src], v)
                    inst = emit(e)
                    if slotkey is None:
                        inst.then_inc(_sem, 1)
                    else:
                        inst.then_inc(sems[slotkey], 16)
            eng_map[eng](section)

    return nc


def _consts():
    ident = np.eye(128, dtype=BF)
    identf = np.eye(8, dtype=np.float32)
    negr = np.tile(np.where(np.arange(128)[:, None] > np.arange(128)[None, :],
                            -1e30, 0.0), (1, 4)).astype(BF)
    ones_c = np.ones((1, 128), np.float32)
    ones_b = np.ones((128, 1), np.float32)
    seg = np.ones((HEADS, CH_B), np.float32)
    seg[:, ::SUB] = 0.0
    sel = np.zeros((HEADS, 4 * 128), BF)
    for g in range(4):
        for pp in range(128):
            sel[2 * g + pp // 64, g * 128 + pp] = 1.0
    selh = np.zeros((HEADS, HEADS * 128), np.float32)
    for h in range(HEADS):
        selh[h, h * 128:(h + 1) * 128] = 1.0
    return dict(ident=ident, identf=identf, negr=negr,
                ones_c=ones_c, ones_b=ones_b, seg=seg, sel=sel, selh=selh)


def kernel(L0_attn_input_tensor, L0_attn_Wx, L0_attn_Wz, L0_attn_conv_w,
           L0_attn_conv_b, L0_attn_Wb, L0_attn_Wc, L0_attn_Wdt, L0_attn_b_dt,
           L0_attn_log_a, L0_attn_d_param, L0_attn_gn_w, L0_attn_gn_b,
           L0_attn_Wout, L0_ffn_weight):
    x = np.ascontiguousarray(L0_attn_input_tensor)
    consts = _consts()
    shared = dict(
        Wx=np.ascontiguousarray(L0_attn_Wx),
        Wz=np.ascontiguousarray(L0_attn_Wz),
        conv_w=np.ascontiguousarray(
            L0_attn_conv_w[:, 0, :].reshape(KI, 128, KCONV).transpose(1, 0, 2)
            .reshape(128, KI * KCONV)),
        conv_b=np.ascontiguousarray(L0_attn_conv_b.reshape(KI, 128).T),
        Wb=np.ascontiguousarray(L0_attn_Wb),
        Wc=np.ascontiguousarray(L0_attn_Wc),
        Wdt=np.ascontiguousarray(L0_attn_Wdt),
        b_dt=np.ascontiguousarray(L0_attn_b_dt.reshape(HEADS, 1)),
        log_a=np.ascontiguousarray(L0_attn_log_a.reshape(HEADS, 1)),
        d_param=np.ascontiguousarray(L0_attn_d_param.reshape(HEADS, 1)),
        gn_w=np.ascontiguousarray(L0_attn_gn_w.reshape(KI, 128).T),
        gn_b=np.ascontiguousarray(L0_attn_gn_b.reshape(KI, 128).T),
        Wout=np.ascontiguousarray(L0_attn_Wout),
        Wf=np.ascontiguousarray(L0_ffn_weight),
        **consts,
    )
    nc = build_program()
    in_maps = []
    for i in range(NCORES):
        m = dict(shared)
        m["x"] = np.ascontiguousarray(x[i * BPC:(i + 1) * BPC])
        in_maps.append(m)
    res = run_bass_kernel_spmd(nc, in_maps, core_ids=list(range(NCORES)))
    out = np.concatenate([r["out"] for r in res.results], axis=0)
    return out.astype(np.float32)


# version 19
# speedup vs baseline: 1.0655x; 1.0655x over previous
import contextlib
import os as _os
import numpy as np
import ml_dtypes
import concourse.bass as bass
import concourse.mybir as mybir
from concourse.bass_utils import run_bass_kernel_spmd

F32 = mybir.dt.float32
BF16 = mybir.dt.bfloat16
AF = mybir.ActivationFunctionType
ALU = mybir.AluOpType
AX = mybir.AxisListType
BF = ml_dtypes.bfloat16

NCORES = 8
BPC = 2
L = 4096
D = 1024
INNER = 2048
HEADS = 8
STATE = 64
HD = 256
KCONV = 4
EPS = 1e-5

CH_A = 256
CH_B = 512
SUB = 128
KD = D // 128
KI = INNER // 128
PO = STATE * HEADS
PK = 2 * PO + HEADS

NT_A = CH_A // 128
NCH_A = L // CH_A
NT_B = CH_B // 128
NSUB = CH_B // SUB
NCH_B = L // CH_B


class Prog:
    ENGS = ("pe", "dve", "act", "pool", "sp")
    NSLOT = 8

    def __init__(self):
        self.ops = {e: [] for e in self.ENGS}
        self.cnt = {e: 0 for e in self.ENGS}
        self.dma_slot_cnt = {}
        self.dma_n = {"sp": 0, "pool": 0}
        self.hw = {}
        self.last_w = {}
        self.readers = {}

    def op(self, eng, reads=(), writes=(), emit=None, dma=False):
        needs = {}

        def need(src, v):
            if v > needs.get(src, -1):
                needs[src] = v

        reads = list(reads)
        writes = list(writes)
        excl = [k for k in reads if k.startswith("pb") or k.startswith("pt")]
        for k in excl:
            reads.remove(k)
            writes.append(k)
        for k in reads:
            for src, v in self.last_w.get(k, {}).items():
                need(src, v)
        for k in writes:
            if not dma:
                for src, v in self.last_w.get(k, {}).items():
                    need(src, v)
            else:
                for src, v in self.last_w.get(k, {}).items():
                    if src in self.ENGS:
                        need(src, v)
            for r_src, r_cnt in self.readers.get(k, {}).items():
                need(r_src, r_cnt)

        if dma:
            slot = self.dma_n[eng] % self.NSLOT
            self.dma_n[eng] += 1
            srckey = f"{eng}{slot}"
            prev = self.dma_slot_cnt.get(srckey, 0)
            if prev > 0:
                need(srckey, prev)
            self.dma_slot_cnt[srckey] = prev + 16
            after_src = srckey
            after_val = prev + 16
        else:
            srckey = eng
            self.cnt[eng] += 1
            after_src = eng
            after_val = self.cnt[eng]
            if after_val > 255 and self.hw.get((eng, eng), 0) < after_val - 255:
                need(eng, after_val - 200)

        waits = []
        for src, v in sorted(needs.items()):
            cur = self.hw.get((eng, src), 0)
            if v > cur:
                waits.append((src, v))
                self.hw[(eng, src)] = v
        self.ops[eng].append((waits, emit, after_src if dma else None))
        for k in reads:
            self.readers.setdefault(k, {})[after_src] = after_val
        for k in writes:
            if dma:
                d = self.last_w.setdefault(k, {})
                for src in [s for s in d if s in self.ENGS]:
                    del d[src]
                d[after_src] = max(d.get(after_src, 0), after_val)
            else:
                self.last_w[k] = {after_src: after_val}
                self.readers[k] = {}
        return after_val

    def fence(self, keys):
        self.op("dve", reads=list(keys), writes=list(keys),
                emit=lambda e: e.engine_nop())


def build_program():
    _PH = int(_os.environ.get("KBISECT", "5"))
    _SB = int(_os.environ.get("KBISECT2", "9"))
    _SM = int(_os.environ.get("KBISECT3", "9"))
    nc = bass.Bass("TRN2", target_bir_lowering=False)

    dx = nc.dram_tensor("x", [BPC, L, D], F32, kind="ExternalInput").ap()
    dWx = nc.dram_tensor("Wx", [INNER, D], F32, kind="ExternalInput").ap()
    dWz = nc.dram_tensor("Wz", [INNER, D], F32, kind="ExternalInput").ap()
    dcw = nc.dram_tensor("conv_w", [128, KI * KCONV], F32, kind="ExternalInput").ap()
    dcb = nc.dram_tensor("conv_b", [128, KI], F32, kind="ExternalInput").ap()
    dWb = nc.dram_tensor("Wb", [PO, INNER], F32, kind="ExternalInput").ap()
    dWc = nc.dram_tensor("Wc", [PO, INNER], F32, kind="ExternalInput").ap()
    dWdt = nc.dram_tensor("Wdt", [HEADS, INNER], F32, kind="ExternalInput").ap()
    dbdt = nc.dram_tensor("b_dt", [HEADS, 1], F32, kind="ExternalInput").ap()
    dloga = nc.dram_tensor("log_a", [HEADS, 1], F32, kind="ExternalInput").ap()
    ddpar = nc.dram_tensor("d_param", [HEADS, 1], F32, kind="ExternalInput").ap()
    dgnw = nc.dram_tensor("gn_w", [128, KI], F32, kind="ExternalInput").ap()
    dgnb = nc.dram_tensor("gn_b", [128, KI], F32, kind="ExternalInput").ap()
    dWout = nc.dram_tensor("Wout", [D, INNER], F32, kind="ExternalInput").ap()
    dWf = nc.dram_tensor("Wf", [D, D], F32, kind="ExternalInput").ap()
    dident = nc.dram_tensor("ident", [128, 128], BF16, kind="ExternalInput").ap()
    didentf = nc.dram_tensor("identf", [8, 8], F32, kind="ExternalInput").ap()
    dnegr = nc.dram_tensor("negr", [128, 512], BF16, kind="ExternalInput").ap()
    dones = nc.dram_tensor("ones_c", [1, 128], F32, kind="ExternalInput").ap()
    donesb = nc.dram_tensor("ones_b", [128, 1], F32, kind="ExternalInput").ap()
    dseg = nc.dram_tensor("seg", [HEADS, CH_B], F32, kind="ExternalInput").ap()
    dsel = nc.dram_tensor("sel", [HEADS, 4 * 128], BF16, kind="ExternalInput").ap()
    dselh = nc.dram_tensor("selh", [HEADS, HEADS * 128], F32, kind="ExternalInput").ap()

    dout = nc.dram_tensor("out", [BPC, L, D], F32, kind="ExternalOutput").ap()
    sp_x1 = nc.dram_tensor("sp_x1", [BPC * L, D], BF16, kind="Internal").ap()
    sp_xc = nc.dram_tensor("sp_xc", [BPC, INNER, L], BF16, kind="Internal").ap()
    sp_z = nc.dram_tensor("sp_z", [BPC, INNER, L], BF16, kind="Internal").ap()
    sp_y = nc.dram_tensor("sp_y", [BPC, INNER, L], BF16, kind="Internal").ap()

    p = Prog()
    ctx = contextlib.ExitStack()
    with ctx:
        ST = lambda name, shape, dt_: ctx.enter_context(nc.sbuf_tensor(name, shape, dt_))

        wS = ST("wS", [128, KD * D], BF16)
        slotA = ST("slotA", [128, KI * PK], BF16)
        slotB = ST("slotB", [128, KD * INNER], BF16)
        G1 = ST("G1", [128, 8192], BF16)
        G2 = ST("G2", [128, 8192], BF16)
        G3 = ST("G3", [128, 8192], BF16)
        G4 = ST("G4", [128, 8192], BF16)
        G5 = ST("G5", [128, 6144], BF16)
        xp = ST("xp", [128, 515], BF16)
        xpB = ST("xpB", [128, 515], BF16)
        halo = ST("halo", [128, KI * 3], BF16)
        acc_a = ST("acc_a", [128, CH_B], F32)
        acc_b = ST("acc_b", [128, CH_B], F32)
        s_sb = ST("s_sb", [8, CH_B], F32)
        nslf = ST("nslf", [8, CH_B], F32)
        dtv = ST("dtv", [8, CH_B], F32)
        scbf = ST("scbf", [8, CH_B], F32)
        esr = ST("esr", [8, CH_B], BF16)
        tin = ST("tin", [8, 2 * CH_B], F32)
        colT = ST("colT", [128, NSUB * 16], F32)
        lamc = ST("lamc", [128, NSUB * 4], F32)
        S_a = ST("S_a", [128, 4 * HD], F32)
        S_b = ST("S_b", [128, 4 * HD], F32)
        S_bf = ST("S_bf", [128, 4 * HD], BF16)
        ys = ST("ys", [128, NCH_B * NSUB * 8], F32)
        yss = ST("yss", [128, NCH_B * NSUB * 8], F32)
        c1 = ST("c1", [128, 2 * NT_A], F32)
        c2 = ST("c2", [128, 4 * NT_A], F32)
        cm = ST("cm", [128, 2 * NT_A], F32)
        cv = ST("cv", [128, 2 * NT_A], F32)
        crv = ST("crv", [128, 2 * NT_A], F32)
        cnm = ST("cnm", [128, 2 * NT_A], F32)
        dumpA = ST("dumpA", [128, 512], BF16)
        esb = ST("esbuf", [128, CH_B], BF16)
        acc_s1 = ST("acc_s1", [128, CH_B], BF16)
        gnwp = ST("gnwp", [128, BPC * KI], F32)
        gnbp = ST("gnbp", [128, BPC * KI], F32)
        stat8 = ST("stat8", [1, 56], F32)
        rinv8 = ST("rinv8", [128, BPC * 16], F32)
        wstg = ST("wstg", [128, 2048], BF16)
        ident = ST("identc", [128, 128], BF16)
        identf = ST("identfc", [8, 8], F32)
        negr = ST("negrc", [128, 512], BF16)
        onesc = ST("onescc", [1, 128], F32)
        onesb = ST("onesbc", [128, 1], F32)
        segc = ST("segcc", [8, CH_B], F32)
        selc = ST("selcc", [8, 4 * 128], BF16)
        selh = ST("selhc", [8, 8 * 128], F32)
        cwc = ST("cwcc", [128, KI * KCONV], F32)
        cbc = ST("cbcc", [128, KI], F32)
        bdtc = ST("bdtcc", [8, 1], F32)
        acol = ST("acolc", [8, 1], F32)
        dcol = ST("dcolc", [128, 8], F32)
        dic = ST("dicc", [128, 8 * 128], BF16)
        gnwc = ST("gnwcc", [128, KI], F32)
        gnbc = ST("gnbcc", [128, KI], F32)
        cst = ST("cstc", [128, 2], F32)

        psum = [ctx.enter_context(nc.psum_tensor(f"pb{i}", [128, 512], F32))
                for i in range(6)]
        ptr = [ctx.enter_context(nc.psum_tensor(f"pt{i}", [128, 1024], BF16))
               for i in range(2)]
        PS = [f"pb{i}" for i in range(6)]
        PT = [f"pt{i}" for i in range(2)]

        sems = {e: ctx.enter_context(nc.semaphore(f"sm_{e}")) for e in Prog.ENGS}
        for q in ("sp", "pool"):
            for s in range(Prog.NSLOT):
                sems[f"{q}{s}"] = ctx.enter_context(nc.semaphore(f"sm_{q}{s}"))
        block = ctx.enter_context(nc.Block())

        def dma(queue, out, in_, reads, writes):
            p.op(queue, reads=reads, writes=writes,
                 emit=lambda e, o=out, i=in_: e.dma_start(out=o, in_=i), dma=True)

        bfv = lambda t, c0, c1_: t[:, c0:c1_]
        f32v = lambda t, c0, c1_: t[:].bitcast(F32)[:, c0:c1_]
        v_x0p = [bfv(G5, 0, 2048), bfv(G4, 0, 2048)]
        v_x1p = [bfv(G5, 2048, 4096), bfv(G4, 2048, 4096)]
        v_utmp = [bfv(G1, 0, 2048), bfv(G1, 2048, 4096)]
        v_ufmp = [bfv(G1, 4096, 6144), bfv(G1, 6144, 8192),
                  bfv(G3, 4096, 6144), bfv(G3, 6144, 8192)]
        u2_src = [(G1, 4096), (G3, 4096)]
        v_u1fp = [bfv(G3, 0, 2048), bfv(G3, 2048, 4096)]
        v_u2 = bfv(G1, 0, KD * CH_B)
        v_xcb1 = bfv(G2, 0, 2048)
        v_zb1 = bfv(G2, 2048, 4096)
        v_xcr = bfv(G1, 0, 8192)
        v_xcs = bfv(G2, 0, 8192)
        v_yt = bfv(G3, 0, 8192)
        v_xtmA = bfv(G4, 0, 2048)
        v_xtmB = bfv(G4, 2048, 4096)
        v_bfm = bfv(G4, 4096, 6144)
        v_ctil = bfv(G4, 6144, 8192)
        v_btmA = bfv(G5, 0, 1024)
        v_btmB = bfv(G5, 1024, 2048)
        v_mg = bfv(G5, 2048, 3072)
        v_eb = bfv(G5, 3072, 4096)
        v_cth = bfv(G5, 4096, 6144)
        v_yrp = [bfv(G1, 0, 4096), bfv(G1, 4096, 8192)]
        v_zrp = [bfv(G2, 0, 4096), bfv(G2, 4096, 8192)]
        v_gyp = [bfv(G3, 0, 4096), bfv(G3, 4096, 8192)]
        v_x1r = bfv(G4, 0, 2048)
        v_x2p = [bfv(G4, 2048, 4096), bfv(G5, 0, 2048)]
        v_utm2 = bfv(G4, 4096, 6144)
        v_ufm2p = [bfv(G4, 6144, 8192), bfv(G5, 2048, 4096)]
        v_x3 = f32v(G5, 2048, 3072)

        ALL_POOL_KEYS = ["x0_0", "x0_1", "x1b_0", "x1b_1", "utm_0", "utm_1",
                         "ufm_0", "ufm_1", "ufm_2", "ufm_3", "u1f_0", "u1f_1", "x2_0", "x2_1", "ufm2_0", "ufm2_1",
                         "u2", "xcb1", "zb1",
                         "xcr", "xcs", "xcs0", "xcs1", "yt", "xtmA", "xtmB", "bfm", "ctil",
                         "btmA", "btmB", "mg", "eb", "cth", "yr", "zr", "gy",
                         "x1r", "utm2", "x3", "yr_0", "yr_1", "zr_0", "zr_1", "gy_0", "gy_1"]

        p.op("dve", [], ["cst"], emit=lambda e: e.memset(cst[:, 0:1], EPS))
        p.op("dve", [], ["cst"], emit=lambda e: e.memset(cst[:, 1:2], -0.5))
        dma("sp", ident[:], dident, [], ["ident"])
        dma("sp", identf[:], didentf, [], ["identf"])
        dma("sp", negr[:], dnegr, [], ["negr"])
        dma("sp", onesc[:], dones, [], ["ones"])
        dma("sp", onesb[:], donesb, [], ["onesb"])
        dma("sp", segc[:], dseg, [], ["seg"])
        dma("sp", selc[:], dsel, [], ["sel"])
        dma("sp", selh[:], dselh, [], ["selh"])
        dma("sp", cwc[:], dcw, [], ["cwc"])
        dma("sp", cbc[:], dcb, [], ["cbc"])
        dma("sp", bdtc[:], dbdt, [], ["bdtc"])
        dma("sp", gnwc[:], dgnw, [], ["gnwc"])
        dma("sp", gnbc[:], dgnb, [], ["gnbc"])
        dma("sp", acol[:], dloga, [], ["acol"])
        dma("sp", dcol[0:8, 0:1], ddpar, [], ["dcol"])

        p.op("act", ["acol"], ["acol"],
             emit=lambda e: e.activation(out=acol[:], in_=acol[:], func=AF.Exp))
        p.op("dve", ["acol"], ["acol"],
             emit=lambda e: e.tensor_scalar(out=acol[:], in0=acol[:], scalar1=-1.0,
                                            scalar2=None, op0=ALU.mult))
        p.op("pe", ["dcol", "identf"], [PS[0]],
             emit=lambda e: e.transpose(psum[0][0:1, 0:8], dcol[0:8, 0:1],
                                        identf[0:8, 0:8]))
        p.op("dve", [PS[0]], ["dcol"],
             emit=lambda e: e.tensor_copy(out=dcol[0:1, 0:8], in_=psum[0][0:1, 0:8]))
        p.op("pe", ["dcol", "ones"], [PS[1]],
             emit=lambda e: e.matmul(psum[1][:, 0:8], onesc[0:1, 0:128],
                                     dcol[0:1, 0:8], start=True, stop=True))
        p.op("dve", [PS[1]], ["dcol"],
             emit=lambda e: e.tensor_copy(out=dcol[:, 0:8], in_=psum[1][:, 0:8]))
        for h in range(8):
            p.op("dve", ["dcol", "ident"], ["dic"],
                 emit=lambda e, h=h: e.tensor_scalar(
                     out=dic[:, h * 128:(h + 1) * 128], in0=ident[:],
                     scalar1=dcol[:, h:h + 1], scalar2=None, op0=ALU.mult))

        def prep_w(dram, rows, cols, dst, dst_key, col_of, sign=False):
            n_ot = rows // 128
            n_kt = cols // 128
            for ot in range(n_ot):
                dma("pool", wstg[:, 0:cols], dram[ot * 128:(ot + 1) * 128, :],
                    [], ["wstg"])
                if sign:
                    p.op("act", ["wstg"], ["wstg"],
                         emit=lambda e, cols=cols: e.sign(out=wstg[:, 0:cols],
                                                          in_=wstg[:, 0:cols]))
                for g in range(0, n_kt, 8):
                    hi = min(g + 8, n_kt)
                    bi = (ot * 2 + g // 8) % 2
                    bank, bkey = ptr[bi], PT[bi]
                    for kt in range(g, hi):
                        p.op("pe", ["wstg", "ident"], [bkey],
                             emit=lambda e, kt=kt, g=g, bank=bank: e.transpose(
                                 bank[:, (kt - g) * 128:(kt - g + 1) * 128],
                                 wstg[:, kt * 128:(kt + 1) * 128], ident[:]))
                    for kt in range(g, hi):
                        col = col_of(kt, ot)
                        p.op("dve", [bkey], [dst_key],
                             emit=lambda e, kt=kt, g=g, col=col, bank=bank:
                             e.tensor_copy(out=dst[:, col:col + 128],
                                           in_=bank[:, (kt - g) * 128:(kt - g + 1) * 128]))

        prep_w(dWf, D, D, wS, "wS", lambda kt, ot: kt * D + ot * 128, sign=True)
        prep_w(dWx, INNER, D, slotA, "slotA", lambda kt, ot: kt * INNER + ot * 128)
        prep_w(dWz, INNER, D, slotB, "slotB", lambda kt, ot: kt * INNER + ot * 128)

        def layernorm(src, src_key, dst, dst_key, par=0):
            o = par * NT_A
            o2 = par * 2 * NT_A
            kc = [f"c1_{par}", f"c2_{par}", f"cm_{par}", f"cv_{par}",
                  f"cnm_{par}", f"crv_{par}"]
            for tt in range(NT_A):
                sl = src[:, tt * D:(tt + 1) * D]
                p.op("dve", [src_key], [kc[0]],
                     emit=lambda e, sl=sl, tt=tt: e.reduce_sum(
                         out=c1[:, o + tt:o + tt + 1], in_=sl, axis=AX.X))
                for hf in range(2):
                    p.op("act", [src_key], [kc[1], "dumpA"],
                         emit=lambda e, src=src, tt=tt, hf=hf: e.activation(
                             out=dumpA[:], in_=src[:, tt * D + hf * 512:
                                                   tt * D + (hf + 1) * 512],
                             func=AF.Square,
                             accum_out=c2[:, o2 + hf * NT_A + tt:
                                          o2 + hf * NT_A + tt + 1]))
            p.op("dve", [kc[0]], [kc[2]],
                 emit=lambda e: e.tensor_scalar(out=cm[:, o:o + NT_A],
                                                in0=c1[:, o:o + NT_A],
                                                scalar1=1.0 / D, scalar2=None,
                                                op0=ALU.mult))
            p.op("dve", [kc[1]], [kc[3]],
                 emit=lambda e: e.tensor_add(out=cv[:, o:o + NT_A],
                                             in0=c2[:, o2:o2 + NT_A],
                                             in1=c2[:, o2 + NT_A:o2 + 2 * NT_A]))
            p.op("dve", [kc[3]], [kc[3]],
                 emit=lambda e: e.tensor_scalar(out=cv[:, o:o + NT_A],
                                                in0=cv[:, o:o + NT_A],
                                                scalar1=1.0 / D, scalar2=None,
                                                op0=ALU.mult))
            p.op("dve", [kc[2]], [kc[4]],
                 emit=lambda e: e.tensor_mul(out=cnm[:, o:o + NT_A],
                                             in0=cm[:, o:o + NT_A],
                                             in1=cm[:, o:o + NT_A]))
            p.op("dve", [kc[3], kc[4]], [kc[3]],
                 emit=lambda e: e.tensor_sub(out=cv[:, o:o + NT_A],
                                             in0=cv[:, o:o + NT_A],
                                             in1=cnm[:, o:o + NT_A]))
            p.op("act", [kc[3], "cst"], [kc[3]],
                 emit=lambda e: e.activation(out=cv[:, o:o + NT_A],
                                             in_=cv[:, o:o + NT_A], func=AF.Ln,
                                             bias=cst[:, 0:1]))
            p.op("act", [kc[3], "cst"], [kc[5]],
                 emit=lambda e: e.activation(out=crv[:, o:o + NT_A],
                                             in_=cv[:, o:o + NT_A], func=AF.Exp,
                                             scale=cst[:, 1:2]))
            p.op("dve", [kc[2], kc[5]], [kc[4]],
                 emit=lambda e: e.tensor_mul(out=cnm[:, o:o + NT_A],
                                             in0=cm[:, o:o + NT_A],
                                             in1=crv[:, o:o + NT_A]))
            p.op("dve", [kc[4]], [kc[4]],
                 emit=lambda e: e.tensor_scalar(out=cnm[:, o:o + NT_A],
                                                in0=cnm[:, o:o + NT_A],
                                                scalar1=-1.0, scalar2=None,
                                                op0=ALU.mult))
            for tt in range(NT_A):
                p.op("act", [src_key, kc[5], kc[4]], [dst_key],
                     emit=lambda e, tt=tt: e.activation(
                         out=dst[:, tt * D:(tt + 1) * D],
                         in_=src[:, tt * D:(tt + 1) * D], func=AF.Identity,
                         bias=cnm[:, o + tt:o + tt + 1],
                         scale=crv[:, o + tt:o + tt + 1]))

        def transpose_u(src, src_key, dst, dst_key, chunk):
            nt = chunk // 128
            for tt in range(nt):
                bi = tt % 2
                bank, bkey = ptr[bi], PT[bi]
                for kt in range(KD):
                    p.op("pe", [src_key, "ident"], [bkey],
                         emit=lambda e, tt=tt, kt=kt, bank=bank: e.transpose(
                             bank[:, kt * 128:(kt + 1) * 128],
                             src[:, tt * D + kt * 128:tt * D + (kt + 1) * 128],
                             ident[:]))
                for kt in range(KD):
                    p.op("dve", [bkey], [dst_key],
                         emit=lambda e, tt=tt, kt=kt, bank=bank, chunk=chunk:
                         e.tensor_copy(
                             out=dst[:, kt * chunk + tt * 128:kt * chunk + (tt + 1) * 128],
                             in_=bank[:, kt * 128:(kt + 1) * 128]))

        def p1a1_chunk(b, c):
            T0 = c * CH_A
            par = c % 2
            v_x0, v_x1 = v_x0p[par], v_x1p[par]
            v_utm, v_u1f, v_ufm = v_utmp[par], v_u1fp[par], v_ufmp[c % 4]
            kx0, kx1 = f"x0_{par}", f"x1b_{par}"
            kut, ku1, kuf = f"utm_{par}", f"u1f_{par}", f"ufm_{c % 4}"
            x_view = bass.AP(dx.tensor, (b * L + T0) * D,
                             [[D, 128], [128 * D, NT_A], [1, D]])
            dma("pool", v_x0.rearrange("p (a b) -> p a b", a=NT_A), x_view,
                [], [kx0])
            layernorm(v_x0, kx0, v_utm, kut, par)
            transpose_u(v_utm, kut, v_u1f, ku1, CH_A)
            for tt in range(NT_A):
                for nh in range(2):
                    bi = 4 + (tt * 2 + nh) % 2
                    bank, bkey = psum[bi], PS[bi]

                    def f1(e, tt=tt, nh=nh, bank=bank, v_u1f=v_u1f):
                        last = None
                        for k in range(KD):
                            last = e.matmul(
                                bank[:],
                                v_u1f[:, k * CH_A + tt * 128:k * CH_A + (tt + 1) * 128],
                                wS[:, k * D + nh * 512:k * D + (nh + 1) * 512],
                                start=(k == 0), stop=(k == KD - 1))
                        return last
                    p.op("pe", [ku1, "wS"], [bkey], emit=f1)
                    p.op("dve", [bkey, kx0], [kx1],
                         emit=lambda e, tt=tt, nh=nh, bank=bank, v_x1=v_x1,
                         v_x0=v_x0:
                         e.scalar_tensor_tensor(
                             out=v_x1[:, tt * D + nh * 512:tt * D + (nh + 1) * 512],
                             in0=bank[:], scalar=0.5,
                             in1=v_x0[:, tt * D + nh * 512:tt * D + (nh + 1) * 512],
                             op0=ALU.mult, op1=ALU.add))
            x1_view = bass.AP(sp_x1.tensor, (b * L + T0) * D,
                              [[D, 128], [128 * D, NT_A], [1, D]])
            dma("sp", x1_view, v_x1.rearrange("p (a b) -> p a b", a=NT_A),
                [kx1], ["sp_x1"])
            layernorm(v_x1, kx1, v_utm, kut, par)
            transpose_u(v_utm, kut, v_ufm, kuf, CH_A)

        def u2_rhs(k, pp):
            t, off = u2_src[pp]
            return bass.AP(t, off + k * CH_A,
                           [[8192, 128], [2048, 2], [1, CH_A]])

        for b in range(BPC if _PH >= 2 else 0):
            p.op("dve", [], ["halo"], emit=lambda e: e.memset(halo[:], 0.0))
            p1a1_chunk(b, 0)
            p1a1_chunk(b, 1)
            for cp in range(NCH_B):
                T0 = cp * CH_B
                if cp + 1 < NCH_B:
                    p1a1_chunk(b, 2 * cp + 2)
                    p1a1_chunk(b, 2 * cp + 3)
                if _PH < 3:
                    continue
                for ot in range(KI):
                    bi = ot % 4
                    bank, bkey = psum[bi], PS[bi]

                    pp = cp % 2
                    ku2a, ku2b = f"ufm_{2 * pp}", f"ufm_{2 * pp + 1}"

                    def wxmm(e, ot=ot, bank=bank, pp=pp):
                        last = None
                        for k in range(KD):
                            last = e.matmul(
                                bank[:],
                                slotA[:, k * INNER + ot * 128:k * INNER + (ot + 1) * 128],
                                u2_rhs(k, pp), start=(k == 0), stop=(k == KD - 1))
                        return last
                    p.op("pe", [ku2a, ku2b, "slotA"], [bkey], emit=wxmm)
                    xpc = xp if ot % 2 == 0 else xpB
                    kxp = "xp" if ot % 2 == 0 else "xpB"
                    p.op("act", [bkey], [kxp],
                         emit=lambda e, bank=bank, xpc=xpc: e.activation(
                             out=xpc[:, 3:3 + CH_B], in_=bank[:], func=AF.Copy))
                    p.op("dve", ["halo", kxp], [kxp],
                         emit=lambda e, ot=ot, xpc=xpc: e.tensor_copy(
                             out=xpc[:, 0:3], in_=halo[:, ot * 3:ot * 3 + 3]))
                    p.op("dve", [kxp], ["halo"],
                         emit=lambda e, ot=ot, xpc=xpc: e.tensor_copy(
                             out=halo[:, ot * 3:ot * 3 + 3],
                             in_=xpc[:, CH_B:CH_B + 3]))
                    acc0 = acc_a if ot % 2 == 0 else acc_b
                    ka0 = "acc_a" if ot % 2 == 0 else "acc_b"
                    p.op("dve", [kxp, "cwc", "cbc"], [ka0],
                         emit=lambda e, ot=ot, xpc=xpc, acc0=acc0: e.tensor_scalar(
                             out=acc0[:], in0=xpc[:, 0:CH_B],
                             scalar1=cwc[:, ot * 4:ot * 4 + 1],
                             scalar2=cbc[:, ot:ot + 1], op0=ALU.mult, op1=ALU.add))
                    p.op("dve", [kxp, "cwc", ka0], [ka0],
                         emit=lambda e, ot=ot, xpc=xpc, acc0=acc0:
                         e.scalar_tensor_tensor(
                             out=acc0[:], in0=xpc[:, 1:1 + CH_B],
                             scalar=cwc[:, ot * 4 + 1:ot * 4 + 2], in1=acc0[:],
                             op0=ALU.mult, op1=ALU.add))
                    p.op("dve", [kxp, "cwc", ka0], [ka0],
                         emit=lambda e, ot=ot, xpc=xpc, acc0=acc0:
                         e.scalar_tensor_tensor(
                             out=acc0[:], in0=xpc[:, 2:2 + CH_B],
                             scalar=cwc[:, ot * 4 + 2:ot * 4 + 3], in1=acc0[:],
                             op0=ALU.mult, op1=ALU.add))
                    p.op("dve", [kxp, "cwc", ka0], [ka0],
                         emit=lambda e, ot=ot, xpc=xpc, acc0=acc0:
                         e.scalar_tensor_tensor(
                             out=acc0[:], in0=xpc[:, 3:3 + CH_B],
                             scalar=cwc[:, ot * 4 + 3:ot * 4 + 4], in1=acc0[:],
                             op0=ALU.mult, op1=ALU.add))
                    p.op("act", [ka0], ["xcb1"],
                         emit=lambda e, ot=ot, acc0=acc0: e.activation(
                             out=v_xcb1[:, (ot % 4) * CH_B:(ot % 4 + 1) * CH_B],
                             in_=acc0[:], func=AF.Silu))

                    def wzmm(e, ot=ot, bank=bank, pp=pp):
                        last = None
                        for k in range(KD):
                            last = e.matmul(
                                bank[:],
                                slotB[:, k * INNER + ot * 128:k * INNER + (ot + 1) * 128],
                                u2_rhs(k, pp), start=(k == 0), stop=(k == KD - 1))
                        return last
                    p.op("pe", [ku2a, ku2b, "slotB"], [bkey], emit=wzmm)
                    p.op("act", [bkey], ["zb1"],
                         emit=lambda e, ot=ot, bank=bank: e.activation(
                             out=v_zb1[:, (ot % 4) * CH_B:(ot % 4 + 1) * CH_B],
                             in_=bank[:], func=AF.Copy))
                    if ot % 4 == 3:
                        q = ot // 4
                        xc_view = bass.AP(sp_xc.tensor,
                                          b * INNER * L + q * 4 * 128 * L + T0,
                                          [[L, 128], [128 * L, 4], [1, CH_B]])
                        dma("sp", xc_view,
                            bass.AP(v_xcb1.tensor, 0,
                                    [[8192, 128], [CH_B, 4], [1, CH_B]]),
                            ["xcb1"], ["sp_xc"])
                        z_view = bass.AP(sp_z.tensor,
                                         b * INNER * L + q * 4 * 128 * L + T0,
                                         [[L, 128], [128 * L, 4], [1, CH_B]])
                        dma("pool", z_view,
                            bass.AP(v_zb1.tensor, 2048,
                                    [[8192, 128], [CH_B, 4], [1, CH_B]]),
                            ["zb1"], ["sp_z"])

        prep_w(dWb, PO, INNER, slotA, "slotA", lambda kt, ot: kt * PK + ot * 128)
        prep_w(dWc, PO, INNER, slotA, "slotA",
               lambda kt, ot: kt * PK + PO + ot * 128)
        dma("pool", wstg[0:8, 0:INNER], dWdt, [], ["wstg"])
        for kt in range(KI):
            bi = kt % 2
            bank, bkey = ptr[bi], PT[bi]
            p.op("pe", ["wstg", "ident"], [bkey],
                 emit=lambda e, kt=kt, bank=bank: e.transpose(
                     bank[:, 0:8], wstg[0:8, kt * 128:(kt + 1) * 128],
                     ident[0:8, 0:8]))
            p.op("dve", [bkey], ["slotA"],
                 emit=lambda e, kt=kt, bank=bank: e.tensor_copy(
                     out=slotA[:, kt * PK + 2 * PO:kt * PK + 2 * PO + 8],
                     in_=bank[:, 0:8]))

        p.fence(ALL_POOL_KEYS)
        p.op("dve", [], ["ctil"], emit=lambda e: e.memset(v_ctil, 0.0))
        p.op("dve", [], ["cth"], emit=lambda e: e.memset(v_cth, 0.0))
        p.op("dve", [], ["btmA"], emit=lambda e: e.memset(v_btmA, 0.0))
        p.op("dve", [], ["btmB"], emit=lambda e: e.memset(v_btmB, 0.0))
        for b in range(BPC if _PH >= 4 else 0):
            p.op("dve", [], ["S_a"], emit=lambda e: e.memset(S_a[:], 0.0))
            p.op("dve", [], ["S_bf"], emit=lambda e: e.memset(S_bf[:], 0.0))
            S_cur, S_nxt = S_a, S_b
            S_keys = ["S_a", "S_b"]
            for c in range(NCH_B):
                T0 = c * CH_B
                xcs_t = [G2, G1][c % 2]
                v_xcs_c = xcs_t[:, 0:8192]
                kxcs = f"xcs{c % 2}"
                for q in range(4):
                    xc_view = bass.AP(sp_xc.tensor,
                                      b * INNER * L + q * 4 * 128 * L + T0,
                                      [[L, 128], [128 * L, 4], [1, CH_B]])
                    dma("sp",
                        bass.AP(xcs_t, q * 4 * CH_B,
                                [[8192, 128], [CH_B, 4], [1, CH_B]]),
                        xc_view, ["sp_xc"], [kxcs])
                if _SB < 2:
                    continue
                def dtmm(e, v_xcs_c=v_xcs_c):
                    last = None
                    for k in range(KI):
                        last = e.matmul(
                            psum[4][0:8, :],
                            slotA[:, k * PK + 2 * PO:k * PK + 2 * PO + 8],
                            v_xcs_c[:, k * CH_B:(k + 1) * CH_B],
                            start=(k == 0), stop=(k == KI - 1))
                    return last
                p.op("pe", [kxcs, "slotA"], [PS[4]], emit=dtmm)
                if _SB < 3:
                    continue
                pr = tin[0:8, 0:CH_B]
                t1 = dtv[0:8, :]
                t2 = tin[0:8, CH_B:2 * CH_B]
                p.op("dve", [PS[4], "bdtc"], ["tin"],
                     emit=lambda e: e.tensor_scalar(out=pr, in0=psum[4][0:8, :],
                                                    scalar1=bdtc[:], scalar2=None,
                                                    op0=ALU.add))
                p.op("dve", ["tin"], ["dtv"],
                     emit=lambda e: e.tensor_scalar(out=t1, in0=pr, scalar1=-1.0,
                                                    scalar2=None, op0=ALU.mult))
                p.op("dve", ["tin", "dtv"], ["tin"],
                     emit=lambda e: e.tensor_tensor(out=t2, in0=pr, in1=t1,
                                                    op=ALU.min))
                p.op("act", ["tin"], ["tin"],
                     emit=lambda e: e.activation(out=t2, in_=t2, func=AF.Exp))
                p.op("act", ["tin"], ["tin"],
                     emit=lambda e: e.activation(out=t2, in_=t2, func=AF.Ln,
                                                 bias=1.0))
                p.op("dve", ["tin"], ["dtv"],
                     emit=lambda e: e.tensor_scalar(out=t1, in0=pr, scalar1=0.0,
                                                    scalar2=None, op0=ALU.max))
                p.op("dve", ["tin", "dtv"], ["dtv"],
                     emit=lambda e: e.tensor_tensor(out=dtv[:], in0=t1, in1=t2,
                                                    op=ALU.add))
                p.op("dve", ["dtv", "acol"], ["scbf"],
                     emit=lambda e: e.tensor_scalar(out=scbf[:], in0=dtv[:],
                                                    scalar1=acol[:], scalar2=None,
                                                    op0=ALU.mult))
                p.op("dve", ["scbf", "seg"], ["s_sb"],
                     emit=lambda e: e.tensor_tensor_scan(
                         out=s_sb[0:8, :], data0=segc[:], data1=scbf[:], initial=0.0,
                         op0=ALU.mult, op1=ALU.add))
                p.op("act", ["dtv"], ["tin"],
                     emit=lambda e: e.activation(out=t2, in_=dtv[:], func=AF.Ln))
                p.op("dve", ["tin", "s_sb"], ["nslf"],
                     emit=lambda e: e.tensor_sub(out=nslf[:], in0=t2,
                                                 in1=s_sb[0:8, :]))
                p.op("act", ["s_sb"], ["esr"],
                     emit=lambda e: e.activation(out=esr[:], in_=s_sb[0:8, :],
                                                 func=AF.Exp))
                for su in range(NSUB):
                    p.op("act", ["s_sb", "nslf"], ["scbf"],
                         emit=lambda e, su=su: e.activation(
                             out=scbf[:, su * SUB:(su + 1) * SUB],
                             in_=nslf[:, su * SUB:(su + 1) * SUB], func=AF.Exp,
                             bias=s_sb[0:8, (su + 1) * SUB - 1:(su + 1) * SUB]))
                if _SB < 4:
                    continue
                for su in range(NSUB):
                    bi = su % 2
                    p.op("pe", ["nslf", "identf"], [PS[5]],
                         emit=lambda e, su=su: e.transpose(
                             psum[5][:, 0:8], nslf[:, su * SUB:(su + 1) * SUB],
                             identf[0:8, 0:8]))
                    p.op("dve", [PS[5]], ["colT"],
                         emit=lambda e, su=su: e.tensor_copy(
                             out=colT[:, su * 16:su * 16 + 8], in_=psum[5][:, 0:8]))
                    p.op("pe", ["scbf", "identf"], [PS[5]],
                         emit=lambda e, su=su: e.transpose(
                             psum[5][:, 0:8], scbf[:, su * SUB:(su + 1) * SUB],
                             identf[0:8, 0:8]))
                    p.op("dve", [PS[5]], ["colT"],
                         emit=lambda e, su=su: e.tensor_copy(
                             out=colT[:, su * 16 + 8:su * 16 + 16],
                             in_=psum[5][:, 0:8]))
                for su in range(NSUB):
                    def lam(e, su=su):
                        last = None
                        for g in range(4):
                            last = e.matmul(
                                psum[5][:, su * 4 + g:su * 4 + g + 1],
                                selc[:, g * 128:(g + 1) * 128],
                                esr[:, (su + 1) * SUB - 1:(su + 1) * SUB],
                                start=True, stop=True)
                        return last
                    p.op("pe", ["esr", "sel"], [PS[5]], emit=lam)
                    p.op("dve", [PS[5]], ["lamc"],
                         emit=lambda e, su=su: e.tensor_copy(
                             out=lamc[:, su * 4:su * 4 + 4],
                             in_=psum[5][:, su * 4:su * 4 + 4]))
                if _SB < 5:
                    continue
                for m in range(4):
                    bi = m % 4
                    bank, bkey = psum[bi], PS[bi]

                    def bmm(e, m=m, bank=bank, v_xcs_c=v_xcs_c):
                        last = None
                        for k in range(KI):
                            last = e.matmul(
                                bank[:],
                                slotA[:, k * PK + m * 128:k * PK + (m + 1) * 128],
                                v_xcs_c[:, k * CH_B:(k + 1) * CH_B],
                                start=(k == 0), stop=(k == KI - 1))
                        return last
                    p.op("pe", [kxcs, "slotA"], [bkey], emit=bmm)
                    p.op("dve", [bkey], ["bfm"],
                         emit=lambda e, m=m, bank=bank: e.tensor_copy(
                             out=v_bfm[:, m * CH_B:(m + 1) * CH_B], in_=bank[:]))
                for m in range(4):
                    bi = m % 4
                    bank, bkey = psum[bi], PS[bi]

                    def cmm(e, m=m, bank=bank, v_xcs_c=v_xcs_c):
                        last = None
                        for k in range(KI):
                            last = e.matmul(
                                bank[:],
                                slotA[:, k * PK + PO + m * 128:k * PK + PO + (m + 1) * 128],
                                v_xcs_c[:, k * CH_B:(k + 1) * CH_B],
                                start=(k == 0), stop=(k == KI - 1))
                        return last
                    p.op("pe", [kxcs, "slotA"], [bkey], emit=cmm)

                    def esmm(e, m=m):
                        return e.matmul(psum[4][:], selc[:, m * 128:(m + 1) * 128],
                                        esr[:], start=True, stop=True)
                    p.op("pe", ["esr", "sel"], [PS[4]], emit=esmm)
                    p.op("dve", [PS[4]], ["esbuf"],
                         emit=lambda e: e.tensor_copy(out=esb[:], in_=psum[4][:]))
                    p.op("dve", [bkey, "esbuf"], ["ctil"],
                         emit=lambda e, m=m, bank=bank: e.tensor_tensor(
                             out=v_ctil[0:64, m * CH_B:(m + 1) * CH_B],
                             in0=bank[0:64, :], in1=esb[0:64, :], op=ALU.mult))
                    p.op("dve", [bkey, "esbuf"], ["cth"],
                         emit=lambda e, m=m, bank=bank: e.tensor_tensor(
                             out=v_cth[64:128, m * CH_B:(m + 1) * CH_B],
                             in0=bank[64:128, :], in1=esb[64:128, :], op=ALU.mult))

                if _SB < 6:
                    continue
                for su in range(NSUB):
                    t0 = su * SUB
                    xtm = v_xtmA if su % 2 == 0 else v_xtmB
                    xkey = "xtmA" if su % 2 == 0 else "xtmB"
                    btm = v_btmA if su % 2 == 0 else v_btmB
                    btkey = "btmA" if su % 2 == 0 else "btmB"
                    xv = bass.AP(sp_xc.tensor, b * INNER * L + T0 + t0,
                                 [[L, INNER], [1, 128]])
                    p.op("sp", ["sp_xc"], [xkey],
                         emit=lambda e, xv=xv, xtm=xtm: e.dma_start(
                             out=xtm[:], in_=xv, transpose=True), dma=True)
                    for h in range(8):
                        bi = h % 2
                        bank, bkey = ptr[bi], PT[bi]
                        p.op("pe", ["bfm", "ident"], [bkey],
                             emit=lambda e, h=h, bank=bank, t0=t0: e.transpose(
                                 bank[:, 0:64],
                                 v_bfm[(h % 2) * 64:(h % 2) * 64 + 64,
                                       (h // 2) * CH_B + t0:(h // 2) * CH_B + t0 + 128],
                                 ident[(h % 2) * 64:(h % 2) * 64 + 64,
                                       (h % 2) * 64:(h % 2) * 64 + 64]))
                        bcol = (h // 2) * 256 + (h % 2) * 192
                        p.op("dve", [bkey, "colT"], [btkey],
                             emit=lambda e, h=h, su=su, bank=bank, btm=btm, bcol=bcol:
                             e.tensor_scalar(
                                 out=btm[:, bcol:bcol + 64],
                                 in0=bank[:, 0:64],
                                 scalar1=colT[:, su * 16 + 8 + h:su * 16 + 9 + h],
                                 scalar2=None, op0=ALU.mult))
                    if _SB < 7:
                        continue
                    for hb in range(2):
                        mb, mkey = psum[2 + hb], PS[2 + hb]
                        gb, gkey = psum[4 + hb], PS[4 + hb]

                        def maskmm(e, hb=hb, mb=mb, t0=t0):
                            e.matmul(mb[:], ident[:], negr[:],
                                     start=True, stop=False)
                            last = None
                            for hq in range(4):
                                h = hb * 4 + hq
                                sl = mb[:, hq * 128:(hq + 1) * 128]
                                last = e.matmul(sl, selh[:, h * 128:(h + 1) * 128],
                                                s_sb[0:8, t0:t0 + 128],
                                                start=False, stop=(hq == 3))
                            return last
                        if _SM >= 1:
                            p.op("pe", ["s_sb", "selh", "ident", "negr"], [mkey],
                                 emit=maskmm)
                        for hq in range(4 if _SM >= 2 else 0):
                            h = hb * 4 + hq
                            p.op("act", [mkey, "colT"], ["eb"],
                                 emit=lambda e, h=h, hq=hq, su=su, mb=mb:
                                 e.activation(
                                     out=v_eb[:, h * 128:(h + 1) * 128],
                                     in_=mb[:, hq * 128:(hq + 1) * 128],
                                     func=AF.Exp,
                                     bias=colT[:, su * 16 + h:su * 16 + h + 1]))

                        def gmm(e, hb=hb, gb=gb, t0=t0):
                            last = None
                            for hq in range(4):
                                h = hb * 4 + hq
                                cpad = v_ctil if h % 2 == 0 else v_cth
                                last = e.matmul(
                                    gb[:, hq * 128:(hq + 1) * 128],
                                    v_bfm[:, (h // 2) * CH_B + t0:(h // 2) * CH_B + t0 + 128],
                                    cpad[:, (h // 2) * CH_B + t0:(h // 2) * CH_B + t0 + 128],
                                    start=True, stop=True)
                            return last
                        if _SM >= 3:
                            p.op("pe", ["bfm", "ctil", "cth"], [gkey], emit=gmm)
                        if _SM >= 4:
                            p.op("dve", [gkey, "eb"], ["mg"],
                                 emit=lambda e, gb=gb, hb=hb: e.tensor_tensor(
                                     out=v_mg[:, hb * 512:(hb + 1) * 512], in0=gb[:],
                                     in1=v_eb[:, hb * 512:(hb + 1) * 512], op=ALU.mult))
                            p.op("dve", ["mg", "dic"], ["mg"],
                                 emit=lambda e, hb=hb: e.tensor_tensor(
                                     out=v_mg[:, hb * 512:(hb + 1) * 512],
                                     in0=v_mg[:, hb * 512:(hb + 1) * 512],
                                     in1=dic[:, hb * 512:(hb + 1) * 512], op=ALU.add))
                    if _SB < 8:
                        continue
                    for j in range(4):
                        yb, ykey = psum[j], PS[j]

                        def ymm(e, j=j, t0=t0, xtm=xtm):
                            last = None
                            for q in range(4):
                                ct = 4 * j + q
                                h = ct // 2
                                sl = psum[j][:, q * 128:(q + 1) * 128]
                                e.matmul(sl,
                                         xtm[:, ct * 128:(ct + 1) * 128],
                                         v_mg[:, h * 128:(h + 1) * 128],
                                         start=True, stop=False)
                                cpad = v_ctil if h % 2 == 0 else v_cth
                                last = e.matmul(
                                    sl,
                                    S_bf[:, (h // 2) * HD + (ct % 2) * 128:
                                         (h // 2) * HD + (ct % 2) * 128 + 128],
                                    cpad[:, (h // 2) * CH_B + t0:(h // 2) * CH_B + t0 + 128],
                                    start=False, stop=True)
                            return last
                        p.op("pe", [xkey, "mg", "dic", "S_bf", "ctil", "cth"], [ykey],
                             emit=ymm)
                        for half in range(2):
                            col = c * NSUB * 8 + su * 8 + j * 2 + half
                            p.op("dve", [ykey], ["ys", "yt"],
                                 emit=lambda e, j=j, su=su, half=half, col=col:
                                 e.tensor_scalar(
                                     out=bass.AP(v_yt.tensor,
                                                 (4 * j + 2 * half) * CH_B + su * 128,
                                                 [[8192, 128], [CH_B, 2], [1, 128]]),
                                     in0=psum[j][:, half * 256:(half + 1) * 256],
                                     scalar1=1.0, scalar2=0.0, op0=ALU.mult,
                                     op1=ALU.add,
                                     accum_out=ys[:, col:col + 1]))
                            p.op("act", [ykey], ["yss", "dumpA"],
                                 emit=lambda e, j=j, half=half, col=col:
                                 e.activation(
                                     out=dumpA[:, 0:256],
                                     in_=psum[j][:, half * 256:(half + 1) * 256],
                                     func=AF.Square, accum_out=yss[:, col:col + 1]))
                    if _SB < 9:
                        continue
                    for g in range(4):
                        bi = 4 + (g % 2)
                        db, dkey = psum[bi], PS[bi]

                        def dsmm(e, g=g, db=db, xtm=xtm, btm=btm):
                            e.matmul(db[:, 0:HD],
                                     btm[:, g * 256:g * 256 + 128],
                                     xtm[:, (2 * g) * HD:(2 * g + 1) * HD],
                                     start=True, stop=False)
                            return e.matmul(
                                db[:, 0:HD],
                                btm[:, g * 256 + 128:g * 256 + 256],
                                xtm[:, (2 * g + 1) * HD:(2 * g + 2) * HD],
                                start=False, stop=True)
                        p.op("pe", [btkey, xkey], [dkey], emit=dsmm)
                        p.op("dve", [dkey, S_keys[0], "lamc"], [S_keys[1]],
                             emit=lambda e, g=g, su=su, db=db, Sc=S_cur, Sn=S_nxt:
                             e.scalar_tensor_tensor(
                                 out=Sn[:, g * HD:(g + 1) * HD],
                                 in0=Sc[:, g * HD:(g + 1) * HD],
                                 scalar=lamc[:, su * 4 + g:su * 4 + g + 1],
                                 in1=db[:, 0:HD], op0=ALU.mult, op1=ALU.add))
                        p.op("dve", [S_keys[1]], ["S_bf"],
                             emit=lambda e, g=g, Sn=S_nxt: e.tensor_copy(
                                 out=S_bf[:, g * HD:(g + 1) * HD],
                                 in_=Sn[:, g * HD:(g + 1) * HD]))
                    S_cur, S_nxt = S_nxt, S_cur
                    S_keys = S_keys[::-1]
                for q in range(4):
                    y_view = bass.AP(sp_y.tensor, b * INNER * L + q * 4 * 128 * L + T0,
                                     [[L, 128], [128 * L, 4], [1, CH_B]])
                    dma("pool", y_view,
                        bass.AP(v_yt.tensor, q * 4 * CH_B,
                                [[8192, 128], [CH_B, 4], [1, CH_B]]),
                        ["yt"], ["sp_y"])
            def gnred(e):
                e.matmul(psum[0][0:1, 0:NCH_B * NSUB * 8], onesb[:, 0:1], ys[:],
                         start=True, stop=True)
                return e.matmul(psum[1][0:1, 0:NCH_B * NSUB * 8], onesb[:, 0:1],
                                yss[:], start=True, stop=True)
            p.op("pe", ["ys", "yss", "onesb"], [PS[0], PS[1]], emit=gnred)
            p.op("dve", [PS[0]], ["stat8"],
                 emit=lambda e: e.reduce_sum(
                     out=stat8[0:1, 0:8],
                     in_=psum[0][0:1, 0:NCH_B * NSUB * 8].rearrange(
                         "p (a b) -> p b a", b=8), axis=AX.X))
            p.op("dve", [PS[1]], ["stat8"],
                 emit=lambda e: e.reduce_sum(
                     out=stat8[0:1, 8:16],
                     in_=psum[1][0:1, 0:NCH_B * NSUB * 8].rearrange(
                         "p (a b) -> p b a", b=8), axis=AX.X))
            NGN = float(L * HD)
            p.op("dve", ["stat8"], ["stat8"],
                 emit=lambda e: e.tensor_scalar(out=stat8[0:1, 16:24],
                                                in0=stat8[0:1, 0:8],
                                                scalar1=1.0 / NGN, scalar2=None,
                                                op0=ALU.mult))
            p.op("dve", ["stat8"], ["stat8"],
                 emit=lambda e: e.tensor_scalar(out=stat8[0:1, 24:32],
                                                in0=stat8[0:1, 8:16],
                                                scalar1=1.0 / NGN, scalar2=None,
                                                op0=ALU.mult))
            p.op("dve", ["stat8"], ["stat8"],
                 emit=lambda e: e.tensor_mul(out=stat8[0:1, 32:40],
                                             in0=stat8[0:1, 16:24],
                                             in1=stat8[0:1, 16:24]))
            p.op("dve", ["stat8"], ["stat8"],
                 emit=lambda e: e.tensor_sub(out=stat8[0:1, 40:48],
                                             in0=stat8[0:1, 24:32],
                                             in1=stat8[0:1, 32:40]))
            p.op("act", ["stat8", "cst"], ["stat8"],
                 emit=lambda e: e.activation(out=stat8[0:1, 40:48],
                                             in_=stat8[0:1, 40:48], func=AF.Ln,
                                             bias=cst[0:1, 0:1]))
            p.op("act", ["stat8", "cst"], ["stat8"],
                 emit=lambda e: e.activation(out=stat8[0:1, 48:56],
                                             in_=stat8[0:1, 40:48], func=AF.Exp,
                                             scale=cst[0:1, 1:2]))
            def bc16(e):
                e.matmul(psum[2][:, 0:8], onesc[0:1, 0:128], stat8[0:1, 48:56],
                         start=True, stop=True)
                return e.matmul(psum[2][:, 8:16], onesc[0:1, 0:128],
                                stat8[0:1, 16:24], start=True, stop=True)
            p.op("pe", ["stat8", "ones"], [PS[2]], emit=bc16)
            p.op("dve", [PS[2]], ["rinv8"],
                 emit=lambda e, b=b: e.tensor_copy(out=rinv8[:, b * 16:b * 16 + 16],
                                                   in_=psum[2][:, 0:16]))
            p.op("dve", ["rinv8", "gnwc"], ["gnwp"],
                 emit=lambda e, b=b: e.tensor_tensor(
                     out=gnwp[:, b * KI:(b + 1) * KI], in0=gnwc[:],
                     in1=bass.AP(rinv8, b * 16,
                                 [[BPC * 16, 128], [1, 8], [0, 2]]),
                     op=ALU.mult))
            p.op("dve", ["gnwp", "rinv8"], ["gnbp"],
                 emit=lambda e, b=b: e.tensor_tensor(
                     out=gnbp[:, b * KI:(b + 1) * KI],
                     in0=gnwp[:, b * KI:(b + 1) * KI],
                     in1=bass.AP(rinv8, b * 16 + 8,
                                 [[BPC * 16, 128], [1, 8], [0, 2]]),
                     op=ALU.mult))
            p.op("dve", ["gnbp", "gnbc"], ["gnbp"],
                 emit=lambda e, b=b: e.tensor_sub(
                     out=gnbp[:, b * KI:(b + 1) * KI], in0=gnbc[:],
                     in1=gnbp[:, b * KI:(b + 1) * KI]))

        prep_w(dWout, D, INNER, slotB, "slotB", lambda kt, ot: kt * D + ot * 128)

        p.fence(ALL_POOL_KEYS)

        def p2_loads(b, c):
            T0 = c * CH_A
            par = c % 2
            v_yr, v_zr = v_yrp[par], v_zrp[par]
            kyr, kzr = f"yr_{par}", f"zr_{par}"
            for q in range(4):
                y_view = bass.AP(sp_y.tensor, b * INNER * L + q * 4 * 128 * L + T0,
                                 [[L, 128], [128 * L, 4], [1, CH_A]])
                dma("sp",
                    bass.AP(v_yr.tensor, par * 4096 + q * 4 * CH_A,
                            [[8192, 128], [CH_A, 4], [1, CH_A]]),
                    y_view, ["sp_y"], [kyr])
                z_view = bass.AP(sp_z.tensor, b * INNER * L + q * 4 * 128 * L + T0,
                                 [[L, 128], [128 * L, 4], [1, CH_A]])
                dma("pool",
                    bass.AP(v_zr.tensor, par * 4096 + q * 4 * CH_A,
                            [[8192, 128], [CH_A, 4], [1, CH_A]]),
                    z_view, ["sp_z"], [kzr])

        def p2_gy(b, c):
            par = c % 2
            v_yr, v_zr, v_gy = v_yrp[par], v_zrp[par], v_gyp[par]
            kyr, kzr, kgy = f"yr_{par}", f"zr_{par}", f"gy_{par}"
            for og in range(KI // 2):
                ot = 2 * og
                acc0 = acc_a if og % 2 == 0 else acc_b
                accs = acc_s1
                ka = f"acca{og % 2}"
                kb = "accb"
                p.op("dve", [kyr, "gnwp", "gnbp"], [ka],
                     emit=lambda e, ot=ot, b=b, acc0=acc0, v_yr=v_yr:
                     e.tensor_scalar(
                         out=acc0[:, 0:CH_A],
                         in0=v_yr[:, ot * CH_A:(ot + 1) * CH_A],
                         scalar1=gnwp[:, b * KI + ot:b * KI + ot + 1],
                         scalar2=gnbp[:, b * KI + ot:b * KI + ot + 1],
                         op0=ALU.mult, op1=ALU.add))
                p.op("dve", [kyr, "gnwp", "gnbp", ka], [ka],
                     emit=lambda e, ot=ot, b=b, acc0=acc0, v_yr=v_yr:
                     e.tensor_scalar(
                         out=acc0[:, CH_A:2 * CH_A],
                         in0=v_yr[:, (ot + 1) * CH_A:(ot + 2) * CH_A],
                         scalar1=gnwp[:, b * KI + ot + 1:b * KI + ot + 2],
                         scalar2=gnbp[:, b * KI + ot + 1:b * KI + ot + 2],
                         op0=ALU.mult, op1=ALU.add))
                p.op("act", [kzr], [kb],
                     emit=lambda e, ot=ot, accs=accs, v_zr=v_zr: e.activation(
                         out=accs[:, 0:2 * CH_A],
                         in_=v_zr[:, ot * CH_A:(ot + 2) * CH_A], func=AF.Silu))
                p.op("dve", [ka, kb], [kgy],
                     emit=lambda e, ot=ot, acc0=acc0, accs=accs, v_gy=v_gy:
                     e.tensor_tensor(
                         out=v_gy[:, ot * CH_A:(ot + 2) * CH_A],
                         in0=acc0[:, 0:2 * CH_A], in1=accs[:, 0:2 * CH_A],
                         op=ALU.mult))

        def p2_main_a(b, c):
            T0 = c * CH_A
            par = c % 2
            v_x2 = v_x2p[par]
            v_gy = v_gyp[par]
            kx2, kgy = f"x2_{par}", f"gy_{par}"
            x1_view = bass.AP(sp_x1.tensor, (b * L + T0) * D,
                              [[D, 128], [128 * D, NT_A], [1, D]])
            dma("sp", v_x1r.rearrange("p (a b) -> p a b", a=NT_A), x1_view,
                ["sp_x1"], ["x1r"])
            for tt in range(NT_A):
                for nh in range(2):
                    bi = (tt * 2 + nh) % 4
                    bank, bkey = psum[bi], PS[bi]

                    def womm(e, tt=tt, nh=nh, bank=bank, v_gy=v_gy):
                        last = None
                        for k in range(KI):
                            last = e.matmul(
                                bank[:],
                                v_gy[:, k * CH_A + tt * 128:k * CH_A + (tt + 1) * 128],
                                slotB[:, k * D + nh * 512:k * D + (nh + 1) * 512],
                                start=(k == 0), stop=(k == KI - 1))
                        return last
                    p.op("pe", [kgy, "slotB"], [bkey], emit=womm)
                    p.op("dve", [bkey, "x1r"], [kx2],
                         emit=lambda e, tt=tt, nh=nh, bank=bank, v_x2=v_x2:
                         e.tensor_tensor(
                             out=v_x2[:, tt * D + nh * 512:tt * D + (nh + 1) * 512],
                             in0=bank[:],
                             in1=v_x1r[:, tt * D + nh * 512:tt * D + (nh + 1) * 512],
                             op=ALU.add))

        def p2_ln(b, c):
            par = c % 2
            layernorm(v_x2p[par], f"x2_{par}", v_utm2, "utm2", par)

        def p2_main_b(b, c):
            T0 = c * CH_A
            par = c % 2
            v_x2 = v_x2p[par]
            v_ufm2 = v_ufm2p[par]
            kx2, kuf2 = f"x2_{par}", f"ufm2_{par}"
            transpose_u(v_utm2, "utm2", v_ufm2, kuf2, CH_A)
            for tt in range(NT_A):
                for nh in range(2):
                    bi = 4 + (tt * 2 + nh) % 2
                    bank, bkey = psum[bi], PS[bi]

                    def f2(e, tt=tt, nh=nh, bank=bank, v_ufm2=v_ufm2):
                        last = None
                        for k in range(KD):
                            last = e.matmul(
                                bank[:],
                                v_ufm2[:, k * CH_A + tt * 128:k * CH_A + (tt + 1) * 128],
                                wS[:, k * D + nh * 512:k * D + (nh + 1) * 512],
                                start=(k == 0), stop=(k == KD - 1))
                        return last
                    p.op("pe", [kuf2, "wS"], [bkey], emit=f2)
                    p.op("dve", [bkey, kx2], ["x3"],
                         emit=lambda e, tt=tt, nh=nh, bank=bank, v_x2=v_x2:
                         e.scalar_tensor_tensor(
                             out=v_x3[:, nh * 512:(nh + 1) * 512],
                             in0=bank[:], scalar=0.5,
                             in1=v_x2[:, tt * D + nh * 512:tt * D + (nh + 1) * 512],
                             op0=ALU.mult, op1=ALU.add))
                out_view = bass.AP(dout.tensor, (b * L + T0 + tt * 128) * D,
                                   [[D, 128], [1, D]])
                dma("sp", out_view, v_x3[:], ["x3"], ["dout"])

        if _PH >= 5:
            chunks = [(b, c) for b in range(BPC) for c in range(NCH_A)]
            p2_loads(*chunks[0])
            p2_gy(*chunks[0])
            p2_loads(*chunks[1])
            p2_gy(*chunks[1])
            p2_main_a(*chunks[0])
            for i, (b, c) in enumerate(chunks):
                p2_ln(b, c)
                if i + 2 < len(chunks):
                    p2_loads(*chunks[i + 2])
                    p2_gy(*chunks[i + 2])
                if i + 1 < len(chunks):
                    p2_main_a(*chunks[i + 1])
                p2_main_b(b, c)

        p.op("pool", ["dout"], [], emit=lambda e: e.engine_nop())

        eng_map = {"pe": block.tensor, "dve": block.vector, "act": block.scalar,
                   "pool": block.gpsimd, "sp": block.sync}
        for eng in Prog.ENGS:
            ops = p.ops[eng]
            if not ops:
                continue

            def section(e, _ops=ops, _sem=sems[eng]):
                for waits, emit, slotkey in _ops:
                    for src, v in waits:
                        e.wait_ge(sems[src], v)
                    inst = emit(e)
                    if slotkey is None:
                        inst.then_inc(_sem, 1)
                    else:
                        inst.then_inc(sems[slotkey], 16)
            eng_map[eng](section)

    return nc


def _consts():
    ident = np.eye(128, dtype=BF)
    identf = np.eye(8, dtype=np.float32)
    negr = np.tile(np.where(np.arange(128)[:, None] > np.arange(128)[None, :],
                            -1e30, 0.0), (1, 4)).astype(BF)
    ones_c = np.ones((1, 128), np.float32)
    ones_b = np.ones((128, 1), np.float32)
    seg = np.ones((HEADS, CH_B), np.float32)
    seg[:, ::SUB] = 0.0
    sel = np.zeros((HEADS, 4 * 128), BF)
    for g in range(4):
        for pp in range(128):
            sel[2 * g + pp // 64, g * 128 + pp] = 1.0
    selh = np.zeros((HEADS, HEADS * 128), np.float32)
    for h in range(HEADS):
        selh[h, h * 128:(h + 1) * 128] = 1.0
    return dict(ident=ident, identf=identf, negr=negr,
                ones_c=ones_c, ones_b=ones_b, seg=seg, sel=sel, selh=selh)


def kernel(L0_attn_input_tensor, L0_attn_Wx, L0_attn_Wz, L0_attn_conv_w,
           L0_attn_conv_b, L0_attn_Wb, L0_attn_Wc, L0_attn_Wdt, L0_attn_b_dt,
           L0_attn_log_a, L0_attn_d_param, L0_attn_gn_w, L0_attn_gn_b,
           L0_attn_Wout, L0_ffn_weight):
    x = np.ascontiguousarray(L0_attn_input_tensor)
    consts = _consts()
    shared = dict(
        Wx=np.ascontiguousarray(L0_attn_Wx),
        Wz=np.ascontiguousarray(L0_attn_Wz),
        conv_w=np.ascontiguousarray(
            L0_attn_conv_w[:, 0, :].reshape(KI, 128, KCONV).transpose(1, 0, 2)
            .reshape(128, KI * KCONV)),
        conv_b=np.ascontiguousarray(L0_attn_conv_b.reshape(KI, 128).T),
        Wb=np.ascontiguousarray(L0_attn_Wb),
        Wc=np.ascontiguousarray(L0_attn_Wc),
        Wdt=np.ascontiguousarray(L0_attn_Wdt),
        b_dt=np.ascontiguousarray(L0_attn_b_dt.reshape(HEADS, 1)),
        log_a=np.ascontiguousarray(L0_attn_log_a.reshape(HEADS, 1)),
        d_param=np.ascontiguousarray(L0_attn_d_param.reshape(HEADS, 1)),
        gn_w=np.ascontiguousarray(L0_attn_gn_w.reshape(KI, 128).T),
        gn_b=np.ascontiguousarray(L0_attn_gn_b.reshape(KI, 128).T),
        Wout=np.ascontiguousarray(L0_attn_Wout),
        Wf=np.ascontiguousarray(L0_ffn_weight),
        **consts,
    )
    nc = build_program()
    in_maps = []
    for i in range(NCORES):
        m = dict(shared)
        m["x"] = np.ascontiguousarray(x[i * BPC:(i + 1) * BPC])
        in_maps.append(m)
    res = run_bass_kernel_spmd(nc, in_maps, core_ids=list(range(NCORES)))
    out = np.concatenate([r["out"] for r in res.results], axis=0)
    return out.astype(np.float32)


# version 20
# speedup vs baseline: 1.1270x; 1.0577x over previous
import contextlib
import os as _os
import numpy as np
import ml_dtypes
import concourse.bass as bass
import concourse.mybir as mybir
from concourse.bass_utils import run_bass_kernel_spmd

F32 = mybir.dt.float32
BF16 = mybir.dt.bfloat16
AF = mybir.ActivationFunctionType
ALU = mybir.AluOpType
AX = mybir.AxisListType
BF = ml_dtypes.bfloat16

NCORES = 8
BPC = 2
L = 4096
D = 1024
INNER = 2048
HEADS = 8
STATE = 64
HD = 256
KCONV = 4
EPS = 1e-5

CH_A = 256
CH_B = 512
SUB = 128
KD = D // 128
KI = INNER // 128
PO = STATE * HEADS
PK = 2 * PO + HEADS

NT_A = CH_A // 128
NCH_A = L // CH_A
NT_B = CH_B // 128
NSUB = CH_B // SUB
NCH_B = L // CH_B


class Prog:
    ENGS = ("pe", "dve", "act", "pool", "sp")
    NSLOT = 8

    def __init__(self):
        self.ops = {e: [] for e in self.ENGS}
        self.cnt = {e: 0 for e in self.ENGS}
        self.dma_slot_cnt = {}
        self.dma_n = {"sp": 0, "pool": 0}
        self.hw = {}
        self.last_w = {}
        self.readers = {}

    def op(self, eng, reads=(), writes=(), emit=None, dma=False):
        needs = {}

        def need(src, v):
            if v > needs.get(src, -1):
                needs[src] = v

        reads = list(reads)
        writes = list(writes)
        excl = [k for k in reads if k.startswith("pb") or k.startswith("pt")]
        for k in excl:
            reads.remove(k)
            writes.append(k)
        for k in reads:
            for src, v in self.last_w.get(k, {}).items():
                need(src, v)
        for k in writes:
            if not dma:
                for src, v in self.last_w.get(k, {}).items():
                    need(src, v)
            else:
                for src, v in self.last_w.get(k, {}).items():
                    if src in self.ENGS:
                        need(src, v)
            for r_src, r_cnt in self.readers.get(k, {}).items():
                need(r_src, r_cnt)

        if dma:
            slot = self.dma_n[eng] % self.NSLOT
            self.dma_n[eng] += 1
            srckey = f"{eng}{slot}"
            prev = self.dma_slot_cnt.get(srckey, 0)
            if prev > 0:
                need(srckey, prev)
            self.dma_slot_cnt[srckey] = prev + 16
            after_src = srckey
            after_val = prev + 16
        else:
            srckey = eng
            self.cnt[eng] += 1
            after_src = eng
            after_val = self.cnt[eng]
            if after_val > 255 and self.hw.get((eng, eng), 0) < after_val - 255:
                need(eng, after_val - 200)

        waits = []
        for src, v in sorted(needs.items()):
            cur = self.hw.get((eng, src), 0)
            if v > cur:
                waits.append((src, v))
                self.hw[(eng, src)] = v
        self.ops[eng].append((waits, emit, after_src if dma else None))
        for k in reads:
            self.readers.setdefault(k, {})[after_src] = after_val
        for k in writes:
            if dma:
                d = self.last_w.setdefault(k, {})
                for src in [s for s in d if s in self.ENGS]:
                    del d[src]
                d[after_src] = max(d.get(after_src, 0), after_val)
            else:
                self.last_w[k] = {after_src: after_val}
                self.readers[k] = {}
        return after_val

    def fence(self, keys):
        self.op("dve", reads=list(keys), writes=list(keys),
                emit=lambda e: e.engine_nop())


def build_program():
    _PH = int(_os.environ.get("KBISECT", "5"))
    _SB = int(_os.environ.get("KBISECT2", "9"))
    _SM = int(_os.environ.get("KBISECT3", "9"))
    nc = bass.Bass("TRN2", target_bir_lowering=False)

    dx = nc.dram_tensor("x", [BPC, L, D], F32, kind="ExternalInput").ap()
    dWx = nc.dram_tensor("Wx", [INNER, D], F32, kind="ExternalInput").ap()
    dWz = nc.dram_tensor("Wz", [INNER, D], F32, kind="ExternalInput").ap()
    dcw = nc.dram_tensor("conv_w", [128, KI * KCONV], F32, kind="ExternalInput").ap()
    dcb = nc.dram_tensor("conv_b", [128, KI], F32, kind="ExternalInput").ap()
    dWb = nc.dram_tensor("Wb", [PO, INNER], F32, kind="ExternalInput").ap()
    dWc = nc.dram_tensor("Wc", [PO, INNER], F32, kind="ExternalInput").ap()
    dWdt = nc.dram_tensor("Wdt", [HEADS, INNER], F32, kind="ExternalInput").ap()
    dbdt = nc.dram_tensor("b_dt", [HEADS, 1], F32, kind="ExternalInput").ap()
    dloga = nc.dram_tensor("log_a", [HEADS, 1], F32, kind="ExternalInput").ap()
    ddpar = nc.dram_tensor("d_param", [HEADS, 1], F32, kind="ExternalInput").ap()
    dgnw = nc.dram_tensor("gn_w", [128, KI], F32, kind="ExternalInput").ap()
    dgnb = nc.dram_tensor("gn_b", [128, KI], F32, kind="ExternalInput").ap()
    dWout = nc.dram_tensor("Wout", [D, INNER], F32, kind="ExternalInput").ap()
    dWf = nc.dram_tensor("Wf", [D, D], F32, kind="ExternalInput").ap()
    dident = nc.dram_tensor("ident", [128, 128], BF16, kind="ExternalInput").ap()
    didentf = nc.dram_tensor("identf", [8, 8], F32, kind="ExternalInput").ap()
    dnegr = nc.dram_tensor("negr", [128, 512], BF16, kind="ExternalInput").ap()
    dones = nc.dram_tensor("ones_c", [1, 128], F32, kind="ExternalInput").ap()
    donesb = nc.dram_tensor("ones_b", [128, 1], F32, kind="ExternalInput").ap()
    dseg = nc.dram_tensor("seg", [HEADS, CH_B], F32, kind="ExternalInput").ap()
    dsel = nc.dram_tensor("sel", [HEADS, 4 * 128], BF16, kind="ExternalInput").ap()
    dselh = nc.dram_tensor("selh", [HEADS, HEADS * 128], F32, kind="ExternalInput").ap()

    dout = nc.dram_tensor("out", [BPC, L, D], F32, kind="ExternalOutput").ap()
    sp_x1 = nc.dram_tensor("sp_x1", [BPC * L, D], BF16, kind="Internal").ap()
    sp_xc = nc.dram_tensor("sp_xc", [BPC, INNER, L], BF16, kind="Internal").ap()
    sp_z = nc.dram_tensor("sp_z", [BPC, INNER, L], BF16, kind="Internal").ap()
    sp_y = nc.dram_tensor("sp_y", [BPC, INNER, L], BF16, kind="Internal").ap()

    p = Prog()
    ctx = contextlib.ExitStack()
    with ctx:
        ST = lambda name, shape, dt_: ctx.enter_context(nc.sbuf_tensor(name, shape, dt_))

        wS = ST("wS", [128, KD * D], BF16)
        slotA = ST("slotA", [128, KI * PK], BF16)
        slotB = ST("slotB", [128, KD * INNER], BF16)
        G1 = ST("G1", [128, 8192], BF16)
        G2 = ST("G2", [128, 8192], BF16)
        G3 = ST("G3", [128, 8192], BF16)
        G4 = ST("G4", [128, 8192], BF16)
        G5 = ST("G5", [128, 6144], BF16)
        xp = ST("xp", [128, 515], BF16)
        xpB = ST("xpB", [128, 515], BF16)
        halo = ST("halo", [128, KI * 3], BF16)
        acc_a = ST("acc_a", [128, CH_B], F32)
        acc_b = ST("acc_b", [128, CH_B], F32)
        s_sb = ST("s_sb", [8, CH_B], F32)
        nslf = ST("nslf", [8, CH_B], F32)
        dtv = ST("dtv", [8, CH_B], F32)
        scbf = ST("scbf", [8, CH_B], F32)
        esr = ST("esr", [8, CH_B], BF16)
        tin = ST("tin", [8, 2 * CH_B], F32)
        colT = ST("colT", [128, NSUB * 16], F32)
        lamc = ST("lamc", [128, NSUB * 4], F32)
        S_a = ST("S_a", [128, 4 * HD], F32)
        S_b = ST("S_b", [128, 4 * HD], F32)
        S_bf = ST("S_bf", [128, 4 * HD], BF16)
        ys = ST("ys", [128, NCH_B * NSUB * 8], F32)
        yss = ST("yss", [128, NCH_B * NSUB * 8], F32)
        c1 = ST("c1", [128, 2 * NT_A], F32)
        c2 = ST("c2", [128, 4 * NT_A], F32)
        cm = ST("cm", [128, 2 * NT_A], F32)
        cv = ST("cv", [128, 2 * NT_A], F32)
        crv = ST("crv", [128, 2 * NT_A], F32)
        cnm = ST("cnm", [128, 2 * NT_A], F32)
        dumpA = ST("dumpA", [128, 512], BF16)
        esb = ST("esbuf", [128, CH_B], BF16)
        acc_s1 = ST("acc_s1", [128, CH_B], BF16)
        gnwp = ST("gnwp", [128, BPC * KI], F32)
        gnbp = ST("gnbp", [128, BPC * KI], F32)
        stat8 = ST("stat8", [1, 56], F32)
        rinv8 = ST("rinv8", [128, BPC * 16], F32)
        wstg = ST("wstg", [128, 2048], BF16)
        ident = ST("identc", [128, 128], BF16)
        identf = ST("identfc", [8, 8], F32)
        negr = ST("negrc", [128, 512], BF16)
        onesc = ST("onescc", [1, 128], F32)
        onesb = ST("onesbc", [128, 1], F32)
        segc = ST("segcc", [8, CH_B], F32)
        selc = ST("selcc", [8, 4 * 128], BF16)
        selh = ST("selhc", [8, 8 * 128], F32)
        cwc = ST("cwcc", [128, KI * KCONV], F32)
        cbc = ST("cbcc", [128, KI], F32)
        bdtc = ST("bdtcc", [8, 1], F32)
        acol = ST("acolc", [8, 1], F32)
        dcol = ST("dcolc", [128, 8], F32)
        dic = ST("dicc", [128, 8 * 128], BF16)
        gnwc = ST("gnwcc", [128, KI], F32)
        gnbc = ST("gnbcc", [128, KI], F32)
        cst = ST("cstc", [128, 2], F32)

        psum = [ctx.enter_context(nc.psum_tensor(f"pb{i}", [128, 512], F32))
                for i in range(6)]
        ptr = [ctx.enter_context(nc.psum_tensor(f"pt{i}", [128, 1024], BF16))
               for i in range(2)]
        PS = [f"pb{i}" for i in range(6)]
        PT = [f"pt{i}" for i in range(2)]

        sems = {e: ctx.enter_context(nc.semaphore(f"sm_{e}")) for e in Prog.ENGS}
        for q in ("sp", "pool"):
            for s in range(Prog.NSLOT):
                sems[f"{q}{s}"] = ctx.enter_context(nc.semaphore(f"sm_{q}{s}"))
        block = ctx.enter_context(nc.Block())

        def dma(queue, out, in_, reads, writes):
            p.op(queue, reads=reads, writes=writes,
                 emit=lambda e, o=out, i=in_: e.dma_start(out=o, in_=i), dma=True)

        bfv = lambda t, c0, c1_: t[:, c0:c1_]
        f32v = lambda t, c0, c1_: t[:].bitcast(F32)[:, c0:c1_]
        v_x0p = [bfv(G5, 0, 2048), bfv(G4, 0, 2048)]
        v_x1p = [bfv(G5, 2048, 4096), bfv(G4, 2048, 4096)]
        v_utmp = [bfv(G1, 0, 2048), bfv(G1, 2048, 4096)]
        v_ufmp = [bfv(G1, 4096, 6144), bfv(G1, 6144, 8192),
                  bfv(G3, 4096, 6144), bfv(G3, 6144, 8192)]
        u2_src = [(G1, 4096), (G3, 4096)]
        v_u1fp = [bfv(G3, 0, 2048), bfv(G3, 2048, 4096)]
        v_u2 = bfv(G1, 0, KD * CH_B)
        v_xcb1 = bfv(G2, 0, 2048)
        v_zb1 = bfv(G2, 2048, 4096)
        v_xcr = bfv(G1, 0, 8192)
        v_xcs = bfv(G2, 0, 8192)
        v_yt = bfv(G3, 0, 8192)
        v_xtmA = bfv(G4, 0, 2048)
        v_xtmB = bfv(G4, 2048, 4096)
        v_bfm = bfv(G4, 4096, 6144)
        v_ctil = bfv(G4, 6144, 8192)
        v_btmA = bfv(G5, 0, 1024)
        v_btmB = bfv(G5, 1024, 2048)
        v_mg = bfv(G5, 2048, 3072)
        v_eb = bfv(G5, 3072, 4096)
        v_cth = bfv(G5, 4096, 6144)
        v_yrp = [bfv(G1, 0, 4096), bfv(G1, 4096, 8192)]
        v_zrp = [bfv(G2, 0, 4096), bfv(G2, 4096, 8192)]
        v_gyp = [bfv(G3, 0, 4096), bfv(G3, 4096, 8192)]
        v_x1r = bfv(G4, 0, 2048)
        v_x2p = [bfv(G4, 2048, 4096), bfv(G5, 0, 2048)]
        v_utm2 = bfv(G4, 4096, 6144)
        v_ufm2p = [bfv(G4, 6144, 8192), bfv(G5, 2048, 4096)]
        v_x3 = f32v(G5, 2048, 3072)

        ALL_POOL_KEYS = ["x0_0", "x0_1", "x1b_0", "x1b_1", "utm_0", "utm_1",
                         "ufm_0", "ufm_1", "ufm_2", "ufm_3", "u1f_0", "u1f_1", "x2_0", "x2_1", "ufm2_0", "ufm2_1",
                         "u2", "xcb1", "zb1",
                         "xcr", "xcs", "xcs0", "xcs1", "yt", "xtmA", "xtmB", "bfm", "ctil",
                         "btmA", "btmB", "mg", "eb", "cth", "yr", "zr", "gy",
                         "x1r", "utm2", "x3", "yr_0", "yr_1", "zr_0", "zr_1", "gy_0", "gy_1"]

        p.op("dve", [], ["cst"], emit=lambda e: e.memset(cst[:, 0:1], EPS))
        p.op("dve", [], ["cst"], emit=lambda e: e.memset(cst[:, 1:2], -0.5))
        dma("sp", ident[:], dident, [], ["ident"])
        dma("sp", identf[:], didentf, [], ["identf"])
        dma("sp", negr[:], dnegr, [], ["negr"])
        dma("sp", onesc[:], dones, [], ["ones"])
        dma("sp", onesb[:], donesb, [], ["onesb"])
        dma("sp", segc[:], dseg, [], ["seg"])
        dma("sp", selc[:], dsel, [], ["sel"])
        dma("sp", selh[:], dselh, [], ["selh"])
        dma("sp", cwc[:], dcw, [], ["cwc"])
        dma("sp", cbc[:], dcb, [], ["cbc"])
        dma("sp", bdtc[:], dbdt, [], ["bdtc"])
        dma("sp", gnwc[:], dgnw, [], ["gnwc"])
        dma("sp", gnbc[:], dgnb, [], ["gnbc"])
        dma("sp", acol[:], dloga, [], ["acol"])
        dma("sp", dcol[0:8, 0:1], ddpar, [], ["dcol"])

        p.op("act", ["acol"], ["acol"],
             emit=lambda e: e.activation(out=acol[:], in_=acol[:], func=AF.Exp))
        p.op("dve", ["acol"], ["acol"],
             emit=lambda e: e.tensor_scalar(out=acol[:], in0=acol[:], scalar1=-1.0,
                                            scalar2=None, op0=ALU.mult))
        p.op("pe", ["dcol", "identf"], [PS[0]],
             emit=lambda e: e.transpose(psum[0][0:1, 0:8], dcol[0:8, 0:1],
                                        identf[0:8, 0:8]))
        p.op("dve", [PS[0]], ["dcol"],
             emit=lambda e: e.tensor_copy(out=dcol[0:1, 0:8], in_=psum[0][0:1, 0:8]))
        p.op("pe", ["dcol", "ones"], [PS[1]],
             emit=lambda e: e.matmul(psum[1][:, 0:8], onesc[0:1, 0:128],
                                     dcol[0:1, 0:8], start=True, stop=True))
        p.op("dve", [PS[1]], ["dcol"],
             emit=lambda e: e.tensor_copy(out=dcol[:, 0:8], in_=psum[1][:, 0:8]))
        for h in range(8):
            p.op("dve", ["dcol", "ident"], ["dic"],
                 emit=lambda e, h=h: e.tensor_scalar(
                     out=dic[:, h * 128:(h + 1) * 128], in0=ident[:],
                     scalar1=dcol[:, h:h + 1], scalar2=None, op0=ALU.mult))

        def prep_w(dram, rows, cols, dst, dst_key, col_of, sign=False):
            n_ot = rows // 128
            n_kt = cols // 128
            for ot in range(n_ot):
                dma("pool", wstg[:, 0:cols], dram[ot * 128:(ot + 1) * 128, :],
                    [], ["wstg"])
                if sign:
                    p.op("act", ["wstg"], ["wstg"],
                         emit=lambda e, cols=cols: e.sign(out=wstg[:, 0:cols],
                                                          in_=wstg[:, 0:cols]))
                for g in range(0, n_kt, 8):
                    hi = min(g + 8, n_kt)
                    bi = (ot * 2 + g // 8) % 2
                    bank, bkey = ptr[bi], PT[bi]
                    for kt in range(g, hi):
                        p.op("pe", ["wstg", "ident"], [bkey],
                             emit=lambda e, kt=kt, g=g, bank=bank: e.transpose(
                                 bank[:, (kt - g) * 128:(kt - g + 1) * 128],
                                 wstg[:, kt * 128:(kt + 1) * 128], ident[:]))
                    for kt in range(g, hi):
                        col = col_of(kt, ot)
                        p.op("dve", [bkey], [dst_key],
                             emit=lambda e, kt=kt, g=g, col=col, bank=bank:
                             e.tensor_copy(out=dst[:, col:col + 128],
                                           in_=bank[:, (kt - g) * 128:(kt - g + 1) * 128]))

        prep_w(dWf, D, D, wS, "wS", lambda kt, ot: kt * D + ot * 128, sign=True)
        prep_w(dWx, INNER, D, slotA, "slotA", lambda kt, ot: kt * INNER + ot * 128)
        prep_w(dWz, INNER, D, slotB, "slotB", lambda kt, ot: kt * INNER + ot * 128)

        def layernorm(src, src_key, dst, dst_key, par=0):
            o = par * NT_A
            o2 = par * 2 * NT_A
            kc = [f"c1_{par}", f"c2_{par}", f"cm_{par}", f"cv_{par}",
                  f"cnm_{par}", f"crv_{par}"]
            for tt in range(NT_A):
                sl = src[:, tt * D:(tt + 1) * D]
                p.op("dve", [src_key], [kc[0]],
                     emit=lambda e, sl=sl, tt=tt: e.reduce_sum(
                         out=c1[:, o + tt:o + tt + 1], in_=sl, axis=AX.X))
                for hf in range(2):
                    p.op("act", [src_key], [kc[1], "dumpA"],
                         emit=lambda e, src=src, tt=tt, hf=hf: e.activation(
                             out=dumpA[:], in_=src[:, tt * D + hf * 512:
                                                   tt * D + (hf + 1) * 512],
                             func=AF.Square,
                             accum_out=c2[:, o2 + hf * NT_A + tt:
                                          o2 + hf * NT_A + tt + 1]))
            p.op("dve", [kc[0]], [kc[2]],
                 emit=lambda e: e.tensor_scalar(out=cm[:, o:o + NT_A],
                                                in0=c1[:, o:o + NT_A],
                                                scalar1=1.0 / D, scalar2=None,
                                                op0=ALU.mult))
            p.op("dve", [kc[1]], [kc[3]],
                 emit=lambda e: e.tensor_add(out=cv[:, o:o + NT_A],
                                             in0=c2[:, o2:o2 + NT_A],
                                             in1=c2[:, o2 + NT_A:o2 + 2 * NT_A]))
            p.op("dve", [kc[3]], [kc[3]],
                 emit=lambda e: e.tensor_scalar(out=cv[:, o:o + NT_A],
                                                in0=cv[:, o:o + NT_A],
                                                scalar1=1.0 / D, scalar2=None,
                                                op0=ALU.mult))
            p.op("dve", [kc[2]], [kc[4]],
                 emit=lambda e: e.tensor_mul(out=cnm[:, o:o + NT_A],
                                             in0=cm[:, o:o + NT_A],
                                             in1=cm[:, o:o + NT_A]))
            p.op("dve", [kc[3], kc[4]], [kc[3]],
                 emit=lambda e: e.tensor_sub(out=cv[:, o:o + NT_A],
                                             in0=cv[:, o:o + NT_A],
                                             in1=cnm[:, o:o + NT_A]))
            p.op("act", [kc[3], "cst"], [kc[3]],
                 emit=lambda e: e.activation(out=cv[:, o:o + NT_A],
                                             in_=cv[:, o:o + NT_A], func=AF.Ln,
                                             bias=cst[:, 0:1]))
            p.op("act", [kc[3], "cst"], [kc[5]],
                 emit=lambda e: e.activation(out=crv[:, o:o + NT_A],
                                             in_=cv[:, o:o + NT_A], func=AF.Exp,
                                             scale=cst[:, 1:2]))
            p.op("dve", [kc[2], kc[5]], [kc[4]],
                 emit=lambda e: e.tensor_mul(out=cnm[:, o:o + NT_A],
                                             in0=cm[:, o:o + NT_A],
                                             in1=crv[:, o:o + NT_A]))
            p.op("dve", [kc[4]], [kc[4]],
                 emit=lambda e: e.tensor_scalar(out=cnm[:, o:o + NT_A],
                                                in0=cnm[:, o:o + NT_A],
                                                scalar1=-1.0, scalar2=None,
                                                op0=ALU.mult))
            for tt in range(NT_A):
                p.op("act", [src_key, kc[5], kc[4]], [dst_key],
                     emit=lambda e, tt=tt: e.activation(
                         out=dst[:, tt * D:(tt + 1) * D],
                         in_=src[:, tt * D:(tt + 1) * D], func=AF.Identity,
                         bias=cnm[:, o + tt:o + tt + 1],
                         scale=crv[:, o + tt:o + tt + 1]))

        def transpose_u(src, src_key, dst, dst_key, chunk):
            nt = chunk // 128
            for tt in range(nt):
                bi = tt % 2
                bank, bkey = ptr[bi], PT[bi]
                for kt in range(KD):
                    p.op("pe", [src_key, "ident"], [bkey],
                         emit=lambda e, tt=tt, kt=kt, bank=bank: e.transpose(
                             bank[:, kt * 128:(kt + 1) * 128],
                             src[:, tt * D + kt * 128:tt * D + (kt + 1) * 128],
                             ident[:]))
                for kt in range(KD):
                    p.op("dve", [bkey], [dst_key],
                         emit=lambda e, tt=tt, kt=kt, bank=bank, chunk=chunk:
                         e.tensor_copy(
                             out=dst[:, kt * chunk + tt * 128:kt * chunk + (tt + 1) * 128],
                             in_=bank[:, kt * 128:(kt + 1) * 128]))

        def p1a1_ln1(b, c):
            T0 = c * CH_A
            par = c % 2
            v_x0, v_utm = v_x0p[par], v_utmp[par]
            kx0, kut = f"x0_{par}", f"utm_{par}"
            x_view = bass.AP(dx.tensor, (b * L + T0) * D,
                             [[D, 128], [128 * D, NT_A], [1, D]])
            dma("pool", v_x0.rearrange("p (a b) -> p a b", a=NT_A), x_view,
                [], [kx0])
            layernorm(v_x0, kx0, v_utm, kut, par)

        def p1a1_t1f1(b, c):
            T0 = c * CH_A
            par = c % 2
            v_x0, v_x1 = v_x0p[par], v_x1p[par]
            v_utm, v_u1f = v_utmp[par], v_u1fp[par]
            kx0, kx1 = f"x0_{par}", f"x1b_{par}"
            kut, ku1 = f"utm_{par}", f"u1f_{par}"
            transpose_u(v_utm, kut, v_u1f, ku1, CH_A)
            for tt in range(NT_A):
                for nh in range(2):
                    bi = 4 + (tt * 2 + nh) % 2
                    bank, bkey = psum[bi], PS[bi]

                    def f1(e, tt=tt, nh=nh, bank=bank, v_u1f=v_u1f):
                        last = None
                        for k in range(KD):
                            last = e.matmul(
                                bank[:],
                                v_u1f[:, k * CH_A + tt * 128:k * CH_A + (tt + 1) * 128],
                                wS[:, k * D + nh * 512:k * D + (nh + 1) * 512],
                                start=(k == 0), stop=(k == KD - 1))
                        return last
                    p.op("pe", [ku1, "wS"], [bkey], emit=f1)
                    p.op("dve", [bkey, kx0], [kx1],
                         emit=lambda e, tt=tt, nh=nh, bank=bank, v_x1=v_x1,
                         v_x0=v_x0:
                         e.scalar_tensor_tensor(
                             out=v_x1[:, tt * D + nh * 512:tt * D + (nh + 1) * 512],
                             in0=bank[:], scalar=0.5,
                             in1=v_x0[:, tt * D + nh * 512:tt * D + (nh + 1) * 512],
                             op0=ALU.mult, op1=ALU.add))
            x1_view = bass.AP(sp_x1.tensor, (b * L + T0) * D,
                              [[D, 128], [128 * D, NT_A], [1, D]])
            dma("sp", x1_view, v_x1.rearrange("p (a b) -> p a b", a=NT_A),
                [kx1], ["sp_x1"])

        def p1a1_ln2(b, c):
            par = c % 2
            layernorm(v_x1p[par], f"x1b_{par}", v_utmp[par], f"utm_{par}", par)

        def p1a1_t2(b, c):
            par = c % 2
            transpose_u(v_utmp[par], f"utm_{par}", v_ufmp[c % 4],
                        f"ufm_{c % 4}", CH_A)

        def u2_rhs(k, pp):
            t, off = u2_src[pp]
            return bass.AP(t, off + k * CH_A,
                           [[8192, 128], [2048, 2], [1, CH_A]])

        for b in range(BPC if _PH >= 2 else 0):
            p.op("dve", [], ["halo"], emit=lambda e: e.memset(halo[:], 0.0))

            def wxwz(cp, ot_lo, ot_hi, b=b):
                T0 = cp * CH_B
                pp = cp % 2
                ku2a, ku2b = f"ufm_{2 * pp}", f"ufm_{2 * pp + 1}"
                for ot in range(ot_lo, ot_hi):
                    bi = ot % 4
                    bank, bkey = psum[bi], PS[bi]

                    def wxmm(e, ot=ot, bank=bank, pp=pp):
                        last = None
                        for k in range(KD):
                            last = e.matmul(
                                bank[:],
                                slotA[:, k * INNER + ot * 128:k * INNER + (ot + 1) * 128],
                                u2_rhs(k, pp), start=(k == 0), stop=(k == KD - 1))
                        return last
                    p.op("pe", [ku2a, ku2b, "slotA"], [bkey], emit=wxmm)
                    xpc = xp if ot % 2 == 0 else xpB
                    kxp = "xp" if ot % 2 == 0 else "xpB"
                    p.op("act", [bkey], [kxp],
                         emit=lambda e, bank=bank, xpc=xpc: e.activation(
                             out=xpc[:, 3:3 + CH_B], in_=bank[:], func=AF.Copy))
                    p.op("dve", ["halo", kxp], [kxp],
                         emit=lambda e, ot=ot, xpc=xpc: e.tensor_copy(
                             out=xpc[:, 0:3], in_=halo[:, ot * 3:ot * 3 + 3]))
                    p.op("dve", [kxp], ["halo"],
                         emit=lambda e, ot=ot, xpc=xpc: e.tensor_copy(
                             out=halo[:, ot * 3:ot * 3 + 3],
                             in_=xpc[:, CH_B:CH_B + 3]))
                    acc0 = acc_a if ot % 2 == 0 else acc_b
                    ka0 = "acc_a" if ot % 2 == 0 else "acc_b"
                    p.op("dve", [kxp, "cwc", "cbc"], [ka0],
                         emit=lambda e, ot=ot, xpc=xpc, acc0=acc0: e.tensor_scalar(
                             out=acc0[:], in0=xpc[:, 0:CH_B],
                             scalar1=cwc[:, ot * 4:ot * 4 + 1],
                             scalar2=cbc[:, ot:ot + 1], op0=ALU.mult, op1=ALU.add))
                    p.op("dve", [kxp, "cwc", ka0], [ka0],
                         emit=lambda e, ot=ot, xpc=xpc, acc0=acc0:
                         e.scalar_tensor_tensor(
                             out=acc0[:], in0=xpc[:, 1:1 + CH_B],
                             scalar=cwc[:, ot * 4 + 1:ot * 4 + 2], in1=acc0[:],
                             op0=ALU.mult, op1=ALU.add))
                    p.op("dve", [kxp, "cwc", ka0], [ka0],
                         emit=lambda e, ot=ot, xpc=xpc, acc0=acc0:
                         e.scalar_tensor_tensor(
                             out=acc0[:], in0=xpc[:, 2:2 + CH_B],
                             scalar=cwc[:, ot * 4 + 2:ot * 4 + 3], in1=acc0[:],
                             op0=ALU.mult, op1=ALU.add))
                    p.op("dve", [kxp, "cwc", ka0], [ka0],
                         emit=lambda e, ot=ot, xpc=xpc, acc0=acc0:
                         e.scalar_tensor_tensor(
                             out=acc0[:], in0=xpc[:, 3:3 + CH_B],
                             scalar=cwc[:, ot * 4 + 3:ot * 4 + 4], in1=acc0[:],
                             op0=ALU.mult, op1=ALU.add))
                    p.op("act", [ka0], ["xcb1"],
                         emit=lambda e, ot=ot, acc0=acc0: e.activation(
                             out=v_xcb1[:, (ot % 4) * CH_B:(ot % 4 + 1) * CH_B],
                             in_=acc0[:], func=AF.Silu))

                    def wzmm(e, ot=ot, bank=bank, pp=pp):
                        last = None
                        for k in range(KD):
                            last = e.matmul(
                                bank[:],
                                slotB[:, k * INNER + ot * 128:k * INNER + (ot + 1) * 128],
                                u2_rhs(k, pp), start=(k == 0), stop=(k == KD - 1))
                        return last
                    p.op("pe", [ku2a, ku2b, "slotB"], [bkey], emit=wzmm)
                    p.op("act", [bkey], ["zb1"],
                         emit=lambda e, ot=ot, bank=bank: e.activation(
                             out=v_zb1[:, (ot % 4) * CH_B:(ot % 4 + 1) * CH_B],
                             in_=bank[:], func=AF.Copy))
                    if ot % 4 == 3:
                        q = ot // 4
                        xc_view = bass.AP(sp_xc.tensor,
                                          b * INNER * L + q * 4 * 128 * L + T0,
                                          [[L, 128], [128 * L, 4], [1, CH_B]])
                        dma("sp", xc_view,
                            bass.AP(v_xcb1.tensor, 0,
                                    [[8192, 128], [CH_B, 4], [1, CH_B]]),
                            ["xcb1"], ["sp_xc"])
                        z_view = bass.AP(sp_z.tensor,
                                         b * INNER * L + q * 4 * 128 * L + T0,
                                         [[L, 128], [128 * L, 4], [1, CH_B]])
                        dma("pool", z_view,
                            bass.AP(v_zb1.tensor, 2048,
                                    [[8192, 128], [CH_B, 4], [1, CH_B]]),
                            ["zb1"], ["sp_z"])

            p1a1_ln1(b, 0)
            p1a1_ln1(b, 1)
            p1a1_t1f1(b, 0)
            p1a1_t1f1(b, 1)
            p1a1_ln2(b, 0)
            p1a1_ln2(b, 1)
            p1a1_t2(b, 0)
            p1a1_t2(b, 1)
            if NCH_B > 1:
                p1a1_ln1(b, 2)
                p1a1_ln1(b, 3)
            for cp in range(NCH_B):
                if cp + 1 < NCH_B:
                    p1a1_t1f1(b, 2 * cp + 2)
                    p1a1_t1f1(b, 2 * cp + 3)
                    p1a1_ln2(b, 2 * cp + 2)
                    p1a1_ln2(b, 2 * cp + 3)
                    p1a1_t2(b, 2 * cp + 2)
                    if cp + 2 < NCH_B:
                        p1a1_ln1(b, 2 * cp + 4)
                    if _PH >= 3:
                        wxwz(cp, 0, 2)
                    p1a1_t2(b, 2 * cp + 3)
                    if cp + 2 < NCH_B:
                        p1a1_ln1(b, 2 * cp + 5)
                    if _PH >= 3:
                        wxwz(cp, 2, KI)
                elif _PH >= 3:
                    wxwz(cp, 0, KI)

        prep_w(dWb, PO, INNER, slotA, "slotA", lambda kt, ot: kt * PK + ot * 128)
        prep_w(dWc, PO, INNER, slotA, "slotA",
               lambda kt, ot: kt * PK + PO + ot * 128)
        dma("pool", wstg[0:8, 0:INNER], dWdt, [], ["wstg"])
        for kt in range(KI):
            bi = kt % 2
            bank, bkey = ptr[bi], PT[bi]
            p.op("pe", ["wstg", "ident"], [bkey],
                 emit=lambda e, kt=kt, bank=bank: e.transpose(
                     bank[:, 0:8], wstg[0:8, kt * 128:(kt + 1) * 128],
                     ident[0:8, 0:8]))
            p.op("dve", [bkey], ["slotA"],
                 emit=lambda e, kt=kt, bank=bank: e.tensor_copy(
                     out=slotA[:, kt * PK + 2 * PO:kt * PK + 2 * PO + 8],
                     in_=bank[:, 0:8]))

        p.fence(ALL_POOL_KEYS)
        p.op("dve", [], ["ctil"], emit=lambda e: e.memset(v_ctil, 0.0))
        p.op("dve", [], ["cth"], emit=lambda e: e.memset(v_cth, 0.0))
        p.op("dve", [], ["btmA"], emit=lambda e: e.memset(v_btmA, 0.0))
        p.op("dve", [], ["btmB"], emit=lambda e: e.memset(v_btmB, 0.0))
        for b in range(BPC if _PH >= 4 else 0):
            p.op("dve", [], ["S_a"], emit=lambda e: e.memset(S_a[:], 0.0))
            p.op("dve", [], ["S_bf"], emit=lambda e: e.memset(S_bf[:], 0.0))
            S_cur, S_nxt = S_a, S_b
            S_keys = ["S_a", "S_b"]
            for c in range(NCH_B):
                T0 = c * CH_B
                xcs_t = [G2, G1][c % 2]
                v_xcs_c = xcs_t[:, 0:8192]
                kxcs = f"xcs{c % 2}"
                for q in range(4):
                    xc_view = bass.AP(sp_xc.tensor,
                                      b * INNER * L + q * 4 * 128 * L + T0,
                                      [[L, 128], [128 * L, 4], [1, CH_B]])
                    dma("sp",
                        bass.AP(xcs_t, q * 4 * CH_B,
                                [[8192, 128], [CH_B, 4], [1, CH_B]]),
                        xc_view, ["sp_xc"], [kxcs])
                if _SB < 2:
                    continue
                def dtmm(e, v_xcs_c=v_xcs_c):
                    last = None
                    for k in range(KI):
                        last = e.matmul(
                            psum[4][0:8, :],
                            slotA[:, k * PK + 2 * PO:k * PK + 2 * PO + 8],
                            v_xcs_c[:, k * CH_B:(k + 1) * CH_B],
                            start=(k == 0), stop=(k == KI - 1))
                    return last
                p.op("pe", [kxcs, "slotA"], [PS[4]], emit=dtmm)
                if _SB < 3:
                    continue
                pr = tin[0:8, 0:CH_B]
                t1 = dtv[0:8, :]
                t2 = tin[0:8, CH_B:2 * CH_B]
                p.op("dve", [PS[4], "bdtc"], ["tin"],
                     emit=lambda e: e.tensor_scalar(out=pr, in0=psum[4][0:8, :],
                                                    scalar1=bdtc[:], scalar2=None,
                                                    op0=ALU.add))
                p.op("dve", ["tin"], ["dtv"],
                     emit=lambda e: e.tensor_scalar(out=t1, in0=pr, scalar1=-1.0,
                                                    scalar2=None, op0=ALU.mult))
                p.op("dve", ["tin", "dtv"], ["tin"],
                     emit=lambda e: e.tensor_tensor(out=t2, in0=pr, in1=t1,
                                                    op=ALU.min))
                p.op("act", ["tin"], ["tin"],
                     emit=lambda e: e.activation(out=t2, in_=t2, func=AF.Exp))
                p.op("act", ["tin"], ["tin"],
                     emit=lambda e: e.activation(out=t2, in_=t2, func=AF.Ln,
                                                 bias=1.0))
                p.op("dve", ["tin"], ["dtv"],
                     emit=lambda e: e.tensor_scalar(out=t1, in0=pr, scalar1=0.0,
                                                    scalar2=None, op0=ALU.max))
                p.op("dve", ["tin", "dtv"], ["dtv"],
                     emit=lambda e: e.tensor_tensor(out=dtv[:], in0=t1, in1=t2,
                                                    op=ALU.add))
                p.op("dve", ["dtv", "acol"], ["scbf"],
                     emit=lambda e: e.tensor_scalar(out=scbf[:], in0=dtv[:],
                                                    scalar1=acol[:], scalar2=None,
                                                    op0=ALU.mult))
                p.op("dve", ["scbf", "seg"], ["s_sb"],
                     emit=lambda e: e.tensor_tensor_scan(
                         out=s_sb[0:8, :], data0=segc[:], data1=scbf[:], initial=0.0,
                         op0=ALU.mult, op1=ALU.add))
                p.op("act", ["dtv"], ["tin"],
                     emit=lambda e: e.activation(out=t2, in_=dtv[:], func=AF.Ln))
                p.op("dve", ["tin", "s_sb"], ["nslf"],
                     emit=lambda e: e.tensor_sub(out=nslf[:], in0=t2,
                                                 in1=s_sb[0:8, :]))
                p.op("act", ["s_sb"], ["esr"],
                     emit=lambda e: e.activation(out=esr[:], in_=s_sb[0:8, :],
                                                 func=AF.Exp))
                for su in range(NSUB):
                    p.op("act", ["s_sb", "nslf"], ["scbf"],
                         emit=lambda e, su=su: e.activation(
                             out=scbf[:, su * SUB:(su + 1) * SUB],
                             in_=nslf[:, su * SUB:(su + 1) * SUB], func=AF.Exp,
                             bias=s_sb[0:8, (su + 1) * SUB - 1:(su + 1) * SUB]))
                if _SB < 4:
                    continue
                for su in range(NSUB):
                    bi = su % 2
                    p.op("pe", ["nslf", "identf"], [PS[5]],
                         emit=lambda e, su=su: e.transpose(
                             psum[5][:, 0:8], nslf[:, su * SUB:(su + 1) * SUB],
                             identf[0:8, 0:8]))
                    p.op("dve", [PS[5]], ["colT"],
                         emit=lambda e, su=su: e.tensor_copy(
                             out=colT[:, su * 16:su * 16 + 8], in_=psum[5][:, 0:8]))
                    p.op("pe", ["scbf", "identf"], [PS[5]],
                         emit=lambda e, su=su: e.transpose(
                             psum[5][:, 0:8], scbf[:, su * SUB:(su + 1) * SUB],
                             identf[0:8, 0:8]))
                    p.op("dve", [PS[5]], ["colT"],
                         emit=lambda e, su=su: e.tensor_copy(
                             out=colT[:, su * 16 + 8:su * 16 + 16],
                             in_=psum[5][:, 0:8]))
                for su in range(NSUB):
                    def lam(e, su=su):
                        last = None
                        for g in range(4):
                            last = e.matmul(
                                psum[5][:, su * 4 + g:su * 4 + g + 1],
                                selc[:, g * 128:(g + 1) * 128],
                                esr[:, (su + 1) * SUB - 1:(su + 1) * SUB],
                                start=True, stop=True)
                        return last
                    p.op("pe", ["esr", "sel"], [PS[5]], emit=lam)
                    p.op("dve", [PS[5]], ["lamc"],
                         emit=lambda e, su=su: e.tensor_copy(
                             out=lamc[:, su * 4:su * 4 + 4],
                             in_=psum[5][:, su * 4:su * 4 + 4]))
                if _SB < 5:
                    continue
                for m in range(4):
                    bi = m % 4
                    bank, bkey = psum[bi], PS[bi]

                    def bmm(e, m=m, bank=bank, v_xcs_c=v_xcs_c):
                        last = None
                        for k in range(KI):
                            last = e.matmul(
                                bank[:],
                                slotA[:, k * PK + m * 128:k * PK + (m + 1) * 128],
                                v_xcs_c[:, k * CH_B:(k + 1) * CH_B],
                                start=(k == 0), stop=(k == KI - 1))
                        return last
                    p.op("pe", [kxcs, "slotA"], [bkey], emit=bmm)
                    p.op("dve", [bkey], ["bfm"],
                         emit=lambda e, m=m, bank=bank: e.tensor_copy(
                             out=v_bfm[:, m * CH_B:(m + 1) * CH_B], in_=bank[:]))
                for m in range(4):
                    bi = m % 4
                    bank, bkey = psum[bi], PS[bi]

                    def cmm(e, m=m, bank=bank, v_xcs_c=v_xcs_c):
                        last = None
                        for k in range(KI):
                            last = e.matmul(
                                bank[:],
                                slotA[:, k * PK + PO + m * 128:k * PK + PO + (m + 1) * 128],
                                v_xcs_c[:, k * CH_B:(k + 1) * CH_B],
                                start=(k == 0), stop=(k == KI - 1))
                        return last
                    p.op("pe", [kxcs, "slotA"], [bkey], emit=cmm)

                    def esmm(e, m=m):
                        return e.matmul(psum[4][:], selc[:, m * 128:(m + 1) * 128],
                                        esr[:], start=True, stop=True)
                    p.op("pe", ["esr", "sel"], [PS[4]], emit=esmm)
                    p.op("dve", [PS[4]], ["esbuf"],
                         emit=lambda e: e.tensor_copy(out=esb[:], in_=psum[4][:]))
                    p.op("dve", [bkey, "esbuf"], ["ctil"],
                         emit=lambda e, m=m, bank=bank: e.tensor_tensor(
                             out=v_ctil[0:64, m * CH_B:(m + 1) * CH_B],
                             in0=bank[0:64, :], in1=esb[0:64, :], op=ALU.mult))
                    p.op("dve", [bkey, "esbuf"], ["cth"],
                         emit=lambda e, m=m, bank=bank: e.tensor_tensor(
                             out=v_cth[64:128, m * CH_B:(m + 1) * CH_B],
                             in0=bank[64:128, :], in1=esb[64:128, :], op=ALU.mult))

                if _SB < 6:
                    continue
                for su in range(NSUB):
                    t0 = su * SUB
                    xtm = v_xtmA if su % 2 == 0 else v_xtmB
                    xkey = "xtmA" if su % 2 == 0 else "xtmB"
                    btm = v_btmA if su % 2 == 0 else v_btmB
                    btkey = "btmA" if su % 2 == 0 else "btmB"
                    xv = bass.AP(sp_xc.tensor, b * INNER * L + T0 + t0,
                                 [[L, INNER], [1, 128]])
                    p.op("sp", ["sp_xc"], [xkey],
                         emit=lambda e, xv=xv, xtm=xtm: e.dma_start(
                             out=xtm[:], in_=xv, transpose=True), dma=True)
                    for h in range(8):
                        bi = h % 2
                        bank, bkey = ptr[bi], PT[bi]
                        p.op("pe", ["bfm", "ident"], [bkey],
                             emit=lambda e, h=h, bank=bank, t0=t0: e.transpose(
                                 bank[:, 0:64],
                                 v_bfm[(h % 2) * 64:(h % 2) * 64 + 64,
                                       (h // 2) * CH_B + t0:(h // 2) * CH_B + t0 + 128],
                                 ident[(h % 2) * 64:(h % 2) * 64 + 64,
                                       (h % 2) * 64:(h % 2) * 64 + 64]))
                        bcol = (h // 2) * 256 + (h % 2) * 192
                        p.op("dve", [bkey, "colT"], [btkey],
                             emit=lambda e, h=h, su=su, bank=bank, btm=btm, bcol=bcol:
                             e.tensor_scalar(
                                 out=btm[:, bcol:bcol + 64],
                                 in0=bank[:, 0:64],
                                 scalar1=colT[:, su * 16 + 8 + h:su * 16 + 9 + h],
                                 scalar2=None, op0=ALU.mult))
                    if _SB < 7:
                        continue
                    for hb in range(2):
                        mb, mkey = psum[2 + hb], PS[2 + hb]
                        gb, gkey = psum[4 + hb], PS[4 + hb]

                        def maskmm(e, hb=hb, mb=mb, t0=t0):
                            e.matmul(mb[:], ident[:], negr[:],
                                     start=True, stop=False)
                            last = None
                            for hq in range(4):
                                h = hb * 4 + hq
                                sl = mb[:, hq * 128:(hq + 1) * 128]
                                last = e.matmul(sl, selh[:, h * 128:(h + 1) * 128],
                                                s_sb[0:8, t0:t0 + 128],
                                                start=False, stop=(hq == 3))
                            return last
                        if _SM >= 1:
                            p.op("pe", ["s_sb", "selh", "ident", "negr"], [mkey],
                                 emit=maskmm)
                        for hq in range(4 if _SM >= 2 else 0):
                            h = hb * 4 + hq
                            p.op("act", [mkey, "colT"], ["eb"],
                                 emit=lambda e, h=h, hq=hq, su=su, mb=mb:
                                 e.activation(
                                     out=v_eb[:, h * 128:(h + 1) * 128],
                                     in_=mb[:, hq * 128:(hq + 1) * 128],
                                     func=AF.Exp,
                                     bias=colT[:, su * 16 + h:su * 16 + h + 1]))

                        def gmm(e, hb=hb, gb=gb, t0=t0):
                            last = None
                            for hq in range(4):
                                h = hb * 4 + hq
                                cpad = v_ctil if h % 2 == 0 else v_cth
                                last = e.matmul(
                                    gb[:, hq * 128:(hq + 1) * 128],
                                    v_bfm[:, (h // 2) * CH_B + t0:(h // 2) * CH_B + t0 + 128],
                                    cpad[:, (h // 2) * CH_B + t0:(h // 2) * CH_B + t0 + 128],
                                    start=True, stop=True)
                            return last
                        if _SM >= 3:
                            p.op("pe", ["bfm", "ctil", "cth"], [gkey], emit=gmm)
                        if _SM >= 4:
                            p.op("dve", [gkey, "eb"], ["mg"],
                                 emit=lambda e, gb=gb, hb=hb: e.tensor_tensor(
                                     out=v_mg[:, hb * 512:(hb + 1) * 512], in0=gb[:],
                                     in1=v_eb[:, hb * 512:(hb + 1) * 512], op=ALU.mult))
                            p.op("dve", ["mg", "dic"], ["mg"],
                                 emit=lambda e, hb=hb: e.tensor_tensor(
                                     out=v_mg[:, hb * 512:(hb + 1) * 512],
                                     in0=v_mg[:, hb * 512:(hb + 1) * 512],
                                     in1=dic[:, hb * 512:(hb + 1) * 512], op=ALU.add))
                    if _SB < 8:
                        continue
                    for j in range(4):
                        yb, ykey = psum[j], PS[j]

                        def ymm(e, j=j, t0=t0, xtm=xtm):
                            last = None
                            for q in range(4):
                                ct = 4 * j + q
                                h = ct // 2
                                sl = psum[j][:, q * 128:(q + 1) * 128]
                                e.matmul(sl,
                                         xtm[:, ct * 128:(ct + 1) * 128],
                                         v_mg[:, h * 128:(h + 1) * 128],
                                         start=True, stop=False)
                                cpad = v_ctil if h % 2 == 0 else v_cth
                                last = e.matmul(
                                    sl,
                                    S_bf[:, (h // 2) * HD + (ct % 2) * 128:
                                         (h // 2) * HD + (ct % 2) * 128 + 128],
                                    cpad[:, (h // 2) * CH_B + t0:(h // 2) * CH_B + t0 + 128],
                                    start=False, stop=True)
                            return last
                        p.op("pe", [xkey, "mg", "dic", "S_bf", "ctil", "cth"], [ykey],
                             emit=ymm)
                        for half in range(2):
                            col = c * NSUB * 8 + su * 8 + j * 2 + half
                            p.op("dve", [ykey], ["ys", "yt"],
                                 emit=lambda e, j=j, su=su, half=half, col=col:
                                 e.tensor_scalar(
                                     out=bass.AP(v_yt.tensor,
                                                 (4 * j + 2 * half) * CH_B + su * 128,
                                                 [[8192, 128], [CH_B, 2], [1, 128]]),
                                     in0=psum[j][:, half * 256:(half + 1) * 256],
                                     scalar1=1.0, scalar2=0.0, op0=ALU.mult,
                                     op1=ALU.add,
                                     accum_out=ys[:, col:col + 1]))
                            p.op("act", [ykey], ["yss", "dumpA"],
                                 emit=lambda e, j=j, half=half, col=col:
                                 e.activation(
                                     out=dumpA[:, 0:256],
                                     in_=psum[j][:, half * 256:(half + 1) * 256],
                                     func=AF.Square, accum_out=yss[:, col:col + 1]))
                    if _SB < 9:
                        continue
                    for g in range(4):
                        bi = 4 + (g % 2)
                        db, dkey = psum[bi], PS[bi]

                        def dsmm(e, g=g, db=db, xtm=xtm, btm=btm):
                            e.matmul(db[:, 0:HD],
                                     btm[:, g * 256:g * 256 + 128],
                                     xtm[:, (2 * g) * HD:(2 * g + 1) * HD],
                                     start=True, stop=False)
                            return e.matmul(
                                db[:, 0:HD],
                                btm[:, g * 256 + 128:g * 256 + 256],
                                xtm[:, (2 * g + 1) * HD:(2 * g + 2) * HD],
                                start=False, stop=True)
                        p.op("pe", [btkey, xkey], [dkey], emit=dsmm)
                        p.op("dve", [dkey, S_keys[0], "lamc"], [S_keys[1]],
                             emit=lambda e, g=g, su=su, db=db, Sc=S_cur, Sn=S_nxt:
                             e.scalar_tensor_tensor(
                                 out=Sn[:, g * HD:(g + 1) * HD],
                                 in0=Sc[:, g * HD:(g + 1) * HD],
                                 scalar=lamc[:, su * 4 + g:su * 4 + g + 1],
                                 in1=db[:, 0:HD], op0=ALU.mult, op1=ALU.add))
                        p.op("dve", [S_keys[1]], ["S_bf"],
                             emit=lambda e, g=g, Sn=S_nxt: e.tensor_copy(
                                 out=S_bf[:, g * HD:(g + 1) * HD],
                                 in_=Sn[:, g * HD:(g + 1) * HD]))
                    S_cur, S_nxt = S_nxt, S_cur
                    S_keys = S_keys[::-1]
                for q in range(4):
                    y_view = bass.AP(sp_y.tensor, b * INNER * L + q * 4 * 128 * L + T0,
                                     [[L, 128], [128 * L, 4], [1, CH_B]])
                    dma("pool", y_view,
                        bass.AP(v_yt.tensor, q * 4 * CH_B,
                                [[8192, 128], [CH_B, 4], [1, CH_B]]),
                        ["yt"], ["sp_y"])
            def gnred(e):
                e.matmul(psum[0][0:1, 0:NCH_B * NSUB * 8], onesb[:, 0:1], ys[:],
                         start=True, stop=True)
                return e.matmul(psum[1][0:1, 0:NCH_B * NSUB * 8], onesb[:, 0:1],
                                yss[:], start=True, stop=True)
            p.op("pe", ["ys", "yss", "onesb"], [PS[0], PS[1]], emit=gnred)
            p.op("dve", [PS[0]], ["stat8"],
                 emit=lambda e: e.reduce_sum(
                     out=stat8[0:1, 0:8],
                     in_=psum[0][0:1, 0:NCH_B * NSUB * 8].rearrange(
                         "p (a b) -> p b a", b=8), axis=AX.X))
            p.op("dve", [PS[1]], ["stat8"],
                 emit=lambda e: e.reduce_sum(
                     out=stat8[0:1, 8:16],
                     in_=psum[1][0:1, 0:NCH_B * NSUB * 8].rearrange(
                         "p (a b) -> p b a", b=8), axis=AX.X))
            NGN = float(L * HD)
            p.op("dve", ["stat8"], ["stat8"],
                 emit=lambda e: e.tensor_scalar(out=stat8[0:1, 16:24],
                                                in0=stat8[0:1, 0:8],
                                                scalar1=1.0 / NGN, scalar2=None,
                                                op0=ALU.mult))
            p.op("dve", ["stat8"], ["stat8"],
                 emit=lambda e: e.tensor_scalar(out=stat8[0:1, 24:32],
                                                in0=stat8[0:1, 8:16],
                                                scalar1=1.0 / NGN, scalar2=None,
                                                op0=ALU.mult))
            p.op("dve", ["stat8"], ["stat8"],
                 emit=lambda e: e.tensor_mul(out=stat8[0:1, 32:40],
                                             in0=stat8[0:1, 16:24],
                                             in1=stat8[0:1, 16:24]))
            p.op("dve", ["stat8"], ["stat8"],
                 emit=lambda e: e.tensor_sub(out=stat8[0:1, 40:48],
                                             in0=stat8[0:1, 24:32],
                                             in1=stat8[0:1, 32:40]))
            p.op("act", ["stat8", "cst"], ["stat8"],
                 emit=lambda e: e.activation(out=stat8[0:1, 40:48],
                                             in_=stat8[0:1, 40:48], func=AF.Ln,
                                             bias=cst[0:1, 0:1]))
            p.op("act", ["stat8", "cst"], ["stat8"],
                 emit=lambda e: e.activation(out=stat8[0:1, 48:56],
                                             in_=stat8[0:1, 40:48], func=AF.Exp,
                                             scale=cst[0:1, 1:2]))
            def bc16(e):
                e.matmul(psum[2][:, 0:8], onesc[0:1, 0:128], stat8[0:1, 48:56],
                         start=True, stop=True)
                return e.matmul(psum[2][:, 8:16], onesc[0:1, 0:128],
                                stat8[0:1, 16:24], start=True, stop=True)
            p.op("pe", ["stat8", "ones"], [PS[2]], emit=bc16)
            p.op("dve", [PS[2]], ["rinv8"],
                 emit=lambda e, b=b: e.tensor_copy(out=rinv8[:, b * 16:b * 16 + 16],
                                                   in_=psum[2][:, 0:16]))
            p.op("dve", ["rinv8", "gnwc"], ["gnwp"],
                 emit=lambda e, b=b: e.tensor_tensor(
                     out=gnwp[:, b * KI:(b + 1) * KI], in0=gnwc[:],
                     in1=bass.AP(rinv8, b * 16,
                                 [[BPC * 16, 128], [1, 8], [0, 2]]),
                     op=ALU.mult))
            p.op("dve", ["gnwp", "rinv8"], ["gnbp"],
                 emit=lambda e, b=b: e.tensor_tensor(
                     out=gnbp[:, b * KI:(b + 1) * KI],
                     in0=gnwp[:, b * KI:(b + 1) * KI],
                     in1=bass.AP(rinv8, b * 16 + 8,
                                 [[BPC * 16, 128], [1, 8], [0, 2]]),
                     op=ALU.mult))
            p.op("dve", ["gnbp", "gnbc"], ["gnbp"],
                 emit=lambda e, b=b: e.tensor_sub(
                     out=gnbp[:, b * KI:(b + 1) * KI], in0=gnbc[:],
                     in1=gnbp[:, b * KI:(b + 1) * KI]))

        prep_w(dWout, D, INNER, slotB, "slotB", lambda kt, ot: kt * D + ot * 128)

        p.fence(ALL_POOL_KEYS)

        def p2_loads(b, c):
            T0 = c * CH_A
            par = c % 2
            v_yr, v_zr = v_yrp[par], v_zrp[par]
            kyr, kzr = f"yr_{par}", f"zr_{par}"
            for q in range(4):
                y_view = bass.AP(sp_y.tensor, b * INNER * L + q * 4 * 128 * L + T0,
                                 [[L, 128], [128 * L, 4], [1, CH_A]])
                dma("sp",
                    bass.AP(v_yr.tensor, par * 4096 + q * 4 * CH_A,
                            [[8192, 128], [CH_A, 4], [1, CH_A]]),
                    y_view, ["sp_y"], [kyr])
                z_view = bass.AP(sp_z.tensor, b * INNER * L + q * 4 * 128 * L + T0,
                                 [[L, 128], [128 * L, 4], [1, CH_A]])
                dma("pool",
                    bass.AP(v_zr.tensor, par * 4096 + q * 4 * CH_A,
                            [[8192, 128], [CH_A, 4], [1, CH_A]]),
                    z_view, ["sp_z"], [kzr])

        def p2_gy(b, c):
            par = c % 2
            v_yr, v_zr, v_gy = v_yrp[par], v_zrp[par], v_gyp[par]
            kyr, kzr, kgy = f"yr_{par}", f"zr_{par}", f"gy_{par}"
            for og in range(KI // 2):
                ot = 2 * og
                acc0 = acc_a if og % 2 == 0 else acc_b
                accs = acc_s1
                ka = f"acca{og % 2}"
                kb = "accb"
                p.op("dve", [kyr, "gnwp", "gnbp"], [ka],
                     emit=lambda e, ot=ot, b=b, acc0=acc0, v_yr=v_yr:
                     e.tensor_scalar(
                         out=acc0[:, 0:CH_A],
                         in0=v_yr[:, ot * CH_A:(ot + 1) * CH_A],
                         scalar1=gnwp[:, b * KI + ot:b * KI + ot + 1],
                         scalar2=gnbp[:, b * KI + ot:b * KI + ot + 1],
                         op0=ALU.mult, op1=ALU.add))
                p.op("dve", [kyr, "gnwp", "gnbp", ka], [ka],
                     emit=lambda e, ot=ot, b=b, acc0=acc0, v_yr=v_yr:
                     e.tensor_scalar(
                         out=acc0[:, CH_A:2 * CH_A],
                         in0=v_yr[:, (ot + 1) * CH_A:(ot + 2) * CH_A],
                         scalar1=gnwp[:, b * KI + ot + 1:b * KI + ot + 2],
                         scalar2=gnbp[:, b * KI + ot + 1:b * KI + ot + 2],
                         op0=ALU.mult, op1=ALU.add))
                p.op("act", [kzr], [kb],
                     emit=lambda e, ot=ot, accs=accs, v_zr=v_zr: e.activation(
                         out=accs[:, 0:2 * CH_A],
                         in_=v_zr[:, ot * CH_A:(ot + 2) * CH_A], func=AF.Silu))
                p.op("dve", [ka, kb], [kgy],
                     emit=lambda e, ot=ot, acc0=acc0, accs=accs, v_gy=v_gy:
                     e.tensor_tensor(
                         out=v_gy[:, ot * CH_A:(ot + 2) * CH_A],
                         in0=acc0[:, 0:2 * CH_A], in1=accs[:, 0:2 * CH_A],
                         op=ALU.mult))

        def p2_main_a(b, c):
            T0 = c * CH_A
            par = c % 2
            v_x2 = v_x2p[par]
            v_gy = v_gyp[par]
            kx2, kgy = f"x2_{par}", f"gy_{par}"
            x1_view = bass.AP(sp_x1.tensor, (b * L + T0) * D,
                              [[D, 128], [128 * D, NT_A], [1, D]])
            dma("sp", v_x1r.rearrange("p (a b) -> p a b", a=NT_A), x1_view,
                ["sp_x1"], ["x1r"])
            for tt in range(NT_A):
                for nh in range(2):
                    bi = (tt * 2 + nh) % 4
                    bank, bkey = psum[bi], PS[bi]

                    def womm(e, tt=tt, nh=nh, bank=bank, v_gy=v_gy):
                        last = None
                        for k in range(KI):
                            last = e.matmul(
                                bank[:],
                                v_gy[:, k * CH_A + tt * 128:k * CH_A + (tt + 1) * 128],
                                slotB[:, k * D + nh * 512:k * D + (nh + 1) * 512],
                                start=(k == 0), stop=(k == KI - 1))
                        return last
                    p.op("pe", [kgy, "slotB"], [bkey], emit=womm)
                    p.op("dve", [bkey, "x1r"], [kx2],
                         emit=lambda e, tt=tt, nh=nh, bank=bank, v_x2=v_x2:
                         e.tensor_tensor(
                             out=v_x2[:, tt * D + nh * 512:tt * D + (nh + 1) * 512],
                             in0=bank[:],
                             in1=v_x1r[:, tt * D + nh * 512:tt * D + (nh + 1) * 512],
                             op=ALU.add))

        def p2_ln(b, c):
            par = c % 2
            layernorm(v_x2p[par], f"x2_{par}", v_utm2, "utm2", par)

        def p2_main_b(b, c):
            T0 = c * CH_A
            par = c % 2
            v_x2 = v_x2p[par]
            v_ufm2 = v_ufm2p[par]
            kx2, kuf2 = f"x2_{par}", f"ufm2_{par}"
            transpose_u(v_utm2, "utm2", v_ufm2, kuf2, CH_A)
            for tt in range(NT_A):
                for nh in range(2):
                    bi = 4 + (tt * 2 + nh) % 2
                    bank, bkey = psum[bi], PS[bi]

                    def f2(e, tt=tt, nh=nh, bank=bank, v_ufm2=v_ufm2):
                        last = None
                        for k in range(KD):
                            last = e.matmul(
                                bank[:],
                                v_ufm2[:, k * CH_A + tt * 128:k * CH_A + (tt + 1) * 128],
                                wS[:, k * D + nh * 512:k * D + (nh + 1) * 512],
                                start=(k == 0), stop=(k == KD - 1))
                        return last
                    p.op("pe", [kuf2, "wS"], [bkey], emit=f2)
                    p.op("dve", [bkey, kx2], ["x3"],
                         emit=lambda e, tt=tt, nh=nh, bank=bank, v_x2=v_x2:
                         e.scalar_tensor_tensor(
                             out=v_x3[:, nh * 512:(nh + 1) * 512],
                             in0=bank[:], scalar=0.5,
                             in1=v_x2[:, tt * D + nh * 512:tt * D + (nh + 1) * 512],
                             op0=ALU.mult, op1=ALU.add))
                out_view = bass.AP(dout.tensor, (b * L + T0 + tt * 128) * D,
                                   [[D, 128], [1, D]])
                dma("sp", out_view, v_x3[:], ["x3"], ["dout"])

        if _PH >= 5:
            chunks = [(b, c) for b in range(BPC) for c in range(NCH_A)]
            p2_loads(*chunks[0])
            p2_gy(*chunks[0])
            p2_loads(*chunks[1])
            p2_gy(*chunks[1])
            p2_main_a(*chunks[0])
            for i, (b, c) in enumerate(chunks):
                p2_ln(b, c)
                if i + 2 < len(chunks):
                    p2_loads(*chunks[i + 2])
                    p2_gy(*chunks[i + 2])
                if i + 1 < len(chunks):
                    p2_main_a(*chunks[i + 1])
                p2_main_b(b, c)

        p.op("pool", ["dout"], [], emit=lambda e: e.engine_nop())

        eng_map = {"pe": block.tensor, "dve": block.vector, "act": block.scalar,
                   "pool": block.gpsimd, "sp": block.sync}
        for eng in Prog.ENGS:
            ops = p.ops[eng]
            if not ops:
                continue

            def section(e, _ops=ops, _sem=sems[eng]):
                for waits, emit, slotkey in _ops:
                    for src, v in waits:
                        e.wait_ge(sems[src], v)
                    inst = emit(e)
                    if slotkey is None:
                        inst.then_inc(_sem, 1)
                    else:
                        inst.then_inc(sems[slotkey], 16)
            eng_map[eng](section)

    return nc


def _consts():
    ident = np.eye(128, dtype=BF)
    identf = np.eye(8, dtype=np.float32)
    negr = np.tile(np.where(np.arange(128)[:, None] > np.arange(128)[None, :],
                            -1e30, 0.0), (1, 4)).astype(BF)
    ones_c = np.ones((1, 128), np.float32)
    ones_b = np.ones((128, 1), np.float32)
    seg = np.ones((HEADS, CH_B), np.float32)
    seg[:, ::SUB] = 0.0
    sel = np.zeros((HEADS, 4 * 128), BF)
    for g in range(4):
        for pp in range(128):
            sel[2 * g + pp // 64, g * 128 + pp] = 1.0
    selh = np.zeros((HEADS, HEADS * 128), np.float32)
    for h in range(HEADS):
        selh[h, h * 128:(h + 1) * 128] = 1.0
    return dict(ident=ident, identf=identf, negr=negr,
                ones_c=ones_c, ones_b=ones_b, seg=seg, sel=sel, selh=selh)


def kernel(L0_attn_input_tensor, L0_attn_Wx, L0_attn_Wz, L0_attn_conv_w,
           L0_attn_conv_b, L0_attn_Wb, L0_attn_Wc, L0_attn_Wdt, L0_attn_b_dt,
           L0_attn_log_a, L0_attn_d_param, L0_attn_gn_w, L0_attn_gn_b,
           L0_attn_Wout, L0_ffn_weight):
    x = np.ascontiguousarray(L0_attn_input_tensor)
    consts = _consts()
    shared = dict(
        Wx=np.ascontiguousarray(L0_attn_Wx),
        Wz=np.ascontiguousarray(L0_attn_Wz),
        conv_w=np.ascontiguousarray(
            L0_attn_conv_w[:, 0, :].reshape(KI, 128, KCONV).transpose(1, 0, 2)
            .reshape(128, KI * KCONV)),
        conv_b=np.ascontiguousarray(L0_attn_conv_b.reshape(KI, 128).T),
        Wb=np.ascontiguousarray(L0_attn_Wb),
        Wc=np.ascontiguousarray(L0_attn_Wc),
        Wdt=np.ascontiguousarray(L0_attn_Wdt),
        b_dt=np.ascontiguousarray(L0_attn_b_dt.reshape(HEADS, 1)),
        log_a=np.ascontiguousarray(L0_attn_log_a.reshape(HEADS, 1)),
        d_param=np.ascontiguousarray(L0_attn_d_param.reshape(HEADS, 1)),
        gn_w=np.ascontiguousarray(L0_attn_gn_w.reshape(KI, 128).T),
        gn_b=np.ascontiguousarray(L0_attn_gn_b.reshape(KI, 128).T),
        Wout=np.ascontiguousarray(L0_attn_Wout),
        Wf=np.ascontiguousarray(L0_ffn_weight),
        **consts,
    )
    nc = build_program()
    in_maps = []
    for i in range(NCORES):
        m = dict(shared)
        m["x"] = np.ascontiguousarray(x[i * BPC:(i + 1) * BPC])
        in_maps.append(m)
    res = run_bass_kernel_spmd(nc, in_maps, core_ids=list(range(NCORES)))
    out = np.concatenate([r["out"] for r in res.results], axis=0)
    return out.astype(np.float32)


# version 23
# speedup vs baseline: 1.1360x; 1.0080x over previous
import contextlib
import os as _os
import numpy as np
import ml_dtypes
import concourse.bass as bass
import concourse.mybir as mybir
from concourse.bass_utils import run_bass_kernel_spmd

F32 = mybir.dt.float32
BF16 = mybir.dt.bfloat16
AF = mybir.ActivationFunctionType
ALU = mybir.AluOpType
AX = mybir.AxisListType
BF = ml_dtypes.bfloat16

NCORES = 8
BPC = 2
L = 4096
D = 1024
INNER = 2048
HEADS = 8
STATE = 64
HD = 256
KCONV = 4
EPS = 1e-5

CH_A = 256
CH_B = 512
SUB = 128
KD = D // 128
KI = INNER // 128
PO = STATE * HEADS
PK = 2 * PO + HEADS

NT_A = CH_A // 128
NCH_A = L // CH_A
NT_B = CH_B // 128
NSUB = CH_B // SUB
NCH_B = L // CH_B


class Prog:
    ENGS = ("pe", "dve", "act", "pool", "sp")
    NSLOT = 8

    def __init__(self):
        self.ops = {e: [] for e in self.ENGS}
        self.cnt = {e: 0 for e in self.ENGS}
        self.dma_slot_cnt = {}
        self.dma_n = {"sp": 0, "pool": 0}
        self.hw = {}
        self.last_w = {}
        self.readers = {}

    def op(self, eng, reads=(), writes=(), emit=None, dma=False):
        needs = {}

        def need(src, v):
            if v > needs.get(src, -1):
                needs[src] = v

        reads = list(reads)
        writes = list(writes)
        excl = [k for k in reads if k.startswith("pb") or k.startswith("pt")]
        for k in excl:
            reads.remove(k)
            writes.append(k)
        for k in reads:
            for src, v in self.last_w.get(k, {}).items():
                need(src, v)
        for k in writes:
            if not dma:
                for src, v in self.last_w.get(k, {}).items():
                    need(src, v)
            else:
                for src, v in self.last_w.get(k, {}).items():
                    if src in self.ENGS:
                        need(src, v)
            for r_src, r_cnt in self.readers.get(k, {}).items():
                need(r_src, r_cnt)

        if dma:
            slot = self.dma_n[eng] % self.NSLOT
            self.dma_n[eng] += 1
            srckey = f"{eng}{slot}"
            prev = self.dma_slot_cnt.get(srckey, 0)
            if prev > 0:
                need(srckey, prev)
            self.dma_slot_cnt[srckey] = prev + 16
            after_src = srckey
            after_val = prev + 16
        else:
            srckey = eng
            self.cnt[eng] += 1
            after_src = eng
            after_val = self.cnt[eng]
            if after_val > 255 and self.hw.get((eng, eng), 0) < after_val - 255:
                need(eng, after_val - 200)

        waits = []
        for src, v in sorted(needs.items()):
            cur = self.hw.get((eng, src), 0)
            if v > cur:
                waits.append((src, v))
                self.hw[(eng, src)] = v
        self.ops[eng].append((waits, emit, after_src if dma else None))
        for k in reads:
            self.readers.setdefault(k, {})[after_src] = after_val
        for k in writes:
            if dma:
                d = self.last_w.setdefault(k, {})
                for src in [s for s in d if s in self.ENGS]:
                    del d[src]
                d[after_src] = max(d.get(after_src, 0), after_val)
            else:
                self.last_w[k] = {after_src: after_val}
                self.readers[k] = {}
        return after_val

    def fence(self, keys):
        self.op("dve", reads=list(keys), writes=list(keys),
                emit=lambda e: e.engine_nop())


def build_program():
    _PH = int(_os.environ.get("KBISECT", "5"))
    _SB = int(_os.environ.get("KBISECT2", "9"))
    _SM = int(_os.environ.get("KBISECT3", "9"))
    nc = bass.Bass("TRN2", target_bir_lowering=False)

    dx = nc.dram_tensor("x", [BPC, L, D], F32, kind="ExternalInput").ap()
    dWx = nc.dram_tensor("Wx", [INNER, D], F32, kind="ExternalInput").ap()
    dWz = nc.dram_tensor("Wz", [INNER, D], F32, kind="ExternalInput").ap()
    dcw = nc.dram_tensor("conv_w", [128, KI * KCONV], F32, kind="ExternalInput").ap()
    dcb = nc.dram_tensor("conv_b", [128, KI], F32, kind="ExternalInput").ap()
    dWb = nc.dram_tensor("Wb", [PO, INNER], F32, kind="ExternalInput").ap()
    dWc = nc.dram_tensor("Wc", [PO, INNER], F32, kind="ExternalInput").ap()
    dWdt = nc.dram_tensor("Wdt", [HEADS, INNER], F32, kind="ExternalInput").ap()
    dbdt = nc.dram_tensor("b_dt", [HEADS, 1], F32, kind="ExternalInput").ap()
    dloga = nc.dram_tensor("log_a", [HEADS, 1], F32, kind="ExternalInput").ap()
    ddpar = nc.dram_tensor("d_param", [HEADS, 1], F32, kind="ExternalInput").ap()
    dgnw = nc.dram_tensor("gn_w", [128, KI], F32, kind="ExternalInput").ap()
    dgnb = nc.dram_tensor("gn_b", [128, KI], F32, kind="ExternalInput").ap()
    dWout = nc.dram_tensor("Wout", [D, INNER], F32, kind="ExternalInput").ap()
    dWf = nc.dram_tensor("Wf", [D, D], F32, kind="ExternalInput").ap()
    dident = nc.dram_tensor("ident", [128, 128], BF16, kind="ExternalInput").ap()
    didentf = nc.dram_tensor("identf", [8, 8], F32, kind="ExternalInput").ap()
    dnegr = nc.dram_tensor("negr", [128, 512], BF16, kind="ExternalInput").ap()
    dones = nc.dram_tensor("ones_c", [1, 128], F32, kind="ExternalInput").ap()
    donesb = nc.dram_tensor("ones_b", [128, 1], F32, kind="ExternalInput").ap()
    dseg = nc.dram_tensor("seg", [HEADS, CH_B], F32, kind="ExternalInput").ap()
    dsel = nc.dram_tensor("sel", [HEADS, 4 * 128], BF16, kind="ExternalInput").ap()
    dselh = nc.dram_tensor("selh", [HEADS, HEADS * 128], F32, kind="ExternalInput").ap()

    dout = nc.dram_tensor("out", [BPC, L, D], F32, kind="ExternalOutput").ap()
    sp_x1 = nc.dram_tensor("sp_x1", [BPC * L, D], BF16, kind="Internal").ap()
    sp_xc = nc.dram_tensor("sp_xc", [BPC, INNER, L], BF16, kind="Internal").ap()
    sp_z = nc.dram_tensor("sp_z", [BPC, INNER, L], BF16, kind="Internal").ap()
    sp_y = nc.dram_tensor("sp_y", [BPC, INNER, L], BF16, kind="Internal").ap()

    p = Prog()
    ctx = contextlib.ExitStack()
    with ctx:
        ST = lambda name, shape, dt_: ctx.enter_context(nc.sbuf_tensor(name, shape, dt_))

        wS = ST("wS", [128, KD * D], BF16)
        slotA = ST("slotA", [128, KI * PK], BF16)
        slotB = ST("slotB", [128, KD * INNER], BF16)
        G1 = ST("G1", [128, 8192], BF16)
        G2 = ST("G2", [128, 8192], BF16)
        G3 = ST("G3", [128, 8192], BF16)
        G4 = ST("G4", [128, 8192], BF16)
        G5 = ST("G5", [128, 6144], BF16)
        xp = ST("xp", [128, 515], BF16)
        xpB = ST("xpB", [128, 515], BF16)
        halo = ST("halo", [128, KI * 3], BF16)
        acc_a = ST("acc_a", [128, CH_B], F32)
        acc_b = ST("acc_b", [128, CH_B], F32)
        s_sb = ST("s_sb", [8, CH_B], F32)
        nslf = ST("nslf", [8, CH_B], F32)
        dtv = ST("dtv", [8, CH_B], F32)
        scbf = ST("scbf", [8, CH_B], F32)
        esr = ST("esr", [8, CH_B], BF16)
        tin = ST("tin", [8, 2 * CH_B], F32)
        colT = ST("colT", [128, NSUB * 16], F32)
        lamc = ST("lamc", [128, NSUB * 4], F32)
        S_a = ST("S_a", [128, 4 * HD], F32)
        S_b = ST("S_b", [128, 4 * HD], F32)
        S_bf = ST("S_bf", [128, 4 * HD], BF16)
        ys = ST("ys", [128, NCH_B * NSUB * 8], F32)
        yss = ST("yss", [128, NCH_B * NSUB * 8], F32)
        c1 = ST("c1", [128, 2 * NT_A], F32)
        c2 = ST("c2", [128, 4 * NT_A], F32)
        cm = ST("cm", [128, 2 * NT_A], F32)
        cv = ST("cv", [128, 2 * NT_A], F32)
        crv = ST("crv", [128, 2 * NT_A], F32)
        cnm = ST("cnm", [128, 2 * NT_A], F32)
        dumpA = ST("dumpA", [128, 512], BF16)
        esb = ST("esbuf", [128, CH_B], BF16)
        gnwp = ST("gnwp", [128, BPC * KI], F32)
        gnbp = ST("gnbp", [128, BPC * KI], F32)
        stat8 = ST("stat8", [1, 56], F32)
        rinv8 = ST("rinv8", [128, BPC * 16], F32)
        wstg = ST("wstg", [128, 2048], BF16)
        ident = ST("identc", [128, 128], BF16)
        identf = ST("identfc", [8, 8], F32)
        negr = ST("negrc", [128, 512], BF16)
        onesc = ST("onescc", [1, 128], F32)
        onesb = ST("onesbc", [128, 1], F32)
        segc = ST("segcc", [8, CH_B], F32)
        selc = ST("selcc", [8, 4 * 128], BF16)
        selh = ST("selhc", [8, 8 * 128], F32)
        cwc = ST("cwcc", [128, KI * KCONV], F32)
        cbc = ST("cbcc", [128, KI], F32)
        bdtc = ST("bdtcc", [8, 1], F32)
        acol = ST("acolc", [8, 1], F32)
        dcol = ST("dcolc", [128, 8], F32)
        dic = ST("dicc", [128, 8 * 128], BF16)
        gnwc = ST("gnwcc", [128, KI], F32)
        gnbc = ST("gnbcc", [128, KI], F32)
        cst = ST("cstc", [128, 2], F32)

        psum = [ctx.enter_context(nc.psum_tensor(f"pb{i}", [128, 512], F32))
                for i in range(6)]
        ptr = [ctx.enter_context(nc.psum_tensor(f"pt{i}", [128, 1024], BF16))
               for i in range(2)]
        PS = [f"pb{i}" for i in range(6)]
        PT = [f"pt{i}" for i in range(2)]

        sems = {e: ctx.enter_context(nc.semaphore(f"sm_{e}")) for e in Prog.ENGS}
        for q in ("sp", "pool"):
            for s in range(Prog.NSLOT):
                sems[f"{q}{s}"] = ctx.enter_context(nc.semaphore(f"sm_{q}{s}"))
        block = ctx.enter_context(nc.Block())

        def dma(queue, out, in_, reads, writes):
            p.op(queue, reads=reads, writes=writes,
                 emit=lambda e, o=out, i=in_: e.dma_start(out=o, in_=i), dma=True)

        bfv = lambda t, c0, c1_: t[:, c0:c1_]
        f32v = lambda t, c0, c1_: t[:].bitcast(F32)[:, c0:c1_]
        v_x0p = [bfv(G5, 0, 2048), bfv(G4, 0, 2048)]
        v_x1p = [bfv(G5, 2048, 4096), bfv(G4, 2048, 4096)]
        v_utmp = [bfv(G1, 0, 2048), bfv(G1, 2048, 4096)]
        v_ufmp = [bfv(G1, 4096, 6144), bfv(G1, 6144, 8192),
                  bfv(G3, 4096, 6144), bfv(G3, 6144, 8192)]
        u2_src = [(G1, 4096), (G3, 4096)]
        v_u1fp = [bfv(G3, 0, 2048), bfv(G3, 2048, 4096)]
        v_u2 = bfv(G1, 0, KD * CH_B)
        v_xcb1 = bfv(G2, 0, 2048)
        v_zb1 = bfv(G2, 2048, 4096)
        v_xcr = bfv(G1, 0, 8192)
        v_xcs = bfv(G2, 0, 8192)
        v_yt = bfv(G3, 0, 8192)
        v_xtmA = bfv(G4, 0, 2048)
        v_xtmB = bfv(G4, 2048, 4096)
        v_bfm = bfv(G4, 4096, 6144)
        v_ctil = bfv(G4, 6144, 8192)
        v_btmA = bfv(G5, 0, 1024)
        v_btmB = bfv(G5, 1024, 2048)
        v_mg = bfv(G5, 2048, 3072)
        v_eb = bfv(G5, 3072, 4096)
        v_cth = bfv(G5, 4096, 6144)
        v_yrp = [bfv(G1, 0, 4096), bfv(G1, 4096, 8192)]
        v_zrp = [bfv(G2, 0, 4096), bfv(G2, 4096, 8192)]
        v_gyp = [bfv(G3, 0, 4096), bfv(G3, 4096, 8192)]
        v_x1r = bfv(G4, 0, 2048)
        v_x2p = [bfv(G4, 2048, 4096), bfv(G5, 0, 2048)]
        v_utm2 = bfv(G4, 4096, 6144)
        v_ufm2p = [bfv(G4, 6144, 8192), bfv(G5, 2048, 4096)]
        v_x3 = f32v(G5, 2048, 3072)

        ALL_POOL_KEYS = ["x0_0", "x0_1", "x1b_0", "x1b_1", "utm_0", "utm_1",
                         "ufm_0", "ufm_1", "ufm_2", "ufm_3", "u1f_0", "u1f_1", "x2_0", "x2_1", "ufm2_0", "ufm2_1",
                         "u2", "xcb1", "zb1",
                         "xcr", "xcs", "xcs0", "xcs1", "yt", "xtmA", "xtmB", "bfm", "ctil",
                         "btmA", "btmB", "mg", "eb", "cth", "yr", "zr", "gy",
                         "x1r", "utm2", "x3", "yr_0", "yr_1", "zr_0", "zr_1", "gy_0", "gy_1"]

        p.op("dve", [], ["cst"], emit=lambda e: e.memset(cst[:, 0:1], EPS))
        p.op("dve", [], ["cst"], emit=lambda e: e.memset(cst[:, 1:2], -0.5))
        dma("sp", ident[:], dident, [], ["ident"])
        dma("sp", identf[:], didentf, [], ["identf"])
        dma("sp", negr[:], dnegr, [], ["negr"])
        dma("sp", onesc[:], dones, [], ["ones"])
        dma("sp", onesb[:], donesb, [], ["onesb"])
        dma("sp", segc[:], dseg, [], ["seg"])
        dma("sp", selc[:], dsel, [], ["sel"])
        dma("sp", selh[:], dselh, [], ["selh"])
        dma("sp", cwc[:], dcw, [], ["cwc"])
        dma("sp", cbc[:], dcb, [], ["cbc"])
        dma("sp", bdtc[:], dbdt, [], ["bdtc"])
        dma("sp", gnwc[:], dgnw, [], ["gnwc"])
        dma("sp", gnbc[:], dgnb, [], ["gnbc"])
        dma("sp", acol[:], dloga, [], ["acol"])
        dma("sp", dcol[0:8, 0:1], ddpar, [], ["dcol"])

        p.op("act", ["acol"], ["acol"],
             emit=lambda e: e.activation(out=acol[:], in_=acol[:], func=AF.Exp))
        p.op("dve", ["acol"], ["acol"],
             emit=lambda e: e.tensor_scalar(out=acol[:], in0=acol[:], scalar1=-1.0,
                                            scalar2=None, op0=ALU.mult))
        p.op("pe", ["dcol", "identf"], [PS[0]],
             emit=lambda e: e.transpose(psum[0][0:1, 0:8], dcol[0:8, 0:1],
                                        identf[0:8, 0:8]))
        p.op("dve", [PS[0]], ["dcol"],
             emit=lambda e: e.tensor_copy(out=dcol[0:1, 0:8], in_=psum[0][0:1, 0:8]))
        p.op("pe", ["dcol", "ones"], [PS[1]],
             emit=lambda e: e.matmul(psum[1][:, 0:8], onesc[0:1, 0:128],
                                     dcol[0:1, 0:8], start=True, stop=True))
        p.op("dve", [PS[1]], ["dcol"],
             emit=lambda e: e.tensor_copy(out=dcol[:, 0:8], in_=psum[1][:, 0:8]))
        for h in range(8):
            p.op("dve", ["dcol", "ident"], ["dic"],
                 emit=lambda e, h=h: e.tensor_scalar(
                     out=dic[:, h * 128:(h + 1) * 128], in0=ident[:],
                     scalar1=dcol[:, h:h + 1], scalar2=None, op0=ALU.mult))

        def prep_w(dram, rows, cols, dst, dst_key, col_of, sign=False):
            n_ot = rows // 128
            n_kt = cols // 128
            for ot in range(n_ot):
                dma("pool", wstg[:, 0:cols], dram[ot * 128:(ot + 1) * 128, :],
                    [], ["wstg"])
                if sign:
                    p.op("act", ["wstg"], ["wstg"],
                         emit=lambda e, cols=cols: e.sign(out=wstg[:, 0:cols],
                                                          in_=wstg[:, 0:cols]))
                for g in range(0, n_kt, 8):
                    hi = min(g + 8, n_kt)
                    bi = (ot * 2 + g // 8) % 2
                    bank, bkey = ptr[bi], PT[bi]
                    for kt in range(g, hi):
                        p.op("pe", ["wstg", "ident"], [bkey],
                             emit=lambda e, kt=kt, g=g, bank=bank: e.transpose(
                                 bank[:, (kt - g) * 128:(kt - g + 1) * 128],
                                 wstg[:, kt * 128:(kt + 1) * 128], ident[:]))
                    for kt in range(g, hi):
                        col = col_of(kt, ot)
                        p.op("dve", [bkey], [dst_key],
                             emit=lambda e, kt=kt, g=g, col=col, bank=bank:
                             e.tensor_copy(out=dst[:, col:col + 128],
                                           in_=bank[:, (kt - g) * 128:(kt - g + 1) * 128]))

        prep_w(dWf, D, D, wS, "wS", lambda kt, ot: kt * D + ot * 128, sign=True)
        prep_w(dWx, INNER, D, slotA, "slotA", lambda kt, ot: kt * INNER + ot * 128)
        prep_w(dWz, INNER, D, slotB, "slotB", lambda kt, ot: kt * INNER + ot * 128)

        def layernorm(src, src_key, dst, dst_key, par=0, stage=0):
            o = par * NT_A
            o2 = par * 2 * NT_A
            kc = [f"c1_{par}", f"c2_{par}", f"cm_{par}", f"cv_{par}",
                  f"cnm_{par}", f"crv_{par}"]
            if stage != 2:
                for tt in range(NT_A):
                    sl = src[:, tt * D:(tt + 1) * D]
                    p.op("dve", [src_key], [kc[0]],
                         emit=lambda e, sl=sl, tt=tt: e.reduce_sum(
                             out=c1[:, o + tt:o + tt + 1], in_=sl, axis=AX.X))
                    for hf in range(2):
                        p.op("act", [src_key], [kc[1], "dumpA"],
                             emit=lambda e, src=src, tt=tt, hf=hf: e.activation(
                                 out=dumpA[:], in_=src[:, tt * D + hf * 512:
                                                       tt * D + (hf + 1) * 512],
                                 func=AF.Square,
                                 accum_out=c2[:, o2 + hf * NT_A + tt:
                                              o2 + hf * NT_A + tt + 1]))
            if stage == 1:
                return
            p.op("dve", [kc[0]], [kc[2]],
                 emit=lambda e: e.tensor_scalar(out=cm[:, o:o + NT_A],
                                                in0=c1[:, o:o + NT_A],
                                                scalar1=1.0 / D, scalar2=None,
                                                op0=ALU.mult))
            p.op("dve", [kc[1]], [kc[3]],
                 emit=lambda e: e.tensor_add(out=cv[:, o:o + NT_A],
                                             in0=c2[:, o2:o2 + NT_A],
                                             in1=c2[:, o2 + NT_A:o2 + 2 * NT_A]))
            p.op("dve", [kc[3]], [kc[3]],
                 emit=lambda e: e.tensor_scalar(out=cv[:, o:o + NT_A],
                                                in0=cv[:, o:o + NT_A],
                                                scalar1=1.0 / D, scalar2=None,
                                                op0=ALU.mult))
            p.op("dve", [kc[2]], [kc[4]],
                 emit=lambda e: e.tensor_mul(out=cnm[:, o:o + NT_A],
                                             in0=cm[:, o:o + NT_A],
                                             in1=cm[:, o:o + NT_A]))
            p.op("dve", [kc[3], kc[4]], [kc[3]],
                 emit=lambda e: e.tensor_sub(out=cv[:, o:o + NT_A],
                                             in0=cv[:, o:o + NT_A],
                                             in1=cnm[:, o:o + NT_A]))
            p.op("act", [kc[3], "cst"], [kc[3]],
                 emit=lambda e: e.activation(out=cv[:, o:o + NT_A],
                                             in_=cv[:, o:o + NT_A], func=AF.Ln,
                                             bias=cst[:, 0:1]))
            p.op("act", [kc[3], "cst"], [kc[5]],
                 emit=lambda e: e.activation(out=crv[:, o:o + NT_A],
                                             in_=cv[:, o:o + NT_A], func=AF.Exp,
                                             scale=cst[:, 1:2]))
            p.op("dve", [kc[2], kc[5]], [kc[4]],
                 emit=lambda e: e.tensor_mul(out=cnm[:, o:o + NT_A],
                                             in0=cm[:, o:o + NT_A],
                                             in1=crv[:, o:o + NT_A]))
            p.op("dve", [kc[4]], [kc[4]],
                 emit=lambda e: e.tensor_scalar(out=cnm[:, o:o + NT_A],
                                                in0=cnm[:, o:o + NT_A],
                                                scalar1=-1.0, scalar2=None,
                                                op0=ALU.mult))
            for tt in range(NT_A):
                p.op("act", [src_key, kc[5], kc[4]], [dst_key],
                     emit=lambda e, tt=tt: e.activation(
                         out=dst[:, tt * D:(tt + 1) * D],
                         in_=src[:, tt * D:(tt + 1) * D], func=AF.Identity,
                         bias=cnm[:, o + tt:o + tt + 1],
                         scale=crv[:, o + tt:o + tt + 1]))

        def transpose_u(src, src_key, dst, dst_key, chunk):
            nt = chunk // 128
            for tt in range(nt):
                bi = tt % 2
                bank, bkey = ptr[bi], PT[bi]
                for kt in range(KD):
                    p.op("pe", [src_key, "ident"], [bkey],
                         emit=lambda e, tt=tt, kt=kt, bank=bank: e.transpose(
                             bank[:, kt * 128:(kt + 1) * 128],
                             src[:, tt * D + kt * 128:tt * D + (kt + 1) * 128],
                             ident[:]))
                for kt in range(KD):
                    p.op("dve", [bkey], [dst_key],
                         emit=lambda e, tt=tt, kt=kt, bank=bank, chunk=chunk:
                         e.tensor_copy(
                             out=dst[:, kt * chunk + tt * 128:kt * chunk + (tt + 1) * 128],
                             in_=bank[:, kt * 128:(kt + 1) * 128]))

        def p1a1_ln1(b, c, stage=0):
            T0 = c * CH_A
            par = c % 2
            v_x0, v_utm = v_x0p[par], v_utmp[par]
            kx0, kut = f"x0_{par}", f"utm_{par}"
            if stage != 2:
                x_view = bass.AP(dx.tensor, (b * L + T0) * D,
                                 [[D, 128], [128 * D, NT_A], [1, D]])
                dma("pool", v_x0.rearrange("p (a b) -> p a b", a=NT_A), x_view,
                    [], [kx0])
            layernorm(v_x0, kx0, v_utm, kut, par, stage=stage)

        def p1a1_t1f1(b, c):
            T0 = c * CH_A
            par = c % 2
            v_x0, v_x1 = v_x0p[par], v_x1p[par]
            v_utm, v_u1f = v_utmp[par], v_u1fp[par]
            kx0, kx1 = f"x0_{par}", f"x1b_{par}"
            kut, ku1 = f"utm_{par}", f"u1f_{par}"
            transpose_u(v_utm, kut, v_u1f, ku1, CH_A)
            for tt in range(NT_A):
                for nh in range(2):
                    bi = 4 + (tt * 2 + nh) % 2
                    bank, bkey = psum[bi], PS[bi]

                    def f1(e, tt=tt, nh=nh, bank=bank, v_u1f=v_u1f):
                        last = None
                        for k in range(KD):
                            last = e.matmul(
                                bank[:],
                                v_u1f[:, k * CH_A + tt * 128:k * CH_A + (tt + 1) * 128],
                                wS[:, k * D + nh * 512:k * D + (nh + 1) * 512],
                                start=(k == 0), stop=(k == KD - 1))
                        return last
                    p.op("pe", [ku1, "wS"], [bkey], emit=f1)
                    p.op("dve", [bkey, kx0], [kx1],
                         emit=lambda e, tt=tt, nh=nh, bank=bank, v_x1=v_x1,
                         v_x0=v_x0:
                         e.scalar_tensor_tensor(
                             out=v_x1[:, tt * D + nh * 512:tt * D + (nh + 1) * 512],
                             in0=bank[:], scalar=0.5,
                             in1=v_x0[:, tt * D + nh * 512:tt * D + (nh + 1) * 512],
                             op0=ALU.mult, op1=ALU.add))
            x1_view = bass.AP(sp_x1.tensor, (b * L + T0) * D,
                              [[D, 128], [128 * D, NT_A], [1, D]])
            dma("sp", x1_view, v_x1.rearrange("p (a b) -> p a b", a=NT_A),
                [kx1], ["sp_x1"])

        def p1a1_ln2(b, c):
            par = c % 2
            layernorm(v_x1p[par], f"x1b_{par}", v_utmp[par], f"utm_{par}", par)

        def p1a1_t2(b, c):
            par = c % 2
            transpose_u(v_utmp[par], f"utm_{par}", v_ufmp[c % 4],
                        f"ufm_{c % 4}", CH_A)

        def u2_rhs(k, pp):
            t, off = u2_src[pp]
            return bass.AP(t, off + k * CH_A,
                           [[8192, 128], [2048, 2], [1, CH_A]])

        for b in range(BPC if _PH >= 2 else 0):
            p.op("dve", [], ["halo"], emit=lambda e: e.memset(halo[:], 0.0))

            def wxwz(cp, ot_lo, ot_hi, b=b):
                T0 = cp * CH_B
                pp = cp % 2
                ku2a, ku2b = f"ufm_{2 * pp}", f"ufm_{2 * pp + 1}"
                for ot in range(ot_lo, ot_hi):
                    bi = ot % 4
                    bank, bkey = psum[bi], PS[bi]

                    def wxmm(e, ot=ot, bank=bank, pp=pp):
                        last = None
                        for k in range(KD):
                            last = e.matmul(
                                bank[:],
                                slotA[:, k * INNER + ot * 128:k * INNER + (ot + 1) * 128],
                                u2_rhs(k, pp), start=(k == 0), stop=(k == KD - 1))
                        return last
                    p.op("pe", [ku2a, ku2b, "slotA"], [bkey], emit=wxmm)
                    xpc = xp if ot % 2 == 0 else xpB
                    kxp = "xp" if ot % 2 == 0 else "xpB"
                    p.op("act", [bkey], [kxp],
                         emit=lambda e, bank=bank, xpc=xpc: e.activation(
                             out=xpc[:, 3:3 + CH_B], in_=bank[:], func=AF.Copy))
                    p.op("dve", ["halo", kxp], [kxp],
                         emit=lambda e, ot=ot, xpc=xpc: e.tensor_copy(
                             out=xpc[:, 0:3], in_=halo[:, ot * 3:ot * 3 + 3]))
                    p.op("dve", [kxp], ["halo"],
                         emit=lambda e, ot=ot, xpc=xpc: e.tensor_copy(
                             out=halo[:, ot * 3:ot * 3 + 3],
                             in_=xpc[:, CH_B:CH_B + 3]))
                    acc0 = acc_a if ot % 2 == 0 else acc_b
                    ka0 = "acc_a" if ot % 2 == 0 else "acc_b"
                    p.op("dve", [kxp, "cwc", "cbc"], [ka0],
                         emit=lambda e, ot=ot, xpc=xpc, acc0=acc0: e.tensor_scalar(
                             out=acc0[:], in0=xpc[:, 0:CH_B],
                             scalar1=cwc[:, ot * 4:ot * 4 + 1],
                             scalar2=cbc[:, ot:ot + 1], op0=ALU.mult, op1=ALU.add))
                    p.op("dve", [kxp, "cwc", ka0], [ka0],
                         emit=lambda e, ot=ot, xpc=xpc, acc0=acc0:
                         e.scalar_tensor_tensor(
                             out=acc0[:], in0=xpc[:, 1:1 + CH_B],
                             scalar=cwc[:, ot * 4 + 1:ot * 4 + 2], in1=acc0[:],
                             op0=ALU.mult, op1=ALU.add))
                    p.op("dve", [kxp, "cwc", ka0], [ka0],
                         emit=lambda e, ot=ot, xpc=xpc, acc0=acc0:
                         e.scalar_tensor_tensor(
                             out=acc0[:], in0=xpc[:, 2:2 + CH_B],
                             scalar=cwc[:, ot * 4 + 2:ot * 4 + 3], in1=acc0[:],
                             op0=ALU.mult, op1=ALU.add))
                    p.op("dve", [kxp, "cwc", ka0], [ka0],
                         emit=lambda e, ot=ot, xpc=xpc, acc0=acc0:
                         e.scalar_tensor_tensor(
                             out=acc0[:], in0=xpc[:, 3:3 + CH_B],
                             scalar=cwc[:, ot * 4 + 3:ot * 4 + 4], in1=acc0[:],
                             op0=ALU.mult, op1=ALU.add))
                    p.op("act", [ka0], ["xcb1"],
                         emit=lambda e, ot=ot, acc0=acc0: e.activation(
                             out=v_xcb1[:, (ot % 4) * CH_B:(ot % 4 + 1) * CH_B],
                             in_=acc0[:], func=AF.Silu))

                    def wzmm(e, ot=ot, bank=bank, pp=pp):
                        last = None
                        for k in range(KD):
                            last = e.matmul(
                                bank[:],
                                slotB[:, k * INNER + ot * 128:k * INNER + (ot + 1) * 128],
                                u2_rhs(k, pp), start=(k == 0), stop=(k == KD - 1))
                        return last
                    p.op("pe", [ku2a, ku2b, "slotB"], [bkey], emit=wzmm)
                    p.op("act", [bkey], ["zb1"],
                         emit=lambda e, ot=ot, bank=bank: e.activation(
                             out=v_zb1[:, (ot % 4) * CH_B:(ot % 4 + 1) * CH_B],
                             in_=bank[:], func=AF.Copy))
                    if ot % 4 == 3:
                        q = ot // 4
                        xc_view = bass.AP(sp_xc.tensor,
                                          b * INNER * L + q * 4 * 128 * L + T0,
                                          [[L, 128], [128 * L, 4], [1, CH_B]])
                        dma("sp", xc_view,
                            bass.AP(v_xcb1.tensor, 0,
                                    [[8192, 128], [CH_B, 4], [1, CH_B]]),
                            ["xcb1"], ["sp_xc"])
                        z_view = bass.AP(sp_z.tensor,
                                         b * INNER * L + q * 4 * 128 * L + T0,
                                         [[L, 128], [128 * L, 4], [1, CH_B]])
                        dma("pool", z_view,
                            bass.AP(v_zb1.tensor, 2048,
                                    [[8192, 128], [CH_B, 4], [1, CH_B]]),
                            ["zb1"], ["sp_z"])

            p1a1_ln1(b, 0)
            p1a1_ln1(b, 1)
            p1a1_t1f1(b, 0)
            p1a1_t1f1(b, 1)
            p1a1_ln2(b, 0)
            p1a1_ln2(b, 1)
            p1a1_t2(b, 0)
            p1a1_t2(b, 1)
            if NCH_B > 1:
                p1a1_ln1(b, 2)
                p1a1_ln1(b, 3)
            for cp in range(NCH_B):
                if cp + 1 < NCH_B:
                    p1a1_t1f1(b, 2 * cp + 2)
                    p1a1_t1f1(b, 2 * cp + 3)
                    p1a1_ln2(b, 2 * cp + 2)
                    p1a1_ln2(b, 2 * cp + 3)
                    p1a1_t2(b, 2 * cp + 2)
                    if cp + 2 < NCH_B:
                        p1a1_ln1(b, 2 * cp + 4, stage=1)
                    if _PH >= 3:
                        wxwz(cp, 0, 2)
                    p1a1_t2(b, 2 * cp + 3)
                    if cp + 2 < NCH_B:
                        p1a1_ln1(b, 2 * cp + 4, stage=2)
                        p1a1_ln1(b, 2 * cp + 5, stage=1)
                    if _PH >= 3:
                        wxwz(cp, 2, 4)
                    if cp + 2 < NCH_B:
                        p1a1_ln1(b, 2 * cp + 5, stage=2)
                    if _PH >= 3:
                        wxwz(cp, 4, KI)
                elif _PH >= 3:
                    wxwz(cp, 0, KI)

        prep_w(dWb, PO, INNER, slotA, "slotA", lambda kt, ot: kt * PK + ot * 128)
        prep_w(dWc, PO, INNER, slotA, "slotA",
               lambda kt, ot: kt * PK + PO + ot * 128)
        dma("pool", wstg[0:8, 0:INNER], dWdt, [], ["wstg"])
        for kt in range(KI):
            bi = kt % 2
            bank, bkey = ptr[bi], PT[bi]
            p.op("pe", ["wstg", "ident"], [bkey],
                 emit=lambda e, kt=kt, bank=bank: e.transpose(
                     bank[:, 0:8], wstg[0:8, kt * 128:(kt + 1) * 128],
                     ident[0:8, 0:8]))
            p.op("dve", [bkey], ["slotA"],
                 emit=lambda e, kt=kt, bank=bank: e.tensor_copy(
                     out=slotA[:, kt * PK + 2 * PO:kt * PK + 2 * PO + 8],
                     in_=bank[:, 0:8]))

        p.fence(ALL_POOL_KEYS)
        p.op("dve", [], ["ctil"], emit=lambda e: e.memset(v_ctil, 0.0))
        p.op("dve", [], ["cth"], emit=lambda e: e.memset(v_cth, 0.0))
        p.op("dve", [], ["btmA"], emit=lambda e: e.memset(v_btmA, 0.0))
        p.op("dve", [], ["btmB"], emit=lambda e: e.memset(v_btmB, 0.0))
        for b in range(BPC if _PH >= 4 else 0):
            p.op("dve", [], ["S_a"], emit=lambda e: e.memset(S_a[:], 0.0))
            p.op("dve", [], ["S_bf"], emit=lambda e: e.memset(S_bf[:], 0.0))
            S_cur, S_nxt = S_a, S_b
            S_keys = ["S_a", "S_b"]
            for c in range(NCH_B):
                T0 = c * CH_B
                xcs_t = [G2, G1][c % 2]
                v_xcs_c = xcs_t[:, 0:8192]
                kxcs = f"xcs{c % 2}"
                for q in range(4):
                    xc_view = bass.AP(sp_xc.tensor,
                                      b * INNER * L + q * 4 * 128 * L + T0,
                                      [[L, 128], [128 * L, 4], [1, CH_B]])
                    dma("sp",
                        bass.AP(xcs_t, q * 4 * CH_B,
                                [[8192, 128], [CH_B, 4], [1, CH_B]]),
                        xc_view, ["sp_xc"], [kxcs])
                if _SB < 2:
                    continue
                def dtmm(e, v_xcs_c=v_xcs_c):
                    last = None
                    for k in range(KI):
                        last = e.matmul(
                            psum[4][0:8, :],
                            slotA[:, k * PK + 2 * PO:k * PK + 2 * PO + 8],
                            v_xcs_c[:, k * CH_B:(k + 1) * CH_B],
                            start=(k == 0), stop=(k == KI - 1))
                    return last
                p.op("pe", [kxcs, "slotA"], [PS[4]], emit=dtmm)
                if _SB < 3:
                    continue
                pr = tin[0:8, 0:CH_B]
                t1 = dtv[0:8, :]
                t2 = tin[0:8, CH_B:2 * CH_B]
                p.op("dve", [PS[4], "bdtc"], ["tin"],
                     emit=lambda e: e.tensor_scalar(out=pr, in0=psum[4][0:8, :],
                                                    scalar1=bdtc[:], scalar2=None,
                                                    op0=ALU.add))
                p.op("dve", ["tin"], ["dtv"],
                     emit=lambda e: e.tensor_scalar(out=t1, in0=pr, scalar1=-1.0,
                                                    scalar2=None, op0=ALU.mult))
                p.op("dve", ["tin", "dtv"], ["tin"],
                     emit=lambda e: e.tensor_tensor(out=t2, in0=pr, in1=t1,
                                                    op=ALU.min))
                p.op("act", ["tin"], ["tin"],
                     emit=lambda e: e.activation(out=t2, in_=t2, func=AF.Exp))
                p.op("act", ["tin"], ["tin"],
                     emit=lambda e: e.activation(out=t2, in_=t2, func=AF.Ln,
                                                 bias=1.0))
                p.op("dve", ["tin"], ["dtv"],
                     emit=lambda e: e.tensor_scalar(out=t1, in0=pr, scalar1=0.0,
                                                    scalar2=None, op0=ALU.max))
                p.op("dve", ["tin", "dtv"], ["dtv"],
                     emit=lambda e: e.tensor_tensor(out=dtv[:], in0=t1, in1=t2,
                                                    op=ALU.add))
                p.op("dve", ["dtv", "acol"], ["scbf"],
                     emit=lambda e: e.tensor_scalar(out=scbf[:], in0=dtv[:],
                                                    scalar1=acol[:], scalar2=None,
                                                    op0=ALU.mult))
                p.op("dve", ["scbf", "seg"], ["s_sb"],
                     emit=lambda e: e.tensor_tensor_scan(
                         out=s_sb[0:8, :], data0=segc[:], data1=scbf[:], initial=0.0,
                         op0=ALU.mult, op1=ALU.add))
                p.op("act", ["dtv"], ["tin"],
                     emit=lambda e: e.activation(out=t2, in_=dtv[:], func=AF.Ln))
                p.op("dve", ["tin", "s_sb"], ["nslf"],
                     emit=lambda e: e.tensor_sub(out=nslf[:], in0=t2,
                                                 in1=s_sb[0:8, :]))
                p.op("act", ["s_sb"], ["esr"],
                     emit=lambda e: e.activation(out=esr[:], in_=s_sb[0:8, :],
                                                 func=AF.Exp))
                for su in range(NSUB):
                    p.op("act", ["s_sb", "nslf"], ["scbf"],
                         emit=lambda e, su=su: e.activation(
                             out=scbf[:, su * SUB:(su + 1) * SUB],
                             in_=nslf[:, su * SUB:(su + 1) * SUB], func=AF.Exp,
                             bias=s_sb[0:8, (su + 1) * SUB - 1:(su + 1) * SUB]))
                if _SB < 4:
                    continue
                for su in range(NSUB):
                    bi = su % 2
                    p.op("pe", ["nslf", "identf"], [PS[5]],
                         emit=lambda e, su=su: e.transpose(
                             psum[5][:, 0:8], nslf[:, su * SUB:(su + 1) * SUB],
                             identf[0:8, 0:8]))
                    p.op("dve", [PS[5]], ["colT"],
                         emit=lambda e, su=su: e.tensor_copy(
                             out=colT[:, su * 16:su * 16 + 8], in_=psum[5][:, 0:8]))
                    p.op("pe", ["scbf", "identf"], [PS[5]],
                         emit=lambda e, su=su: e.transpose(
                             psum[5][:, 0:8], scbf[:, su * SUB:(su + 1) * SUB],
                             identf[0:8, 0:8]))
                    p.op("dve", [PS[5]], ["colT"],
                         emit=lambda e, su=su: e.tensor_copy(
                             out=colT[:, su * 16 + 8:su * 16 + 16],
                             in_=psum[5][:, 0:8]))
                for su in range(NSUB):
                    def lam(e, su=su):
                        last = None
                        for g in range(4):
                            last = e.matmul(
                                psum[5][:, su * 4 + g:su * 4 + g + 1],
                                selc[:, g * 128:(g + 1) * 128],
                                esr[:, (su + 1) * SUB - 1:(su + 1) * SUB],
                                start=True, stop=True)
                        return last
                    p.op("pe", ["esr", "sel"], [PS[5]], emit=lam)
                    p.op("dve", [PS[5]], ["lamc"],
                         emit=lambda e, su=su: e.tensor_copy(
                             out=lamc[:, su * 4:su * 4 + 4],
                             in_=psum[5][:, su * 4:su * 4 + 4]))
                if _SB < 5:
                    continue
                for m in range(4):
                    bi = m % 4
                    bank, bkey = psum[bi], PS[bi]

                    def bmm(e, m=m, bank=bank, v_xcs_c=v_xcs_c):
                        last = None
                        for k in range(KI):
                            last = e.matmul(
                                bank[:],
                                slotA[:, k * PK + m * 128:k * PK + (m + 1) * 128],
                                v_xcs_c[:, k * CH_B:(k + 1) * CH_B],
                                start=(k == 0), stop=(k == KI - 1))
                        return last
                    p.op("pe", [kxcs, "slotA"], [bkey], emit=bmm)
                    p.op("dve", [bkey], ["bfm"],
                         emit=lambda e, m=m, bank=bank: e.tensor_copy(
                             out=v_bfm[:, m * CH_B:(m + 1) * CH_B], in_=bank[:]))
                for m in range(4):
                    bi = m % 4
                    bank, bkey = psum[bi], PS[bi]

                    def cmm(e, m=m, bank=bank, v_xcs_c=v_xcs_c):
                        last = None
                        for k in range(KI):
                            last = e.matmul(
                                bank[:],
                                slotA[:, k * PK + PO + m * 128:k * PK + PO + (m + 1) * 128],
                                v_xcs_c[:, k * CH_B:(k + 1) * CH_B],
                                start=(k == 0), stop=(k == KI - 1))
                        return last
                    p.op("pe", [kxcs, "slotA"], [bkey], emit=cmm)

                    def esmm(e, m=m):
                        return e.matmul(psum[4][:], selc[:, m * 128:(m + 1) * 128],
                                        esr[:], start=True, stop=True)
                    p.op("pe", ["esr", "sel"], [PS[4]], emit=esmm)
                    p.op("dve", [PS[4]], ["esbuf"],
                         emit=lambda e: e.tensor_copy(out=esb[:], in_=psum[4][:]))
                    p.op("dve", [bkey, "esbuf"], ["ctil"],
                         emit=lambda e, m=m, bank=bank: e.tensor_tensor(
                             out=v_ctil[0:64, m * CH_B:(m + 1) * CH_B],
                             in0=bank[0:64, :], in1=esb[0:64, :], op=ALU.mult))
                    p.op("dve", [bkey, "esbuf"], ["cth"],
                         emit=lambda e, m=m, bank=bank: e.tensor_tensor(
                             out=v_cth[64:128, m * CH_B:(m + 1) * CH_B],
                             in0=bank[64:128, :], in1=esb[64:128, :], op=ALU.mult))

                if _SB < 6:
                    continue
                for su in range(NSUB):
                    t0 = su * SUB
                    xtm = v_xtmA if su % 2 == 0 else v_xtmB
                    xkey = "xtmA" if su % 2 == 0 else "xtmB"
                    btm = v_btmA if su % 2 == 0 else v_btmB
                    btkey = "btmA" if su % 2 == 0 else "btmB"
                    xv = bass.AP(sp_xc.tensor, b * INNER * L + T0 + t0,
                                 [[L, INNER], [1, 128]])
                    p.op("sp", ["sp_xc"], [xkey],
                         emit=lambda e, xv=xv, xtm=xtm: e.dma_start(
                             out=xtm[:], in_=xv, transpose=True), dma=True)
                    for h in range(8):
                        bi = h % 2
                        bank, bkey = ptr[bi], PT[bi]
                        p.op("pe", ["bfm", "ident"], [bkey],
                             emit=lambda e, h=h, bank=bank, t0=t0: e.transpose(
                                 bank[:, 0:64],
                                 v_bfm[(h % 2) * 64:(h % 2) * 64 + 64,
                                       (h // 2) * CH_B + t0:(h // 2) * CH_B + t0 + 128],
                                 ident[(h % 2) * 64:(h % 2) * 64 + 64,
                                       (h % 2) * 64:(h % 2) * 64 + 64]))
                        bcol = (h // 2) * 256 + (h % 2) * 192
                        p.op("dve", [bkey, "colT"], [btkey],
                             emit=lambda e, h=h, su=su, bank=bank, btm=btm, bcol=bcol:
                             e.tensor_scalar(
                                 out=btm[:, bcol:bcol + 64],
                                 in0=bank[:, 0:64],
                                 scalar1=colT[:, su * 16 + 8 + h:su * 16 + 9 + h],
                                 scalar2=None, op0=ALU.mult))
                    if _SB < 7:
                        continue
                    for hb in range(2):
                        mb, mkey = psum[2 + hb], PS[2 + hb]
                        gb, gkey = psum[4 + hb], PS[4 + hb]

                        def maskmm(e, hb=hb, mb=mb, t0=t0):
                            e.matmul(mb[:], ident[:], negr[:],
                                     start=True, stop=False)
                            last = None
                            for hq in range(4):
                                h = hb * 4 + hq
                                sl = mb[:, hq * 128:(hq + 1) * 128]
                                last = e.matmul(sl, selh[:, h * 128:(h + 1) * 128],
                                                s_sb[0:8, t0:t0 + 128],
                                                start=False, stop=(hq == 3))
                            return last
                        if _SM >= 1:
                            p.op("pe", ["s_sb", "selh", "ident", "negr"], [mkey],
                                 emit=maskmm)
                        for hq in range(4 if _SM >= 2 else 0):
                            h = hb * 4 + hq
                            p.op("act", [mkey, "colT"], ["eb"],
                                 emit=lambda e, h=h, hq=hq, su=su, mb=mb:
                                 e.activation(
                                     out=v_eb[:, h * 128:(h + 1) * 128],
                                     in_=mb[:, hq * 128:(hq + 1) * 128],
                                     func=AF.Exp,
                                     bias=colT[:, su * 16 + h:su * 16 + h + 1]))

                        def gmm(e, hb=hb, gb=gb, t0=t0):
                            last = None
                            for hq in range(4):
                                h = hb * 4 + hq
                                cpad = v_ctil if h % 2 == 0 else v_cth
                                last = e.matmul(
                                    gb[:, hq * 128:(hq + 1) * 128],
                                    v_bfm[:, (h // 2) * CH_B + t0:(h // 2) * CH_B + t0 + 128],
                                    cpad[:, (h // 2) * CH_B + t0:(h // 2) * CH_B + t0 + 128],
                                    start=True, stop=True)
                            return last
                        if _SM >= 3:
                            p.op("pe", ["bfm", "ctil", "cth"], [gkey], emit=gmm)
                        if _SM >= 4:
                            p.op("dve", [gkey, "eb"], ["mg"],
                                 emit=lambda e, gb=gb, hb=hb: e.tensor_tensor(
                                     out=v_mg[:, hb * 512:(hb + 1) * 512], in0=gb[:],
                                     in1=v_eb[:, hb * 512:(hb + 1) * 512], op=ALU.mult))
                            p.op("dve", ["mg", "dic"], ["mg"],
                                 emit=lambda e, hb=hb: e.tensor_tensor(
                                     out=v_mg[:, hb * 512:(hb + 1) * 512],
                                     in0=v_mg[:, hb * 512:(hb + 1) * 512],
                                     in1=dic[:, hb * 512:(hb + 1) * 512], op=ALU.add))
                    if _SB < 8:
                        continue
                    for j in range(4):
                        yb, ykey = psum[j], PS[j]

                        def ymm(e, j=j, t0=t0, xtm=xtm):
                            last = None
                            for q in range(4):
                                ct = 4 * j + q
                                h = ct // 2
                                sl = psum[j][:, q * 128:(q + 1) * 128]
                                e.matmul(sl,
                                         xtm[:, ct * 128:(ct + 1) * 128],
                                         v_mg[:, h * 128:(h + 1) * 128],
                                         start=True, stop=False)
                                cpad = v_ctil if h % 2 == 0 else v_cth
                                last = e.matmul(
                                    sl,
                                    S_bf[:, (h // 2) * HD + (ct % 2) * 128:
                                         (h // 2) * HD + (ct % 2) * 128 + 128],
                                    cpad[:, (h // 2) * CH_B + t0:(h // 2) * CH_B + t0 + 128],
                                    start=False, stop=True)
                            return last
                        p.op("pe", [xkey, "mg", "dic", "S_bf", "ctil", "cth"], [ykey],
                             emit=ymm)
                        for half in range(2):
                            col = c * NSUB * 8 + su * 8 + j * 2 + half
                            p.op("dve", [ykey], ["ys", "yt"],
                                 emit=lambda e, j=j, su=su, half=half, col=col:
                                 e.tensor_scalar(
                                     out=bass.AP(v_yt.tensor,
                                                 (4 * j + 2 * half) * CH_B + su * 128,
                                                 [[8192, 128], [CH_B, 2], [1, 128]]),
                                     in0=psum[j][:, half * 256:(half + 1) * 256],
                                     scalar1=1.0, scalar2=0.0, op0=ALU.mult,
                                     op1=ALU.add,
                                     accum_out=ys[:, col:col + 1]))
                            p.op("act", [ykey], ["yss", "dumpA"],
                                 emit=lambda e, j=j, half=half, col=col:
                                 e.activation(
                                     out=dumpA[:, 0:256],
                                     in_=psum[j][:, half * 256:(half + 1) * 256],
                                     func=AF.Square, accum_out=yss[:, col:col + 1]))
                    if _SB < 9:
                        continue
                    for g in range(4):
                        bi = 4 + (g % 2)
                        db, dkey = psum[bi], PS[bi]

                        def dsmm(e, g=g, db=db, xtm=xtm, btm=btm):
                            e.matmul(db[:, 0:HD],
                                     btm[:, g * 256:g * 256 + 128],
                                     xtm[:, (2 * g) * HD:(2 * g + 1) * HD],
                                     start=True, stop=False)
                            return e.matmul(
                                db[:, 0:HD],
                                btm[:, g * 256 + 128:g * 256 + 256],
                                xtm[:, (2 * g + 1) * HD:(2 * g + 2) * HD],
                                start=False, stop=True)
                        p.op("pe", [btkey, xkey], [dkey], emit=dsmm)
                        p.op("dve", [dkey, S_keys[0], "lamc"], [S_keys[1]],
                             emit=lambda e, g=g, su=su, db=db, Sc=S_cur, Sn=S_nxt:
                             e.scalar_tensor_tensor(
                                 out=Sn[:, g * HD:(g + 1) * HD],
                                 in0=Sc[:, g * HD:(g + 1) * HD],
                                 scalar=lamc[:, su * 4 + g:su * 4 + g + 1],
                                 in1=db[:, 0:HD], op0=ALU.mult, op1=ALU.add))
                        p.op("dve", [S_keys[1]], ["S_bf"],
                             emit=lambda e, g=g, Sn=S_nxt: e.tensor_copy(
                                 out=S_bf[:, g * HD:(g + 1) * HD],
                                 in_=Sn[:, g * HD:(g + 1) * HD]))
                    S_cur, S_nxt = S_nxt, S_cur
                    S_keys = S_keys[::-1]
                for q in range(4):
                    y_view = bass.AP(sp_y.tensor, b * INNER * L + q * 4 * 128 * L + T0,
                                     [[L, 128], [128 * L, 4], [1, CH_B]])
                    dma("pool", y_view,
                        bass.AP(v_yt.tensor, q * 4 * CH_B,
                                [[8192, 128], [CH_B, 4], [1, CH_B]]),
                        ["yt"], ["sp_y"])
            def gnred(e):
                e.matmul(psum[0][0:1, 0:NCH_B * NSUB * 8], onesb[:, 0:1], ys[:],
                         start=True, stop=True)
                return e.matmul(psum[1][0:1, 0:NCH_B * NSUB * 8], onesb[:, 0:1],
                                yss[:], start=True, stop=True)
            p.op("pe", ["ys", "yss", "onesb"], [PS[0], PS[1]], emit=gnred)
            p.op("dve", [PS[0]], ["stat8"],
                 emit=lambda e: e.reduce_sum(
                     out=stat8[0:1, 0:8],
                     in_=psum[0][0:1, 0:NCH_B * NSUB * 8].rearrange(
                         "p (a b) -> p b a", b=8), axis=AX.X))
            p.op("dve", [PS[1]], ["stat8"],
                 emit=lambda e: e.reduce_sum(
                     out=stat8[0:1, 8:16],
                     in_=psum[1][0:1, 0:NCH_B * NSUB * 8].rearrange(
                         "p (a b) -> p b a", b=8), axis=AX.X))
            NGN = float(L * HD)
            p.op("dve", ["stat8"], ["stat8"],
                 emit=lambda e: e.tensor_scalar(out=stat8[0:1, 16:24],
                                                in0=stat8[0:1, 0:8],
                                                scalar1=1.0 / NGN, scalar2=None,
                                                op0=ALU.mult))
            p.op("dve", ["stat8"], ["stat8"],
                 emit=lambda e: e.tensor_scalar(out=stat8[0:1, 24:32],
                                                in0=stat8[0:1, 8:16],
                                                scalar1=1.0 / NGN, scalar2=None,
                                                op0=ALU.mult))
            p.op("dve", ["stat8"], ["stat8"],
                 emit=lambda e: e.tensor_mul(out=stat8[0:1, 32:40],
                                             in0=stat8[0:1, 16:24],
                                             in1=stat8[0:1, 16:24]))
            p.op("dve", ["stat8"], ["stat8"],
                 emit=lambda e: e.tensor_sub(out=stat8[0:1, 40:48],
                                             in0=stat8[0:1, 24:32],
                                             in1=stat8[0:1, 32:40]))
            p.op("act", ["stat8", "cst"], ["stat8"],
                 emit=lambda e: e.activation(out=stat8[0:1, 40:48],
                                             in_=stat8[0:1, 40:48], func=AF.Ln,
                                             bias=cst[0:1, 0:1]))
            p.op("act", ["stat8", "cst"], ["stat8"],
                 emit=lambda e: e.activation(out=stat8[0:1, 48:56],
                                             in_=stat8[0:1, 40:48], func=AF.Exp,
                                             scale=cst[0:1, 1:2]))
            def bc16(e):
                e.matmul(psum[2][:, 0:8], onesc[0:1, 0:128], stat8[0:1, 48:56],
                         start=True, stop=True)
                return e.matmul(psum[2][:, 8:16], onesc[0:1, 0:128],
                                stat8[0:1, 16:24], start=True, stop=True)
            p.op("pe", ["stat8", "ones"], [PS[2]], emit=bc16)
            p.op("dve", [PS[2]], ["rinv8"],
                 emit=lambda e, b=b: e.tensor_copy(out=rinv8[:, b * 16:b * 16 + 16],
                                                   in_=psum[2][:, 0:16]))
            p.op("dve", ["rinv8", "gnwc"], ["gnwp"],
                 emit=lambda e, b=b: e.tensor_tensor(
                     out=gnwp[:, b * KI:(b + 1) * KI], in0=gnwc[:],
                     in1=bass.AP(rinv8, b * 16,
                                 [[BPC * 16, 128], [1, 8], [0, 2]]),
                     op=ALU.mult))
            p.op("dve", ["gnwp", "rinv8"], ["gnbp"],
                 emit=lambda e, b=b: e.tensor_tensor(
                     out=gnbp[:, b * KI:(b + 1) * KI],
                     in0=gnwp[:, b * KI:(b + 1) * KI],
                     in1=bass.AP(rinv8, b * 16 + 8,
                                 [[BPC * 16, 128], [1, 8], [0, 2]]),
                     op=ALU.mult))
            p.op("dve", ["gnbp", "gnbc"], ["gnbp"],
                 emit=lambda e, b=b: e.tensor_sub(
                     out=gnbp[:, b * KI:(b + 1) * KI], in0=gnbc[:],
                     in1=gnbp[:, b * KI:(b + 1) * KI]))

        prep_w(dWout, D, INNER, slotB, "slotB", lambda kt, ot: kt * D + ot * 128)

        p.fence(ALL_POOL_KEYS)

        def p2_loads(b, c):
            T0 = c * CH_A
            par = c % 2
            v_yr, v_zr = v_yrp[par], v_zrp[par]
            kyr, kzr = f"yr_{par}", f"zr_{par}"
            for q in range(4):
                y_view = bass.AP(sp_y.tensor, b * INNER * L + q * 4 * 128 * L + T0,
                                 [[L, 128], [128 * L, 4], [1, CH_A]])
                dma("sp",
                    bass.AP(v_yr.tensor, par * 4096 + q * 4 * CH_A,
                            [[8192, 128], [CH_A, 4], [1, CH_A]]),
                    y_view, ["sp_y"], [kyr])
                z_view = bass.AP(sp_z.tensor, b * INNER * L + q * 4 * 128 * L + T0,
                                 [[L, 128], [128 * L, 4], [1, CH_A]])
                dma("pool",
                    bass.AP(v_zr.tensor, par * 4096 + q * 4 * CH_A,
                            [[8192, 128], [CH_A, 4], [1, CH_A]]),
                    z_view, ["sp_z"], [kzr])

        def p2_gy(b, c):
            par = c % 2
            v_yr, v_zr, v_gy = v_yrp[par], v_zrp[par], v_gyp[par]
            kyr, kzr, kgy = f"yr_{par}", f"zr_{par}", f"gy_{par}"
            for og in range(KI // 2):
                ot = 2 * og
                acc0 = acc_a if og % 2 == 0 else acc_b
                ka = f"acca{og % 2}"
                p.op("dve", [kyr, "gnwp", "gnbp"], [ka],
                     emit=lambda e, ot=ot, b=b, acc0=acc0, v_yr=v_yr:
                     e.tensor_scalar(
                         out=acc0[:, 0:CH_A],
                         in0=v_yr[:, ot * CH_A:(ot + 1) * CH_A],
                         scalar1=gnwp[:, b * KI + ot:b * KI + ot + 1],
                         scalar2=gnbp[:, b * KI + ot:b * KI + ot + 1],
                         op0=ALU.mult, op1=ALU.add))
                p.op("dve", [kyr, "gnwp", "gnbp", ka], [ka],
                     emit=lambda e, ot=ot, b=b, acc0=acc0, v_yr=v_yr:
                     e.tensor_scalar(
                         out=acc0[:, CH_A:2 * CH_A],
                         in0=v_yr[:, (ot + 1) * CH_A:(ot + 2) * CH_A],
                         scalar1=gnwp[:, b * KI + ot + 1:b * KI + ot + 2],
                         scalar2=gnbp[:, b * KI + ot + 1:b * KI + ot + 2],
                         op0=ALU.mult, op1=ALU.add))
                p.op("act", [kzr], [kgy],
                     emit=lambda e, ot=ot, v_gy=v_gy, v_zr=v_zr: e.activation(
                         out=v_gy[:, ot * CH_A:(ot + 2) * CH_A],
                         in_=v_zr[:, ot * CH_A:(ot + 2) * CH_A], func=AF.Silu))
                p.op("dve", [ka, kgy], [kgy],
                     emit=lambda e, ot=ot, acc0=acc0, v_gy=v_gy:
                     e.tensor_tensor(
                         out=v_gy[:, ot * CH_A:(ot + 2) * CH_A],
                         in0=acc0[:, 0:2 * CH_A],
                         in1=v_gy[:, ot * CH_A:(ot + 2) * CH_A],
                         op=ALU.mult))

        def p2_main_a(b, c):
            T0 = c * CH_A
            par = c % 2
            v_x2 = v_x2p[par]
            v_gy = v_gyp[par]
            kx2, kgy = f"x2_{par}", f"gy_{par}"
            x1_view = bass.AP(sp_x1.tensor, (b * L + T0) * D,
                              [[D, 128], [128 * D, NT_A], [1, D]])
            dma("sp", v_x1r.rearrange("p (a b) -> p a b", a=NT_A), x1_view,
                ["sp_x1"], ["x1r"])
            for tt in range(NT_A):
                for nh in range(2):
                    bi = (tt * 2 + nh) % 4
                    bank, bkey = psum[bi], PS[bi]

                    def womm(e, tt=tt, nh=nh, bank=bank, v_gy=v_gy):
                        last = None
                        for k in range(KI):
                            last = e.matmul(
                                bank[:],
                                v_gy[:, k * CH_A + tt * 128:k * CH_A + (tt + 1) * 128],
                                slotB[:, k * D + nh * 512:k * D + (nh + 1) * 512],
                                start=(k == 0), stop=(k == KI - 1))
                        return last
                    p.op("pe", [kgy, "slotB"], [bkey], emit=womm)
                    p.op("dve", [bkey, "x1r"], [kx2],
                         emit=lambda e, tt=tt, nh=nh, bank=bank, v_x2=v_x2:
                         e.tensor_tensor(
                             out=v_x2[:, tt * D + nh * 512:tt * D + (nh + 1) * 512],
                             in0=bank[:],
                             in1=v_x1r[:, tt * D + nh * 512:tt * D + (nh + 1) * 512],
                             op=ALU.add))

        def p2_ln(b, c):
            par = c % 2
            layernorm(v_x2p[par], f"x2_{par}", v_utm2, "utm2", par)

        def p2_tr(b, c):
            par = c % 2
            transpose_u(v_utm2, "utm2", v_ufm2p[par], f"ufm2_{par}", CH_A)

        def p2_f2(b, c):
            T0 = c * CH_A
            par = c % 2
            v_x2 = v_x2p[par]
            v_ufm2 = v_ufm2p[par]
            kx2, kuf2 = f"x2_{par}", f"ufm2_{par}"
            for tt in range(NT_A):
                for nh in range(2):
                    bi = 4 + (tt * 2 + nh) % 2
                    bank, bkey = psum[bi], PS[bi]

                    def f2(e, tt=tt, nh=nh, bank=bank, v_ufm2=v_ufm2):
                        last = None
                        for k in range(KD):
                            last = e.matmul(
                                bank[:],
                                v_ufm2[:, k * CH_A + tt * 128:k * CH_A + (tt + 1) * 128],
                                wS[:, k * D + nh * 512:k * D + (nh + 1) * 512],
                                start=(k == 0), stop=(k == KD - 1))
                        return last
                    p.op("pe", [kuf2, "wS"], [bkey], emit=f2)
                    p.op("dve", [bkey, kx2], ["x3"],
                         emit=lambda e, tt=tt, nh=nh, bank=bank, v_x2=v_x2:
                         e.scalar_tensor_tensor(
                             out=v_x3[:, nh * 512:(nh + 1) * 512],
                             in0=bank[:], scalar=0.5,
                             in1=v_x2[:, tt * D + nh * 512:tt * D + (nh + 1) * 512],
                             op0=ALU.mult, op1=ALU.add))
                out_view = bass.AP(dout.tensor, (b * L + T0 + tt * 128) * D,
                                   [[D, 128], [1, D]])
                dma("sp", out_view, v_x3[:], ["x3"], ["dout"])

        if _PH >= 5:
            chunks = [(b, c) for b in range(BPC) for c in range(NCH_A)]
            p2_loads(*chunks[0])
            p2_gy(*chunks[0])
            p2_loads(*chunks[1])
            p2_gy(*chunks[1])
            p2_main_a(*chunks[0])
            for i, (b, c) in enumerate(chunks):
                p2_ln(b, c)
                if i + 2 < len(chunks):
                    p2_loads(*chunks[i + 2])
                if i + 1 < len(chunks):
                    p2_main_a(*chunks[i + 1])
                p2_tr(b, c)
                if i + 2 < len(chunks):
                    p2_gy(*chunks[i + 2])
                p2_f2(b, c)

        p.op("pool", ["dout"], [], emit=lambda e: e.engine_nop())

        eng_map = {"pe": block.tensor, "dve": block.vector, "act": block.scalar,
                   "pool": block.gpsimd, "sp": block.sync}
        for eng in Prog.ENGS:
            ops = p.ops[eng]
            if not ops:
                continue

            def section(e, _ops=ops, _sem=sems[eng]):
                for waits, emit, slotkey in _ops:
                    for src, v in waits:
                        e.wait_ge(sems[src], v)
                    inst = emit(e)
                    if slotkey is None:
                        inst.then_inc(_sem, 1)
                    else:
                        inst.then_inc(sems[slotkey], 16)
            eng_map[eng](section)

    return nc


def _consts():
    ident = np.eye(128, dtype=BF)
    identf = np.eye(8, dtype=np.float32)
    negr = np.tile(np.where(np.arange(128)[:, None] > np.arange(128)[None, :],
                            -1e30, 0.0), (1, 4)).astype(BF)
    ones_c = np.ones((1, 128), np.float32)
    ones_b = np.ones((128, 1), np.float32)
    seg = np.ones((HEADS, CH_B), np.float32)
    seg[:, ::SUB] = 0.0
    sel = np.zeros((HEADS, 4 * 128), BF)
    for g in range(4):
        for pp in range(128):
            sel[2 * g + pp // 64, g * 128 + pp] = 1.0
    selh = np.zeros((HEADS, HEADS * 128), np.float32)
    for h in range(HEADS):
        selh[h, h * 128:(h + 1) * 128] = 1.0
    return dict(ident=ident, identf=identf, negr=negr,
                ones_c=ones_c, ones_b=ones_b, seg=seg, sel=sel, selh=selh)


def kernel(L0_attn_input_tensor, L0_attn_Wx, L0_attn_Wz, L0_attn_conv_w,
           L0_attn_conv_b, L0_attn_Wb, L0_attn_Wc, L0_attn_Wdt, L0_attn_b_dt,
           L0_attn_log_a, L0_attn_d_param, L0_attn_gn_w, L0_attn_gn_b,
           L0_attn_Wout, L0_ffn_weight):
    x = np.ascontiguousarray(L0_attn_input_tensor)
    consts = _consts()
    shared = dict(
        Wx=np.ascontiguousarray(L0_attn_Wx),
        Wz=np.ascontiguousarray(L0_attn_Wz),
        conv_w=np.ascontiguousarray(
            L0_attn_conv_w[:, 0, :].reshape(KI, 128, KCONV).transpose(1, 0, 2)
            .reshape(128, KI * KCONV)),
        conv_b=np.ascontiguousarray(L0_attn_conv_b.reshape(KI, 128).T),
        Wb=np.ascontiguousarray(L0_attn_Wb),
        Wc=np.ascontiguousarray(L0_attn_Wc),
        Wdt=np.ascontiguousarray(L0_attn_Wdt),
        b_dt=np.ascontiguousarray(L0_attn_b_dt.reshape(HEADS, 1)),
        log_a=np.ascontiguousarray(L0_attn_log_a.reshape(HEADS, 1)),
        d_param=np.ascontiguousarray(L0_attn_d_param.reshape(HEADS, 1)),
        gn_w=np.ascontiguousarray(L0_attn_gn_w.reshape(KI, 128).T),
        gn_b=np.ascontiguousarray(L0_attn_gn_b.reshape(KI, 128).T),
        Wout=np.ascontiguousarray(L0_attn_Wout),
        Wf=np.ascontiguousarray(L0_ffn_weight),
        **consts,
    )
    nc = build_program()
    in_maps = []
    for i in range(NCORES):
        m = dict(shared)
        m["x"] = np.ascontiguousarray(x[i * BPC:(i + 1) * BPC])
        in_maps.append(m)
    res = run_bass_kernel_spmd(nc, in_maps, core_ids=list(range(NCORES)))
    out = np.concatenate([r["out"] for r in res.results], axis=0)
    return out.astype(np.float32)


# version 25 (final)
# speedup vs baseline: 1.1372x; 1.0011x over previous
import contextlib
import os as _os
import numpy as np
import ml_dtypes
import concourse.bass as bass
import concourse.mybir as mybir
from concourse.bass_utils import run_bass_kernel_spmd

F32 = mybir.dt.float32
BF16 = mybir.dt.bfloat16
AF = mybir.ActivationFunctionType
ALU = mybir.AluOpType
AX = mybir.AxisListType
BF = ml_dtypes.bfloat16

NCORES = 8
BPC = 2
L = 4096
D = 1024
INNER = 2048
HEADS = 8
STATE = 64
HD = 256
KCONV = 4
EPS = 1e-5

CH_A = 256
CH_B = 512
SUB = 128
KD = D // 128
KI = INNER // 128
PO = STATE * HEADS
PK = 2 * PO + HEADS

NT_A = CH_A // 128
NCH_A = L // CH_A
NT_B = CH_B // 128
NSUB = CH_B // SUB
NCH_B = L // CH_B


class Prog:
    ENGS = ("pe", "dve", "act", "pool", "sp")
    NSLOT = 8

    def __init__(self):
        self.ops = {e: [] for e in self.ENGS}
        self.cnt = {e: 0 for e in self.ENGS}
        self.dma_slot_cnt = {}
        self.dma_n = {"sp": 0, "pool": 0}
        self.hw = {}
        self.last_w = {}
        self.readers = {}

    def op(self, eng, reads=(), writes=(), emit=None, dma=False):
        needs = {}

        def need(src, v):
            if v > needs.get(src, -1):
                needs[src] = v

        reads = list(reads)
        writes = list(writes)
        excl = [k for k in reads if k.startswith("pb") or k.startswith("pt")]
        for k in excl:
            reads.remove(k)
            writes.append(k)
        for k in reads:
            for src, v in self.last_w.get(k, {}).items():
                need(src, v)
        for k in writes:
            if not dma:
                for src, v in self.last_w.get(k, {}).items():
                    need(src, v)
            else:
                for src, v in self.last_w.get(k, {}).items():
                    if src in self.ENGS:
                        need(src, v)
            for r_src, r_cnt in self.readers.get(k, {}).items():
                need(r_src, r_cnt)

        if dma:
            slot = self.dma_n[eng] % self.NSLOT
            self.dma_n[eng] += 1
            srckey = f"{eng}{slot}"
            prev = self.dma_slot_cnt.get(srckey, 0)
            if prev > 0:
                need(srckey, prev)
            self.dma_slot_cnt[srckey] = prev + 16
            after_src = srckey
            after_val = prev + 16
        else:
            srckey = eng
            self.cnt[eng] += 1
            after_src = eng
            after_val = self.cnt[eng]
            if after_val > 255 and self.hw.get((eng, eng), 0) < after_val - 255:
                need(eng, after_val - 200)

        waits = []
        for src, v in sorted(needs.items()):
            cur = self.hw.get((eng, src), 0)
            if v > cur:
                waits.append((src, v))
                self.hw[(eng, src)] = v
        self.ops[eng].append((waits, emit, after_src if dma else None))
        for k in reads:
            self.readers.setdefault(k, {})[after_src] = after_val
        for k in writes:
            if dma:
                d = self.last_w.setdefault(k, {})
                for src in [s for s in d if s in self.ENGS]:
                    del d[src]
                d[after_src] = max(d.get(after_src, 0), after_val)
            else:
                self.last_w[k] = {after_src: after_val}
                self.readers[k] = {}
        return after_val

    def fence(self, keys):
        self.op("dve", reads=list(keys), writes=list(keys),
                emit=lambda e: e.engine_nop())


def build_program():
    _PH = int(_os.environ.get("KBISECT", "5"))
    _SB = int(_os.environ.get("KBISECT2", "9"))
    _SM = int(_os.environ.get("KBISECT3", "9"))
    nc = bass.Bass("TRN2", target_bir_lowering=False)

    dx = nc.dram_tensor("x", [BPC, L, D], F32, kind="ExternalInput").ap()
    dWx = nc.dram_tensor("Wx", [INNER, D], F32, kind="ExternalInput").ap()
    dWz = nc.dram_tensor("Wz", [INNER, D], F32, kind="ExternalInput").ap()
    dcw = nc.dram_tensor("conv_w", [128, KI * KCONV], F32, kind="ExternalInput").ap()
    dcb = nc.dram_tensor("conv_b", [128, KI], F32, kind="ExternalInput").ap()
    dWb = nc.dram_tensor("Wb", [PO, INNER], F32, kind="ExternalInput").ap()
    dWc = nc.dram_tensor("Wc", [PO, INNER], F32, kind="ExternalInput").ap()
    dWdt = nc.dram_tensor("Wdt", [HEADS, INNER], F32, kind="ExternalInput").ap()
    dbdt = nc.dram_tensor("b_dt", [HEADS, 1], F32, kind="ExternalInput").ap()
    dloga = nc.dram_tensor("log_a", [HEADS, 1], F32, kind="ExternalInput").ap()
    ddpar = nc.dram_tensor("d_param", [HEADS, 1], F32, kind="ExternalInput").ap()
    dgnw = nc.dram_tensor("gn_w", [128, KI], F32, kind="ExternalInput").ap()
    dgnb = nc.dram_tensor("gn_b", [128, KI], F32, kind="ExternalInput").ap()
    dWout = nc.dram_tensor("Wout", [D, INNER], F32, kind="ExternalInput").ap()
    dWf = nc.dram_tensor("Wf", [D, D], F32, kind="ExternalInput").ap()
    dident = nc.dram_tensor("ident", [128, 128], BF16, kind="ExternalInput").ap()
    didentf = nc.dram_tensor("identf", [8, 8], F32, kind="ExternalInput").ap()
    dnegr = nc.dram_tensor("negr", [128, 512], BF16, kind="ExternalInput").ap()
    dones = nc.dram_tensor("ones_c", [1, 128], F32, kind="ExternalInput").ap()
    donesb = nc.dram_tensor("ones_b", [128, 1], F32, kind="ExternalInput").ap()
    dseg = nc.dram_tensor("seg", [HEADS, CH_B], F32, kind="ExternalInput").ap()
    dsel = nc.dram_tensor("sel", [HEADS, 4 * 128], BF16, kind="ExternalInput").ap()
    dselh = nc.dram_tensor("selh", [HEADS, HEADS * 128], F32, kind="ExternalInput").ap()

    dout = nc.dram_tensor("out", [BPC, L, D], F32, kind="ExternalOutput").ap()
    sp_x1 = nc.dram_tensor("sp_x1", [BPC * L, D], BF16, kind="Internal").ap()
    sp_xc = nc.dram_tensor("sp_xc", [BPC, INNER, L], BF16, kind="Internal").ap()
    sp_z = nc.dram_tensor("sp_z", [BPC, INNER, L], BF16, kind="Internal").ap()
    sp_y = nc.dram_tensor("sp_y", [BPC, INNER, L], BF16, kind="Internal").ap()

    p = Prog()
    ctx = contextlib.ExitStack()
    with ctx:
        ST = lambda name, shape, dt_: ctx.enter_context(nc.sbuf_tensor(name, shape, dt_))

        wS = ST("wS", [128, KD * D], BF16)
        slotA = ST("slotA", [128, KI * PK], BF16)
        slotB = ST("slotB", [128, KD * INNER], BF16)
        G1 = ST("G1", [128, 8192], BF16)
        G2 = ST("G2", [128, 8192], BF16)
        G3 = ST("G3", [128, 8192], BF16)
        G4 = ST("G4", [128, 8192], BF16)
        G5 = ST("G5", [128, 6144], BF16)
        xp = ST("xp", [128, 515], BF16)
        xpB = ST("xpB", [128, 515], BF16)
        halo = ST("halo", [128, KI * 3], BF16)
        acc_a = ST("acc_a", [128, CH_B], F32)
        acc_b = ST("acc_b", [128, CH_B], F32)
        s_sb = ST("s_sb", [8, CH_B], F32)
        nslf = ST("nslf", [8, CH_B], F32)
        dtv = ST("dtv", [8, CH_B], F32)
        scbf = ST("scbf", [8, CH_B], F32)
        esr = ST("esr", [8, CH_B], BF16)
        tin = ST("tin", [8, 2 * CH_B], F32)
        colT = ST("colT", [128, NSUB * 16], F32)
        lamc = ST("lamc", [128, NSUB * 4], F32)
        S_a = ST("S_a", [128, 4 * HD], F32)
        S_b = ST("S_b", [128, 4 * HD], F32)
        S_bf = ST("S_bf", [128, 4 * HD], BF16)
        ys = ST("ys", [128, NCH_B * NSUB * 8], F32)
        yss = ST("yss", [128, NCH_B * NSUB * 8], F32)
        c1 = ST("c1", [128, 2 * NT_A], F32)
        c2 = ST("c2", [128, 4 * NT_A], F32)
        cm = ST("cm", [128, 2 * NT_A], F32)
        cv = ST("cv", [128, 2 * NT_A], F32)
        crv = ST("crv", [128, 2 * NT_A], F32)
        cnm = ST("cnm", [128, 2 * NT_A], F32)
        dumpA = ST("dumpA", [128, 512], BF16)
        esb = ST("esbuf", [128, CH_B], BF16)
        gnwp = ST("gnwp", [128, BPC * KI], F32)
        gnbp = ST("gnbp", [128, BPC * KI], F32)
        stat8 = ST("stat8", [1, 56], F32)
        rinv8 = ST("rinv8", [128, BPC * 16], F32)
        wstg = ST("wstg", [128, 2048], BF16)
        ident = ST("identc", [128, 128], BF16)
        identf = ST("identfc", [8, 8], F32)
        negr = ST("negrc", [128, 512], BF16)
        onesc = ST("onescc", [1, 128], F32)
        onesb = ST("onesbc", [128, 1], F32)
        segc = ST("segcc", [8, CH_B], F32)
        selc = ST("selcc", [8, 4 * 128], BF16)
        selh = ST("selhc", [8, 8 * 128], F32)
        cwc = ST("cwcc", [128, KI * KCONV], F32)
        cbc = ST("cbcc", [128, KI], F32)
        bdtc = ST("bdtcc", [8, 1], F32)
        acol = ST("acolc", [8, 1], F32)
        dcol = ST("dcolc", [128, 8], F32)
        dic = ST("dicc", [128, 8 * 128], BF16)
        gnwc = ST("gnwcc", [128, KI], F32)
        gnbc = ST("gnbcc", [128, KI], F32)
        cst = ST("cstc", [128, 2], F32)

        psum = [ctx.enter_context(nc.psum_tensor(f"pb{i}", [128, 512], F32))
                for i in range(6)]
        ptr = [ctx.enter_context(nc.psum_tensor(f"pt{i}", [128, 1024], BF16))
               for i in range(2)]
        PS = [f"pb{i}" for i in range(6)]
        PT = [f"pt{i}" for i in range(2)]

        sems = {e: ctx.enter_context(nc.semaphore(f"sm_{e}")) for e in Prog.ENGS}
        for q in ("sp", "pool"):
            for s in range(Prog.NSLOT):
                sems[f"{q}{s}"] = ctx.enter_context(nc.semaphore(f"sm_{q}{s}"))
        block = ctx.enter_context(nc.Block())

        def dma(queue, out, in_, reads, writes):
            p.op(queue, reads=reads, writes=writes,
                 emit=lambda e, o=out, i=in_: e.dma_start(out=o, in_=i), dma=True)

        bfv = lambda t, c0, c1_: t[:, c0:c1_]
        f32v = lambda t, c0, c1_: t[:].bitcast(F32)[:, c0:c1_]
        v_x0p = [bfv(G5, 0, 2048), bfv(G4, 0, 2048)]
        v_x1p = [bfv(G5, 2048, 4096), bfv(G4, 2048, 4096)]
        v_utmp = [bfv(G1, 0, 2048), bfv(G1, 2048, 4096)]
        v_ufmp = [bfv(G1, 4096, 6144), bfv(G1, 6144, 8192),
                  bfv(G3, 4096, 6144), bfv(G3, 6144, 8192)]
        u2_src = [(G1, 4096), (G3, 4096)]
        v_u1fp = [bfv(G3, 0, 2048), bfv(G3, 2048, 4096)]
        v_u2 = bfv(G1, 0, KD * CH_B)
        v_xcb1 = bfv(G2, 0, 2048)
        v_zb1 = bfv(G2, 2048, 4096)
        v_xcr = bfv(G1, 0, 8192)
        v_xcs = bfv(G2, 0, 8192)
        v_yt = bfv(G3, 0, 8192)
        v_xtmA = bfv(G4, 0, 2048)
        v_xtmB = bfv(G4, 2048, 4096)
        v_bfm = bfv(G4, 4096, 6144)
        v_ctil = bfv(G4, 6144, 8192)
        v_btmA = bfv(G5, 0, 1024)
        v_btmB = bfv(G5, 1024, 2048)
        v_mg = bfv(G5, 2048, 3072)
        v_eb = bfv(G5, 3072, 4096)
        v_cth = bfv(G5, 4096, 6144)
        v_yrp = [bfv(G1, 0, 4096), bfv(G1, 4096, 8192)]
        v_zrp = [bfv(G2, 0, 4096), bfv(G2, 4096, 8192)]
        v_gyp = [bfv(G3, 0, 4096), bfv(G3, 4096, 8192)]
        v_x1r = bfv(G4, 0, 2048)
        v_x2p = [bfv(G4, 2048, 4096), bfv(G5, 0, 2048)]
        v_utm2 = bfv(G4, 4096, 6144)
        v_ufm2p = [bfv(G4, 6144, 8192), bfv(G5, 2048, 4096)]
        v_x3 = f32v(G5, 2048, 3072)

        ALL_POOL_KEYS = ["x0_0", "x0_1", "x1b_0", "x1b_1", "utm_0", "utm_1",
                         "ufm_0", "ufm_1", "ufm_2", "ufm_3", "u1f_0", "u1f_1", "x2_0", "x2_1", "ufm2_0", "ufm2_1",
                         "u2", "xcb1", "zb1",
                         "xcr", "xcs", "xcs0", "xcs1", "yt", "xtmA", "xtmB", "bfm", "ctil",
                         "btmA", "btmB", "mg", "eb", "cth", "yr", "zr", "gy",
                         "x1r", "utm2", "x3", "yr_0", "yr_1", "zr_0", "zr_1", "gy_0", "gy_1"]

        p.op("dve", [], ["cst"], emit=lambda e: e.memset(cst[:, 0:1], EPS))
        p.op("dve", [], ["cst"], emit=lambda e: e.memset(cst[:, 1:2], -0.5))
        dma("sp", ident[:], dident, [], ["ident"])
        dma("sp", identf[:], didentf, [], ["identf"])
        dma("sp", negr[:], dnegr, [], ["negr"])
        dma("sp", onesc[:], dones, [], ["ones"])
        dma("sp", onesb[:], donesb, [], ["onesb"])
        dma("sp", segc[:], dseg, [], ["seg"])
        dma("sp", selc[:], dsel, [], ["sel"])
        dma("sp", selh[:], dselh, [], ["selh"])
        dma("sp", cwc[:], dcw, [], ["cwc"])
        dma("sp", cbc[:], dcb, [], ["cbc"])
        dma("sp", bdtc[:], dbdt, [], ["bdtc"])
        dma("sp", gnwc[:], dgnw, [], ["gnwc"])
        dma("sp", gnbc[:], dgnb, [], ["gnbc"])
        dma("sp", acol[:], dloga, [], ["acol"])
        dma("sp", dcol[0:8, 0:1], ddpar, [], ["dcol"])

        p.op("act", ["acol"], ["acol"],
             emit=lambda e: e.activation(out=acol[:], in_=acol[:], func=AF.Exp))
        p.op("dve", ["acol"], ["acol"],
             emit=lambda e: e.tensor_scalar(out=acol[:], in0=acol[:], scalar1=-1.0,
                                            scalar2=None, op0=ALU.mult))
        p.op("pe", ["dcol", "identf"], [PS[0]],
             emit=lambda e: e.transpose(psum[0][0:1, 0:8], dcol[0:8, 0:1],
                                        identf[0:8, 0:8]))
        p.op("dve", [PS[0]], ["dcol"],
             emit=lambda e: e.tensor_copy(out=dcol[0:1, 0:8], in_=psum[0][0:1, 0:8]))
        p.op("pe", ["dcol", "ones"], [PS[1]],
             emit=lambda e: e.matmul(psum[1][:, 0:8], onesc[0:1, 0:128],
                                     dcol[0:1, 0:8], start=True, stop=True))
        p.op("dve", [PS[1]], ["dcol"],
             emit=lambda e: e.tensor_copy(out=dcol[:, 0:8], in_=psum[1][:, 0:8]))
        for h in range(8):
            p.op("dve", ["dcol", "ident"], ["dic"],
                 emit=lambda e, h=h: e.tensor_scalar(
                     out=dic[:, h * 128:(h + 1) * 128], in0=ident[:],
                     scalar1=dcol[:, h:h + 1], scalar2=None, op0=ALU.mult))

        def prep_w(dram, rows, cols, dst, dst_key, col_of, sign=False):
            n_ot = rows // 128
            n_kt = cols // 128
            for ot in range(n_ot):
                dma("pool", wstg[:, 0:cols], dram[ot * 128:(ot + 1) * 128, :],
                    [], ["wstg"])
                if sign:
                    p.op("act", ["wstg"], ["wstg"],
                         emit=lambda e, cols=cols: e.sign(out=wstg[:, 0:cols],
                                                          in_=wstg[:, 0:cols]))
                for g in range(0, n_kt, 8):
                    hi = min(g + 8, n_kt)
                    bi = (ot * 2 + g // 8) % 2
                    bank, bkey = ptr[bi], PT[bi]
                    for kt in range(g, hi):
                        p.op("pe", ["wstg", "ident"], [bkey],
                             emit=lambda e, kt=kt, g=g, bank=bank: e.transpose(
                                 bank[:, (kt - g) * 128:(kt - g + 1) * 128],
                                 wstg[:, kt * 128:(kt + 1) * 128], ident[:]))
                    for kt in range(g, hi):
                        col = col_of(kt, ot)
                        p.op("dve", [bkey], [dst_key],
                             emit=lambda e, kt=kt, g=g, col=col, bank=bank:
                             e.tensor_copy(out=dst[:, col:col + 128],
                                           in_=bank[:, (kt - g) * 128:(kt - g + 1) * 128]))

        prep_w(dWf, D, D, wS, "wS", lambda kt, ot: kt * D + ot * 128, sign=True)
        prep_w(dWx, INNER, D, slotA, "slotA", lambda kt, ot: kt * INNER + ot * 128)
        prep_w(dWz, INNER, D, slotB, "slotB", lambda kt, ot: kt * INNER + ot * 128)

        def layernorm(src, src_key, dst, dst_key, par=0, stage=0):
            o = par * NT_A
            o2 = par * 2 * NT_A
            kc = [f"c1_{par}", f"c2_{par}", f"cm_{par}", f"cv_{par}",
                  f"cnm_{par}", f"crv_{par}"]
            if stage != 2:
                for tt in range(NT_A):
                    sl = src[:, tt * D:(tt + 1) * D]
                    p.op("dve", [src_key], [kc[0]],
                         emit=lambda e, sl=sl, tt=tt: e.reduce_sum(
                             out=c1[:, o + tt:o + tt + 1], in_=sl, axis=AX.X))
                    for hf in range(2):
                        p.op("act", [src_key], [kc[1], "dumpA"],
                             emit=lambda e, src=src, tt=tt, hf=hf: e.activation(
                                 out=dumpA[:], in_=src[:, tt * D + hf * 512:
                                                       tt * D + (hf + 1) * 512],
                                 func=AF.Square,
                                 accum_out=c2[:, o2 + hf * NT_A + tt:
                                              o2 + hf * NT_A + tt + 1]))
            if stage == 1:
                return
            p.op("dve", [kc[0]], [kc[2]],
                 emit=lambda e: e.tensor_scalar(out=cm[:, o:o + NT_A],
                                                in0=c1[:, o:o + NT_A],
                                                scalar1=1.0 / D, scalar2=None,
                                                op0=ALU.mult))
            p.op("dve", [kc[1]], [kc[3]],
                 emit=lambda e: e.tensor_add(out=cv[:, o:o + NT_A],
                                             in0=c2[:, o2:o2 + NT_A],
                                             in1=c2[:, o2 + NT_A:o2 + 2 * NT_A]))
            p.op("dve", [kc[3]], [kc[3]],
                 emit=lambda e: e.tensor_scalar(out=cv[:, o:o + NT_A],
                                                in0=cv[:, o:o + NT_A],
                                                scalar1=1.0 / D, scalar2=None,
                                                op0=ALU.mult))
            p.op("dve", [kc[2]], [kc[4]],
                 emit=lambda e: e.tensor_mul(out=cnm[:, o:o + NT_A],
                                             in0=cm[:, o:o + NT_A],
                                             in1=cm[:, o:o + NT_A]))
            p.op("dve", [kc[3], kc[4]], [kc[3]],
                 emit=lambda e: e.tensor_sub(out=cv[:, o:o + NT_A],
                                             in0=cv[:, o:o + NT_A],
                                             in1=cnm[:, o:o + NT_A]))
            p.op("act", [kc[3], "cst"], [kc[3]],
                 emit=lambda e: e.activation(out=cv[:, o:o + NT_A],
                                             in_=cv[:, o:o + NT_A], func=AF.Ln,
                                             bias=cst[:, 0:1]))
            p.op("act", [kc[3], "cst"], [kc[5]],
                 emit=lambda e: e.activation(out=crv[:, o:o + NT_A],
                                             in_=cv[:, o:o + NT_A], func=AF.Exp,
                                             scale=cst[:, 1:2]))
            p.op("dve", [kc[2], kc[5]], [kc[4]],
                 emit=lambda e: e.tensor_mul(out=cnm[:, o:o + NT_A],
                                             in0=cm[:, o:o + NT_A],
                                             in1=crv[:, o:o + NT_A]))
            p.op("dve", [kc[4]], [kc[4]],
                 emit=lambda e: e.tensor_scalar(out=cnm[:, o:o + NT_A],
                                                in0=cnm[:, o:o + NT_A],
                                                scalar1=-1.0, scalar2=None,
                                                op0=ALU.mult))
            for tt in range(NT_A):
                p.op("act", [src_key, kc[5], kc[4]], [dst_key],
                     emit=lambda e, tt=tt: e.activation(
                         out=dst[:, tt * D:(tt + 1) * D],
                         in_=src[:, tt * D:(tt + 1) * D], func=AF.Identity,
                         bias=cnm[:, o + tt:o + tt + 1],
                         scale=crv[:, o + tt:o + tt + 1]))

        def transpose_u(src, src_key, dst, dst_key, chunk):
            nt = chunk // 128
            for tt in range(nt):
                bi = tt % 2
                bank, bkey = ptr[bi], PT[bi]
                for kt in range(KD):
                    p.op("pe", [src_key, "ident"], [bkey],
                         emit=lambda e, tt=tt, kt=kt, bank=bank: e.transpose(
                             bank[:, kt * 128:(kt + 1) * 128],
                             src[:, tt * D + kt * 128:tt * D + (kt + 1) * 128],
                             ident[:]))
                for kt in range(KD):
                    p.op("dve", [bkey], [dst_key],
                         emit=lambda e, tt=tt, kt=kt, bank=bank, chunk=chunk:
                         e.tensor_copy(
                             out=dst[:, kt * chunk + tt * 128:kt * chunk + (tt + 1) * 128],
                             in_=bank[:, kt * 128:(kt + 1) * 128]))

        def p1a1_ln1(b, c, stage=0):
            T0 = c * CH_A
            par = c % 2
            v_x0, v_utm = v_x0p[par], v_utmp[par]
            kx0, kut = f"x0_{par}", f"utm_{par}"
            if stage != 2:
                x_view = bass.AP(dx.tensor, (b * L + T0) * D,
                                 [[D, 128], [128 * D, NT_A], [1, D]])
                dma("pool", v_x0.rearrange("p (a b) -> p a b", a=NT_A), x_view,
                    [], [kx0])
            layernorm(v_x0, kx0, v_utm, kut, par, stage=stage)

        def p1a1_t1f1(b, c):
            T0 = c * CH_A
            par = c % 2
            v_x0, v_x1 = v_x0p[par], v_x1p[par]
            v_utm, v_u1f = v_utmp[par], v_u1fp[par]
            kx0, kx1 = f"x0_{par}", f"x1b_{par}"
            kut, ku1 = f"utm_{par}", f"u1f_{par}"
            transpose_u(v_utm, kut, v_u1f, ku1, CH_A)
            for tt in range(NT_A):
                for nh in range(2):
                    bi = 4 + (tt * 2 + nh) % 2
                    bank, bkey = psum[bi], PS[bi]

                    def f1(e, tt=tt, nh=nh, bank=bank, v_u1f=v_u1f):
                        last = None
                        for k in range(KD):
                            last = e.matmul(
                                bank[:],
                                v_u1f[:, k * CH_A + tt * 128:k * CH_A + (tt + 1) * 128],
                                wS[:, k * D + nh * 512:k * D + (nh + 1) * 512],
                                start=(k == 0), stop=(k == KD - 1))
                        return last
                    p.op("pe", [ku1, "wS"], [bkey], emit=f1)
                    p.op("dve", [bkey, kx0], [kx1],
                         emit=lambda e, tt=tt, nh=nh, bank=bank, v_x1=v_x1,
                         v_x0=v_x0:
                         e.scalar_tensor_tensor(
                             out=v_x1[:, tt * D + nh * 512:tt * D + (nh + 1) * 512],
                             in0=bank[:], scalar=0.5,
                             in1=v_x0[:, tt * D + nh * 512:tt * D + (nh + 1) * 512],
                             op0=ALU.mult, op1=ALU.add))
            x1_view = bass.AP(sp_x1.tensor, (b * L + T0) * D,
                              [[D, 128], [128 * D, NT_A], [1, D]])
            dma("sp", x1_view, v_x1.rearrange("p (a b) -> p a b", a=NT_A),
                [kx1], ["sp_x1"])

        def p1a1_ln2(b, c):
            par = c % 2
            layernorm(v_x1p[par], f"x1b_{par}", v_utmp[par], f"utm_{par}", par)

        def p1a1_t2(b, c):
            par = c % 2
            transpose_u(v_utmp[par], f"utm_{par}", v_ufmp[c % 4],
                        f"ufm_{c % 4}", CH_A)

        def u2_rhs(k, pp):
            t, off = u2_src[pp]
            return bass.AP(t, off + k * CH_A,
                           [[8192, 128], [2048, 2], [1, CH_A]])

        for b in range(BPC if _PH >= 2 else 0):
            p.op("dve", [], ["halo"], emit=lambda e: e.memset(halo[:], 0.0))

            def wxwz(cp, ot_lo, ot_hi, b=b):
                T0 = cp * CH_B
                pp = cp % 2
                ku2a, ku2b = f"ufm_{2 * pp}", f"ufm_{2 * pp + 1}"
                for ot in range(ot_lo, ot_hi):
                    bi = ot % 4
                    bank, bkey = psum[bi], PS[bi]

                    def wxmm(e, ot=ot, bank=bank, pp=pp):
                        last = None
                        for k in range(KD):
                            last = e.matmul(
                                bank[:],
                                slotA[:, k * INNER + ot * 128:k * INNER + (ot + 1) * 128],
                                u2_rhs(k, pp), start=(k == 0), stop=(k == KD - 1))
                        return last
                    p.op("pe", [ku2a, ku2b, "slotA"], [bkey], emit=wxmm)
                    xpc = xp if ot % 2 == 0 else xpB
                    kxp = "xp" if ot % 2 == 0 else "xpB"
                    p.op("act", [bkey], [kxp],
                         emit=lambda e, bank=bank, xpc=xpc: e.activation(
                             out=xpc[:, 3:3 + CH_B], in_=bank[:], func=AF.Copy))
                    p.op("dve", ["halo", kxp], [kxp],
                         emit=lambda e, ot=ot, xpc=xpc: e.tensor_copy(
                             out=xpc[:, 0:3], in_=halo[:, ot * 3:ot * 3 + 3]))
                    p.op("dve", [kxp], ["halo"],
                         emit=lambda e, ot=ot, xpc=xpc: e.tensor_copy(
                             out=halo[:, ot * 3:ot * 3 + 3],
                             in_=xpc[:, CH_B:CH_B + 3]))
                    acc0 = acc_a if ot % 2 == 0 else acc_b
                    ka0 = "acc_a" if ot % 2 == 0 else "acc_b"
                    p.op("dve", [kxp, "cwc", "cbc"], [ka0],
                         emit=lambda e, ot=ot, xpc=xpc, acc0=acc0: e.tensor_scalar(
                             out=acc0[:], in0=xpc[:, 0:CH_B],
                             scalar1=cwc[:, ot * 4:ot * 4 + 1],
                             scalar2=cbc[:, ot:ot + 1], op0=ALU.mult, op1=ALU.add))
                    p.op("dve", [kxp, "cwc", ka0], [ka0],
                         emit=lambda e, ot=ot, xpc=xpc, acc0=acc0:
                         e.scalar_tensor_tensor(
                             out=acc0[:], in0=xpc[:, 1:1 + CH_B],
                             scalar=cwc[:, ot * 4 + 1:ot * 4 + 2], in1=acc0[:],
                             op0=ALU.mult, op1=ALU.add))
                    p.op("dve", [kxp, "cwc", ka0], [ka0],
                         emit=lambda e, ot=ot, xpc=xpc, acc0=acc0:
                         e.scalar_tensor_tensor(
                             out=acc0[:], in0=xpc[:, 2:2 + CH_B],
                             scalar=cwc[:, ot * 4 + 2:ot * 4 + 3], in1=acc0[:],
                             op0=ALU.mult, op1=ALU.add))
                    p.op("dve", [kxp, "cwc", ka0], [ka0],
                         emit=lambda e, ot=ot, xpc=xpc, acc0=acc0:
                         e.scalar_tensor_tensor(
                             out=acc0[:], in0=xpc[:, 3:3 + CH_B],
                             scalar=cwc[:, ot * 4 + 3:ot * 4 + 4], in1=acc0[:],
                             op0=ALU.mult, op1=ALU.add))
                    p.op("act", [ka0], ["xcb1"],
                         emit=lambda e, ot=ot, acc0=acc0: e.activation(
                             out=v_xcb1[:, (ot % 4) * CH_B:(ot % 4 + 1) * CH_B],
                             in_=acc0[:], func=AF.Silu))

                    def wzmm(e, ot=ot, bank=bank, pp=pp):
                        last = None
                        for k in range(KD):
                            last = e.matmul(
                                bank[:],
                                slotB[:, k * INNER + ot * 128:k * INNER + (ot + 1) * 128],
                                u2_rhs(k, pp), start=(k == 0), stop=(k == KD - 1))
                        return last
                    p.op("pe", [ku2a, ku2b, "slotB"], [bkey], emit=wzmm)
                    p.op("act", [bkey], ["zb1"],
                         emit=lambda e, ot=ot, bank=bank: e.activation(
                             out=v_zb1[:, (ot % 4) * CH_B:(ot % 4 + 1) * CH_B],
                             in_=bank[:], func=AF.Copy))
                    if ot % 4 == 3:
                        q = ot // 4
                        xc_view = bass.AP(sp_xc.tensor,
                                          b * INNER * L + q * 4 * 128 * L + T0,
                                          [[L, 128], [128 * L, 4], [1, CH_B]])
                        dma("sp", xc_view,
                            bass.AP(v_xcb1.tensor, 0,
                                    [[8192, 128], [CH_B, 4], [1, CH_B]]),
                            ["xcb1"], ["sp_xc"])
                        z_view = bass.AP(sp_z.tensor,
                                         b * INNER * L + q * 4 * 128 * L + T0,
                                         [[L, 128], [128 * L, 4], [1, CH_B]])
                        dma("pool", z_view,
                            bass.AP(v_zb1.tensor, 2048,
                                    [[8192, 128], [CH_B, 4], [1, CH_B]]),
                            ["zb1"], ["sp_z"])

            p1a1_ln1(b, 0)
            p1a1_ln1(b, 1)
            p1a1_t1f1(b, 0)
            p1a1_t1f1(b, 1)
            p1a1_ln2(b, 0)
            p1a1_ln2(b, 1)
            p1a1_t2(b, 0)
            p1a1_t2(b, 1)
            if NCH_B > 1:
                p1a1_ln1(b, 2)
                p1a1_ln1(b, 3)
            for cp in range(NCH_B):
                if cp + 1 < NCH_B:
                    p1a1_t1f1(b, 2 * cp + 2)
                    p1a1_t1f1(b, 2 * cp + 3)
                    p1a1_ln2(b, 2 * cp + 2)
                    p1a1_ln2(b, 2 * cp + 3)
                    p1a1_t2(b, 2 * cp + 2)
                    if cp + 2 < NCH_B:
                        p1a1_ln1(b, 2 * cp + 4)
                    if _PH >= 3:
                        wxwz(cp, 0, 2)
                    p1a1_t2(b, 2 * cp + 3)
                    if cp + 2 < NCH_B:
                        p1a1_ln1(b, 2 * cp + 5, stage=1)
                    if _PH >= 3:
                        wxwz(cp, 2, 4)
                    if cp + 2 < NCH_B:
                        p1a1_ln1(b, 2 * cp + 5, stage=2)
                    if _PH >= 3:
                        wxwz(cp, 4, KI)
                elif _PH >= 3:
                    wxwz(cp, 0, KI)

        prep_w(dWb, PO, INNER, slotA, "slotA", lambda kt, ot: kt * PK + ot * 128)
        prep_w(dWc, PO, INNER, slotA, "slotA",
               lambda kt, ot: kt * PK + PO + ot * 128)
        dma("pool", wstg[0:8, 0:INNER], dWdt, [], ["wstg"])
        for kt in range(KI):
            bi = kt % 2
            bank, bkey = ptr[bi], PT[bi]
            p.op("pe", ["wstg", "ident"], [bkey],
                 emit=lambda e, kt=kt, bank=bank: e.transpose(
                     bank[:, 0:8], wstg[0:8, kt * 128:(kt + 1) * 128],
                     ident[0:8, 0:8]))
            p.op("dve", [bkey], ["slotA"],
                 emit=lambda e, kt=kt, bank=bank: e.tensor_copy(
                     out=slotA[:, kt * PK + 2 * PO:kt * PK + 2 * PO + 8],
                     in_=bank[:, 0:8]))

        p.fence(ALL_POOL_KEYS)
        p.op("dve", [], ["ctil"], emit=lambda e: e.memset(v_ctil, 0.0))
        p.op("dve", [], ["cth"], emit=lambda e: e.memset(v_cth, 0.0))
        p.op("dve", [], ["btmA"], emit=lambda e: e.memset(v_btmA, 0.0))
        p.op("dve", [], ["btmB"], emit=lambda e: e.memset(v_btmB, 0.0))
        for b in range(BPC if _PH >= 4 else 0):
            p.op("dve", [], ["S_a"], emit=lambda e: e.memset(S_a[:], 0.0))
            p.op("dve", [], ["S_bf"], emit=lambda e: e.memset(S_bf[:], 0.0))
            S_cur, S_nxt = S_a, S_b
            S_keys = ["S_a", "S_b"]
            for c in range(NCH_B):
                T0 = c * CH_B
                xcs_t = [G2, G1][c % 2]
                v_xcs_c = xcs_t[:, 0:8192]
                kxcs = f"xcs{c % 2}"
                for q in range(4):
                    xc_view = bass.AP(sp_xc.tensor,
                                      b * INNER * L + q * 4 * 128 * L + T0,
                                      [[L, 128], [128 * L, 4], [1, CH_B]])
                    dma("sp",
                        bass.AP(xcs_t, q * 4 * CH_B,
                                [[8192, 128], [CH_B, 4], [1, CH_B]]),
                        xc_view, ["sp_xc"], [kxcs])
                if _SB < 2:
                    continue
                def dtmm(e, v_xcs_c=v_xcs_c):
                    last = None
                    for k in range(KI):
                        last = e.matmul(
                            psum[4][0:8, :],
                            slotA[:, k * PK + 2 * PO:k * PK + 2 * PO + 8],
                            v_xcs_c[:, k * CH_B:(k + 1) * CH_B],
                            start=(k == 0), stop=(k == KI - 1))
                    return last
                p.op("pe", [kxcs, "slotA"], [PS[4]], emit=dtmm)
                if _SB < 3:
                    continue
                pr = tin[0:8, 0:CH_B]
                t1 = dtv[0:8, :]
                t2 = tin[0:8, CH_B:2 * CH_B]
                p.op("dve", [PS[4], "bdtc"], ["tin"],
                     emit=lambda e: e.tensor_scalar(out=pr, in0=psum[4][0:8, :],
                                                    scalar1=bdtc[:], scalar2=None,
                                                    op0=ALU.add))
                p.op("dve", ["tin"], ["dtv"],
                     emit=lambda e: e.tensor_scalar(out=t1, in0=pr, scalar1=-1.0,
                                                    scalar2=None, op0=ALU.mult))
                p.op("dve", ["tin", "dtv"], ["tin"],
                     emit=lambda e: e.tensor_tensor(out=t2, in0=pr, in1=t1,
                                                    op=ALU.min))
                p.op("act", ["tin"], ["tin"],
                     emit=lambda e: e.activation(out=t2, in_=t2, func=AF.Exp))
                p.op("act", ["tin"], ["tin"],
                     emit=lambda e: e.activation(out=t2, in_=t2, func=AF.Ln,
                                                 bias=1.0))
                p.op("dve", ["tin"], ["dtv"],
                     emit=lambda e: e.tensor_scalar(out=t1, in0=pr, scalar1=0.0,
                                                    scalar2=None, op0=ALU.max))
                p.op("dve", ["tin", "dtv"], ["dtv"],
                     emit=lambda e: e.tensor_tensor(out=dtv[:], in0=t1, in1=t2,
                                                    op=ALU.add))
                p.op("dve", ["dtv", "acol"], ["scbf"],
                     emit=lambda e: e.tensor_scalar(out=scbf[:], in0=dtv[:],
                                                    scalar1=acol[:], scalar2=None,
                                                    op0=ALU.mult))
                p.op("dve", ["scbf", "seg"], ["s_sb"],
                     emit=lambda e: e.tensor_tensor_scan(
                         out=s_sb[0:8, :], data0=segc[:], data1=scbf[:], initial=0.0,
                         op0=ALU.mult, op1=ALU.add))
                p.op("act", ["dtv"], ["tin"],
                     emit=lambda e: e.activation(out=t2, in_=dtv[:], func=AF.Ln))
                p.op("dve", ["tin", "s_sb"], ["nslf"],
                     emit=lambda e: e.tensor_sub(out=nslf[:], in0=t2,
                                                 in1=s_sb[0:8, :]))
                p.op("act", ["s_sb"], ["esr"],
                     emit=lambda e: e.activation(out=esr[:], in_=s_sb[0:8, :],
                                                 func=AF.Exp))
                for su in range(NSUB):
                    p.op("act", ["s_sb", "nslf"], ["scbf"],
                         emit=lambda e, su=su: e.activation(
                             out=scbf[:, su * SUB:(su + 1) * SUB],
                             in_=nslf[:, su * SUB:(su + 1) * SUB], func=AF.Exp,
                             bias=s_sb[0:8, (su + 1) * SUB - 1:(su + 1) * SUB]))
                if _SB < 4:
                    continue
                for su in range(NSUB):
                    bi = su % 2
                    p.op("pe", ["nslf", "identf"], [PS[5]],
                         emit=lambda e, su=su: e.transpose(
                             psum[5][:, 0:8], nslf[:, su * SUB:(su + 1) * SUB],
                             identf[0:8, 0:8]))
                    p.op("dve", [PS[5]], ["colT"],
                         emit=lambda e, su=su: e.tensor_copy(
                             out=colT[:, su * 16:su * 16 + 8], in_=psum[5][:, 0:8]))
                    p.op("pe", ["scbf", "identf"], [PS[5]],
                         emit=lambda e, su=su: e.transpose(
                             psum[5][:, 0:8], scbf[:, su * SUB:(su + 1) * SUB],
                             identf[0:8, 0:8]))
                    p.op("dve", [PS[5]], ["colT"],
                         emit=lambda e, su=su: e.tensor_copy(
                             out=colT[:, su * 16 + 8:su * 16 + 16],
                             in_=psum[5][:, 0:8]))
                for su in range(NSUB):
                    def lam(e, su=su):
                        last = None
                        for g in range(4):
                            last = e.matmul(
                                psum[5][:, su * 4 + g:su * 4 + g + 1],
                                selc[:, g * 128:(g + 1) * 128],
                                esr[:, (su + 1) * SUB - 1:(su + 1) * SUB],
                                start=True, stop=True)
                        return last
                    p.op("pe", ["esr", "sel"], [PS[5]], emit=lam)
                    p.op("dve", [PS[5]], ["lamc"],
                         emit=lambda e, su=su: e.tensor_copy(
                             out=lamc[:, su * 4:su * 4 + 4],
                             in_=psum[5][:, su * 4:su * 4 + 4]))
                if _SB < 5:
                    continue
                for m in range(4):
                    bi = m % 4
                    bank, bkey = psum[bi], PS[bi]

                    def bmm(e, m=m, bank=bank, v_xcs_c=v_xcs_c):
                        last = None
                        for k in range(KI):
                            last = e.matmul(
                                bank[:],
                                slotA[:, k * PK + m * 128:k * PK + (m + 1) * 128],
                                v_xcs_c[:, k * CH_B:(k + 1) * CH_B],
                                start=(k == 0), stop=(k == KI - 1))
                        return last
                    p.op("pe", [kxcs, "slotA"], [bkey], emit=bmm)
                    p.op("dve", [bkey], ["bfm"],
                         emit=lambda e, m=m, bank=bank: e.tensor_copy(
                             out=v_bfm[:, m * CH_B:(m + 1) * CH_B], in_=bank[:]))
                for m in range(4):
                    bi = m % 4
                    bank, bkey = psum[bi], PS[bi]

                    def cmm(e, m=m, bank=bank, v_xcs_c=v_xcs_c):
                        last = None
                        for k in range(KI):
                            last = e.matmul(
                                bank[:],
                                slotA[:, k * PK + PO + m * 128:k * PK + PO + (m + 1) * 128],
                                v_xcs_c[:, k * CH_B:(k + 1) * CH_B],
                                start=(k == 0), stop=(k == KI - 1))
                        return last
                    p.op("pe", [kxcs, "slotA"], [bkey], emit=cmm)

                    def esmm(e, m=m):
                        return e.matmul(psum[4][:], selc[:, m * 128:(m + 1) * 128],
                                        esr[:], start=True, stop=True)
                    p.op("pe", ["esr", "sel"], [PS[4]], emit=esmm)
                    p.op("dve", [PS[4]], ["esbuf"],
                         emit=lambda e: e.tensor_copy(out=esb[:], in_=psum[4][:]))
                    p.op("dve", [bkey, "esbuf"], ["ctil"],
                         emit=lambda e, m=m, bank=bank: e.tensor_tensor(
                             out=v_ctil[0:64, m * CH_B:(m + 1) * CH_B],
                             in0=bank[0:64, :], in1=esb[0:64, :], op=ALU.mult))
                    p.op("dve", [bkey, "esbuf"], ["cth"],
                         emit=lambda e, m=m, bank=bank: e.tensor_tensor(
                             out=v_cth[64:128, m * CH_B:(m + 1) * CH_B],
                             in0=bank[64:128, :], in1=esb[64:128, :], op=ALU.mult))

                if _SB < 6:
                    continue
                for su in range(NSUB):
                    t0 = su * SUB
                    xtm = v_xtmA if su % 2 == 0 else v_xtmB
                    xkey = "xtmA" if su % 2 == 0 else "xtmB"
                    btm = v_btmA if su % 2 == 0 else v_btmB
                    btkey = "btmA" if su % 2 == 0 else "btmB"
                    xv = bass.AP(sp_xc.tensor, b * INNER * L + T0 + t0,
                                 [[L, INNER], [1, 128]])
                    p.op("sp", ["sp_xc"], [xkey],
                         emit=lambda e, xv=xv, xtm=xtm: e.dma_start(
                             out=xtm[:], in_=xv, transpose=True), dma=True)
                    for h in range(8):
                        bi = h % 2
                        bank, bkey = ptr[bi], PT[bi]
                        p.op("pe", ["bfm", "ident"], [bkey],
                             emit=lambda e, h=h, bank=bank, t0=t0: e.transpose(
                                 bank[:, 0:64],
                                 v_bfm[(h % 2) * 64:(h % 2) * 64 + 64,
                                       (h // 2) * CH_B + t0:(h // 2) * CH_B + t0 + 128],
                                 ident[(h % 2) * 64:(h % 2) * 64 + 64,
                                       (h % 2) * 64:(h % 2) * 64 + 64]))
                        bcol = (h // 2) * 256 + (h % 2) * 192
                        p.op("dve", [bkey, "colT"], [btkey],
                             emit=lambda e, h=h, su=su, bank=bank, btm=btm, bcol=bcol:
                             e.tensor_scalar(
                                 out=btm[:, bcol:bcol + 64],
                                 in0=bank[:, 0:64],
                                 scalar1=colT[:, su * 16 + 8 + h:su * 16 + 9 + h],
                                 scalar2=None, op0=ALU.mult))
                    if _SB < 7:
                        continue
                    for hb in range(2):
                        mb, mkey = psum[2 + hb], PS[2 + hb]
                        gb, gkey = psum[4 + hb], PS[4 + hb]

                        def maskmm(e, hb=hb, mb=mb, t0=t0):
                            e.matmul(mb[:], ident[:], negr[:],
                                     start=True, stop=False)
                            last = None
                            for hq in range(4):
                                h = hb * 4 + hq
                                sl = mb[:, hq * 128:(hq + 1) * 128]
                                last = e.matmul(sl, selh[:, h * 128:(h + 1) * 128],
                                                s_sb[0:8, t0:t0 + 128],
                                                start=False, stop=(hq == 3))
                            return last
                        if _SM >= 1:
                            p.op("pe", ["s_sb", "selh", "ident", "negr"], [mkey],
                                 emit=maskmm)
                        for hq in range(4 if _SM >= 2 else 0):
                            h = hb * 4 + hq
                            p.op("act", [mkey, "colT"], ["eb"],
                                 emit=lambda e, h=h, hq=hq, su=su, mb=mb:
                                 e.activation(
                                     out=v_eb[:, h * 128:(h + 1) * 128],
                                     in_=mb[:, hq * 128:(hq + 1) * 128],
                                     func=AF.Exp,
                                     bias=colT[:, su * 16 + h:su * 16 + h + 1]))

                        def gmm(e, hb=hb, gb=gb, t0=t0):
                            last = None
                            for hq in range(4):
                                h = hb * 4 + hq
                                cpad = v_ctil if h % 2 == 0 else v_cth
                                last = e.matmul(
                                    gb[:, hq * 128:(hq + 1) * 128],
                                    v_bfm[:, (h // 2) * CH_B + t0:(h // 2) * CH_B + t0 + 128],
                                    cpad[:, (h // 2) * CH_B + t0:(h // 2) * CH_B + t0 + 128],
                                    start=True, stop=True)
                            return last
                        if _SM >= 3:
                            p.op("pe", ["bfm", "ctil", "cth"], [gkey], emit=gmm)
                        if _SM >= 4:
                            p.op("dve", [gkey, "eb"], ["mg"],
                                 emit=lambda e, gb=gb, hb=hb: e.tensor_tensor(
                                     out=v_mg[:, hb * 512:(hb + 1) * 512], in0=gb[:],
                                     in1=v_eb[:, hb * 512:(hb + 1) * 512], op=ALU.mult))
                            p.op("dve", ["mg", "dic"], ["mg"],
                                 emit=lambda e, hb=hb: e.tensor_tensor(
                                     out=v_mg[:, hb * 512:(hb + 1) * 512],
                                     in0=v_mg[:, hb * 512:(hb + 1) * 512],
                                     in1=dic[:, hb * 512:(hb + 1) * 512], op=ALU.add))
                    if _SB < 8:
                        continue
                    for j in range(4):
                        yb, ykey = psum[j], PS[j]

                        def ymm(e, j=j, t0=t0, xtm=xtm):
                            last = None
                            for q in range(4):
                                ct = 4 * j + q
                                h = ct // 2
                                sl = psum[j][:, q * 128:(q + 1) * 128]
                                e.matmul(sl,
                                         xtm[:, ct * 128:(ct + 1) * 128],
                                         v_mg[:, h * 128:(h + 1) * 128],
                                         start=True, stop=False)
                                cpad = v_ctil if h % 2 == 0 else v_cth
                                last = e.matmul(
                                    sl,
                                    S_bf[:, (h // 2) * HD + (ct % 2) * 128:
                                         (h // 2) * HD + (ct % 2) * 128 + 128],
                                    cpad[:, (h // 2) * CH_B + t0:(h // 2) * CH_B + t0 + 128],
                                    start=False, stop=True)
                            return last
                        p.op("pe", [xkey, "mg", "dic", "S_bf", "ctil", "cth"], [ykey],
                             emit=ymm)
                        for half in range(2):
                            col = c * NSUB * 8 + su * 8 + j * 2 + half
                            p.op("dve", [ykey], ["ys", "yt"],
                                 emit=lambda e, j=j, su=su, half=half, col=col:
                                 e.tensor_scalar(
                                     out=bass.AP(v_yt.tensor,
                                                 (4 * j + 2 * half) * CH_B + su * 128,
                                                 [[8192, 128], [CH_B, 2], [1, 128]]),
                                     in0=psum[j][:, half * 256:(half + 1) * 256],
                                     scalar1=1.0, scalar2=0.0, op0=ALU.mult,
                                     op1=ALU.add,
                                     accum_out=ys[:, col:col + 1]))
                            p.op("act", [ykey], ["yss", "dumpA"],
                                 emit=lambda e, j=j, half=half, col=col:
                                 e.activation(
                                     out=dumpA[:, 0:256],
                                     in_=psum[j][:, half * 256:(half + 1) * 256],
                                     func=AF.Square, accum_out=yss[:, col:col + 1]))
                    if _SB < 9:
                        continue
                    for g in range(4):
                        bi = 4 + (g % 2)
                        db, dkey = psum[bi], PS[bi]

                        def dsmm(e, g=g, db=db, xtm=xtm, btm=btm):
                            e.matmul(db[:, 0:HD],
                                     btm[:, g * 256:g * 256 + 128],
                                     xtm[:, (2 * g) * HD:(2 * g + 1) * HD],
                                     start=True, stop=False)
                            return e.matmul(
                                db[:, 0:HD],
                                btm[:, g * 256 + 128:g * 256 + 256],
                                xtm[:, (2 * g + 1) * HD:(2 * g + 2) * HD],
                                start=False, stop=True)
                        p.op("pe", [btkey, xkey], [dkey], emit=dsmm)
                        p.op("dve", [dkey, S_keys[0], "lamc"], [S_keys[1]],
                             emit=lambda e, g=g, su=su, db=db, Sc=S_cur, Sn=S_nxt:
                             e.scalar_tensor_tensor(
                                 out=Sn[:, g * HD:(g + 1) * HD],
                                 in0=Sc[:, g * HD:(g + 1) * HD],
                                 scalar=lamc[:, su * 4 + g:su * 4 + g + 1],
                                 in1=db[:, 0:HD], op0=ALU.mult, op1=ALU.add))
                        p.op("dve", [S_keys[1]], ["S_bf"],
                             emit=lambda e, g=g, Sn=S_nxt: e.tensor_copy(
                                 out=S_bf[:, g * HD:(g + 1) * HD],
                                 in_=Sn[:, g * HD:(g + 1) * HD]))
                    S_cur, S_nxt = S_nxt, S_cur
                    S_keys = S_keys[::-1]
                for q in range(4):
                    y_view = bass.AP(sp_y.tensor, b * INNER * L + q * 4 * 128 * L + T0,
                                     [[L, 128], [128 * L, 4], [1, CH_B]])
                    dma("pool", y_view,
                        bass.AP(v_yt.tensor, q * 4 * CH_B,
                                [[8192, 128], [CH_B, 4], [1, CH_B]]),
                        ["yt"], ["sp_y"])
            def gnred(e):
                e.matmul(psum[0][0:1, 0:NCH_B * NSUB * 8], onesb[:, 0:1], ys[:],
                         start=True, stop=True)
                return e.matmul(psum[1][0:1, 0:NCH_B * NSUB * 8], onesb[:, 0:1],
                                yss[:], start=True, stop=True)
            p.op("pe", ["ys", "yss", "onesb"], [PS[0], PS[1]], emit=gnred)
            p.op("dve", [PS[0]], ["stat8"],
                 emit=lambda e: e.reduce_sum(
                     out=stat8[0:1, 0:8],
                     in_=psum[0][0:1, 0:NCH_B * NSUB * 8].rearrange(
                         "p (a b) -> p b a", b=8), axis=AX.X))
            p.op("dve", [PS[1]], ["stat8"],
                 emit=lambda e: e.reduce_sum(
                     out=stat8[0:1, 8:16],
                     in_=psum[1][0:1, 0:NCH_B * NSUB * 8].rearrange(
                         "p (a b) -> p b a", b=8), axis=AX.X))
            NGN = float(L * HD)
            p.op("dve", ["stat8"], ["stat8"],
                 emit=lambda e: e.tensor_scalar(out=stat8[0:1, 16:24],
                                                in0=stat8[0:1, 0:8],
                                                scalar1=1.0 / NGN, scalar2=None,
                                                op0=ALU.mult))
            p.op("dve", ["stat8"], ["stat8"],
                 emit=lambda e: e.tensor_scalar(out=stat8[0:1, 24:32],
                                                in0=stat8[0:1, 8:16],
                                                scalar1=1.0 / NGN, scalar2=None,
                                                op0=ALU.mult))
            p.op("dve", ["stat8"], ["stat8"],
                 emit=lambda e: e.tensor_mul(out=stat8[0:1, 32:40],
                                             in0=stat8[0:1, 16:24],
                                             in1=stat8[0:1, 16:24]))
            p.op("dve", ["stat8"], ["stat8"],
                 emit=lambda e: e.tensor_sub(out=stat8[0:1, 40:48],
                                             in0=stat8[0:1, 24:32],
                                             in1=stat8[0:1, 32:40]))
            p.op("act", ["stat8", "cst"], ["stat8"],
                 emit=lambda e: e.activation(out=stat8[0:1, 40:48],
                                             in_=stat8[0:1, 40:48], func=AF.Ln,
                                             bias=cst[0:1, 0:1]))
            p.op("act", ["stat8", "cst"], ["stat8"],
                 emit=lambda e: e.activation(out=stat8[0:1, 48:56],
                                             in_=stat8[0:1, 40:48], func=AF.Exp,
                                             scale=cst[0:1, 1:2]))
            def bc16(e):
                e.matmul(psum[2][:, 0:8], onesc[0:1, 0:128], stat8[0:1, 48:56],
                         start=True, stop=True)
                return e.matmul(psum[2][:, 8:16], onesc[0:1, 0:128],
                                stat8[0:1, 16:24], start=True, stop=True)
            p.op("pe", ["stat8", "ones"], [PS[2]], emit=bc16)
            p.op("dve", [PS[2]], ["rinv8"],
                 emit=lambda e, b=b: e.tensor_copy(out=rinv8[:, b * 16:b * 16 + 16],
                                                   in_=psum[2][:, 0:16]))
            p.op("dve", ["rinv8", "gnwc"], ["gnwp"],
                 emit=lambda e, b=b: e.tensor_tensor(
                     out=gnwp[:, b * KI:(b + 1) * KI], in0=gnwc[:],
                     in1=bass.AP(rinv8, b * 16,
                                 [[BPC * 16, 128], [1, 8], [0, 2]]),
                     op=ALU.mult))
            p.op("dve", ["gnwp", "rinv8"], ["gnbp"],
                 emit=lambda e, b=b: e.tensor_tensor(
                     out=gnbp[:, b * KI:(b + 1) * KI],
                     in0=gnwp[:, b * KI:(b + 1) * KI],
                     in1=bass.AP(rinv8, b * 16 + 8,
                                 [[BPC * 16, 128], [1, 8], [0, 2]]),
                     op=ALU.mult))
            p.op("dve", ["gnbp", "gnbc"], ["gnbp"],
                 emit=lambda e, b=b: e.tensor_sub(
                     out=gnbp[:, b * KI:(b + 1) * KI], in0=gnbc[:],
                     in1=gnbp[:, b * KI:(b + 1) * KI]))

        prep_w(dWout, D, INNER, slotB, "slotB", lambda kt, ot: kt * D + ot * 128)

        p.fence(ALL_POOL_KEYS)

        def p2_loads(b, c):
            T0 = c * CH_A
            par = c % 2
            v_yr, v_zr = v_yrp[par], v_zrp[par]
            kyr, kzr = f"yr_{par}", f"zr_{par}"
            for q in range(4):
                y_view = bass.AP(sp_y.tensor, b * INNER * L + q * 4 * 128 * L + T0,
                                 [[L, 128], [128 * L, 4], [1, CH_A]])
                dma("sp",
                    bass.AP(v_yr.tensor, par * 4096 + q * 4 * CH_A,
                            [[8192, 128], [CH_A, 4], [1, CH_A]]),
                    y_view, ["sp_y"], [kyr])
                z_view = bass.AP(sp_z.tensor, b * INNER * L + q * 4 * 128 * L + T0,
                                 [[L, 128], [128 * L, 4], [1, CH_A]])
                dma("pool",
                    bass.AP(v_zr.tensor, par * 4096 + q * 4 * CH_A,
                            [[8192, 128], [CH_A, 4], [1, CH_A]]),
                    z_view, ["sp_z"], [kzr])

        def p2_gy(b, c):
            par = c % 2
            v_yr, v_zr, v_gy = v_yrp[par], v_zrp[par], v_gyp[par]
            kyr, kzr, kgy = f"yr_{par}", f"zr_{par}", f"gy_{par}"
            for og in range(KI // 2):
                ot = 2 * og
                acc0 = acc_a if og % 2 == 0 else acc_b
                ka = f"acca{og % 2}"
                p.op("dve", [kyr, "gnwp", "gnbp"], [ka],
                     emit=lambda e, ot=ot, b=b, acc0=acc0, v_yr=v_yr:
                     e.tensor_scalar(
                         out=acc0[:, 0:CH_A],
                         in0=v_yr[:, ot * CH_A:(ot + 1) * CH_A],
                         scalar1=gnwp[:, b * KI + ot:b * KI + ot + 1],
                         scalar2=gnbp[:, b * KI + ot:b * KI + ot + 1],
                         op0=ALU.mult, op1=ALU.add))
                p.op("dve", [kyr, "gnwp", "gnbp", ka], [ka],
                     emit=lambda e, ot=ot, b=b, acc0=acc0, v_yr=v_yr:
                     e.tensor_scalar(
                         out=acc0[:, CH_A:2 * CH_A],
                         in0=v_yr[:, (ot + 1) * CH_A:(ot + 2) * CH_A],
                         scalar1=gnwp[:, b * KI + ot + 1:b * KI + ot + 2],
                         scalar2=gnbp[:, b * KI + ot + 1:b * KI + ot + 2],
                         op0=ALU.mult, op1=ALU.add))
                p.op("act", [kzr], [kgy],
                     emit=lambda e, ot=ot, v_gy=v_gy, v_zr=v_zr: e.activation(
                         out=v_gy[:, ot * CH_A:(ot + 2) * CH_A],
                         in_=v_zr[:, ot * CH_A:(ot + 2) * CH_A], func=AF.Silu))
                p.op("dve", [ka, kgy], [kgy],
                     emit=lambda e, ot=ot, acc0=acc0, v_gy=v_gy:
                     e.tensor_tensor(
                         out=v_gy[:, ot * CH_A:(ot + 2) * CH_A],
                         in0=acc0[:, 0:2 * CH_A],
                         in1=v_gy[:, ot * CH_A:(ot + 2) * CH_A],
                         op=ALU.mult))

        def p2_main_a(b, c):
            T0 = c * CH_A
            par = c % 2
            v_x2 = v_x2p[par]
            v_gy = v_gyp[par]
            kx2, kgy = f"x2_{par}", f"gy_{par}"
            x1_view = bass.AP(sp_x1.tensor, (b * L + T0) * D,
                              [[D, 128], [128 * D, NT_A], [1, D]])
            dma("sp", v_x1r.rearrange("p (a b) -> p a b", a=NT_A), x1_view,
                ["sp_x1"], ["x1r"])
            for tt in range(NT_A):
                for nh in range(2):
                    bi = (tt * 2 + nh) % 4
                    bank, bkey = psum[bi], PS[bi]

                    def womm(e, tt=tt, nh=nh, bank=bank, v_gy=v_gy):
                        last = None
                        for k in range(KI):
                            last = e.matmul(
                                bank[:],
                                v_gy[:, k * CH_A + tt * 128:k * CH_A + (tt + 1) * 128],
                                slotB[:, k * D + nh * 512:k * D + (nh + 1) * 512],
                                start=(k == 0), stop=(k == KI - 1))
                        return last
                    p.op("pe", [kgy, "slotB"], [bkey], emit=womm)
                    p.op("dve", [bkey, "x1r"], [kx2],
                         emit=lambda e, tt=tt, nh=nh, bank=bank, v_x2=v_x2:
                         e.tensor_tensor(
                             out=v_x2[:, tt * D + nh * 512:tt * D + (nh + 1) * 512],
                             in0=bank[:],
                             in1=v_x1r[:, tt * D + nh * 512:tt * D + (nh + 1) * 512],
                             op=ALU.add))

        def p2_ln(b, c):
            par = c % 2
            layernorm(v_x2p[par], f"x2_{par}", v_utm2, "utm2", par)

        def p2_tr(b, c):
            par = c % 2
            transpose_u(v_utm2, "utm2", v_ufm2p[par], f"ufm2_{par}", CH_A)

        def p2_f2(b, c):
            T0 = c * CH_A
            par = c % 2
            v_x2 = v_x2p[par]
            v_ufm2 = v_ufm2p[par]
            kx2, kuf2 = f"x2_{par}", f"ufm2_{par}"
            for tt in range(NT_A):
                for nh in range(2):
                    bi = 4 + (tt * 2 + nh) % 2
                    bank, bkey = psum[bi], PS[bi]

                    def f2(e, tt=tt, nh=nh, bank=bank, v_ufm2=v_ufm2):
                        last = None
                        for k in range(KD):
                            last = e.matmul(
                                bank[:],
                                v_ufm2[:, k * CH_A + tt * 128:k * CH_A + (tt + 1) * 128],
                                wS[:, k * D + nh * 512:k * D + (nh + 1) * 512],
                                start=(k == 0), stop=(k == KD - 1))
                        return last
                    p.op("pe", [kuf2, "wS"], [bkey], emit=f2)
                    p.op("dve", [bkey, kx2], ["x3"],
                         emit=lambda e, tt=tt, nh=nh, bank=bank, v_x2=v_x2:
                         e.scalar_tensor_tensor(
                             out=v_x3[:, nh * 512:(nh + 1) * 512],
                             in0=bank[:], scalar=0.5,
                             in1=v_x2[:, tt * D + nh * 512:tt * D + (nh + 1) * 512],
                             op0=ALU.mult, op1=ALU.add))
                out_view = bass.AP(dout.tensor, (b * L + T0 + tt * 128) * D,
                                   [[D, 128], [1, D]])
                dma("sp", out_view, v_x3[:], ["x3"], ["dout"])

        if _PH >= 5:
            chunks = [(b, c) for b in range(BPC) for c in range(NCH_A)]
            p2_loads(*chunks[0])
            p2_gy(*chunks[0])
            p2_loads(*chunks[1])
            p2_gy(*chunks[1])
            p2_main_a(*chunks[0])
            for i, (b, c) in enumerate(chunks):
                p2_ln(b, c)
                if i + 2 < len(chunks):
                    p2_loads(*chunks[i + 2])
                if i + 1 < len(chunks):
                    p2_main_a(*chunks[i + 1])
                p2_tr(b, c)
                if i + 2 < len(chunks):
                    p2_gy(*chunks[i + 2])
                p2_f2(b, c)

        p.op("pool", ["dout"], [], emit=lambda e: e.engine_nop())

        eng_map = {"pe": block.tensor, "dve": block.vector, "act": block.scalar,
                   "pool": block.gpsimd, "sp": block.sync}
        for eng in Prog.ENGS:
            ops = p.ops[eng]
            if not ops:
                continue

            def section(e, _ops=ops, _sem=sems[eng]):
                for waits, emit, slotkey in _ops:
                    for src, v in waits:
                        e.wait_ge(sems[src], v)
                    inst = emit(e)
                    if slotkey is None:
                        inst.then_inc(_sem, 1)
                    else:
                        inst.then_inc(sems[slotkey], 16)
            eng_map[eng](section)

    return nc


def _consts():
    ident = np.eye(128, dtype=BF)
    identf = np.eye(8, dtype=np.float32)
    negr = np.tile(np.where(np.arange(128)[:, None] > np.arange(128)[None, :],
                            -1e30, 0.0), (1, 4)).astype(BF)
    ones_c = np.ones((1, 128), np.float32)
    ones_b = np.ones((128, 1), np.float32)
    seg = np.ones((HEADS, CH_B), np.float32)
    seg[:, ::SUB] = 0.0
    sel = np.zeros((HEADS, 4 * 128), BF)
    for g in range(4):
        for pp in range(128):
            sel[2 * g + pp // 64, g * 128 + pp] = 1.0
    selh = np.zeros((HEADS, HEADS * 128), np.float32)
    for h in range(HEADS):
        selh[h, h * 128:(h + 1) * 128] = 1.0
    return dict(ident=ident, identf=identf, negr=negr,
                ones_c=ones_c, ones_b=ones_b, seg=seg, sel=sel, selh=selh)


def kernel(L0_attn_input_tensor, L0_attn_Wx, L0_attn_Wz, L0_attn_conv_w,
           L0_attn_conv_b, L0_attn_Wb, L0_attn_Wc, L0_attn_Wdt, L0_attn_b_dt,
           L0_attn_log_a, L0_attn_d_param, L0_attn_gn_w, L0_attn_gn_b,
           L0_attn_Wout, L0_ffn_weight):
    x = np.ascontiguousarray(L0_attn_input_tensor)
    consts = _consts()
    shared = dict(
        Wx=np.ascontiguousarray(L0_attn_Wx),
        Wz=np.ascontiguousarray(L0_attn_Wz),
        conv_w=np.ascontiguousarray(
            L0_attn_conv_w[:, 0, :].reshape(KI, 128, KCONV).transpose(1, 0, 2)
            .reshape(128, KI * KCONV)),
        conv_b=np.ascontiguousarray(L0_attn_conv_b.reshape(KI, 128).T),
        Wb=np.ascontiguousarray(L0_attn_Wb),
        Wc=np.ascontiguousarray(L0_attn_Wc),
        Wdt=np.ascontiguousarray(L0_attn_Wdt),
        b_dt=np.ascontiguousarray(L0_attn_b_dt.reshape(HEADS, 1)),
        log_a=np.ascontiguousarray(L0_attn_log_a.reshape(HEADS, 1)),
        d_param=np.ascontiguousarray(L0_attn_d_param.reshape(HEADS, 1)),
        gn_w=np.ascontiguousarray(L0_attn_gn_w.reshape(KI, 128).T),
        gn_b=np.ascontiguousarray(L0_attn_gn_b.reshape(KI, 128).T),
        Wout=np.ascontiguousarray(L0_attn_Wout),
        Wf=np.ascontiguousarray(L0_ffn_weight),
        **consts,
    )
    nc = build_program()
    in_maps = []
    for i in range(NCORES):
        m = dict(shared)
        m["x"] = np.ascontiguousarray(x[i * BPC:(i + 1) * BPC])
        in_maps.append(m)
    res = run_bass_kernel_spmd(nc, in_maps, core_ids=list(range(NCORES)))
    out = np.concatenate([r["out"] for r in res.results], axis=0)
    return out.astype(np.float32)
